# version 01
# baseline (speedup 1.0000x reference)
import numpy as np
import ml_dtypes
import concourse.bass as bass
import concourse.mybir as mybir
from concourse.bass_utils import run_bass_kernel_spmd

F32 = mybir.dt.float32
BF16 = mybir.dt.bfloat16
AF = mybir.ActivationFunctionType
ALU = mybir.AluOpType
AX = mybir.AxisListType

NCORE = 8
T = 1024
NTT = T // 128
H = 2048
HC = H // 128
I = 8192
IC = I // 128
SB = 512
NS = (NCORE * T) // SB
NQ = NTT
QT4 = T // 4
RS2 = float(1.0 / np.sqrt(2048.0))
RS8 = float(1.0 / np.sqrt(8192.0))
SCL = float(1.0 / np.sqrt(2048.0))
KB = 1024


def _hadamard(n):
    h = np.array([[1.0]], dtype=np.float64)
    while h.shape[0] < n:
        h = np.block([[h, h], [h, -h]])
    return h


H16 = _hadamard(16)


def build_program(dbg=False, phases=99, qkv_tiles=None):
    nc = bass.Bass("TRN2", target_bir_lowering=False, detect_race_conditions=False)

    x_in = nc.dram_tensor("x", [T, H], F32, kind="ExternalInput").ap()
    w_qkv = nc.dram_tensor("w_qkv", [3 * H, H], F32, kind="ExternalInput").ap()
    b_qkv = nc.dram_tensor("b_qkv", [3 * H], F32, kind="ExternalInput").ap()
    w_out = nc.dram_tensor("w_out", [H, H], F32, kind="ExternalInput").ap()
    b_out = nc.dram_tensor("b_out", [H], F32, kind="ExternalInput").ap()
    su_ug = nc.dram_tensor("su_ug", [H], F32, kind="ExternalInput").ap()
    su_dp = nc.dram_tensor("su_dp", [I], F32, kind="ExternalInput").ap()
    wsc_ug = nc.dram_tensor("wsc_ug", [2 * I], F32, kind="ExternalInput").ap()
    wsc_dp = nc.dram_tensor("wsc_dp", [H], F32, kind="ExternalInput").ap()
    w_up = nc.dram_tensor("w_up", [I, H], F32, kind="ExternalInput").ap()
    w_gate = nc.dram_tensor("w_gate", [I, H], F32, kind="ExternalInput").ap()
    w_down = nc.dram_tensor("w_down", [H, I], F32, kind="ExternalInput").ap()
    out_d = nc.dram_tensor("out", [T, H], F32, kind="ExternalOutput").ap()

    ikind = "ExternalOutput" if dbg else "Internal"
    gin = nc.dram_tensor("gin", [4, H * T], BF16)
    gout = nc.dram_tensor("gout", [4 * NCORE, H * T], BF16)
    if dbg:
        gin_o = nc.dram_tensor("gin_o", [4, H * T], BF16, kind="ExternalOutput")
        gout_o = nc.dram_tensor("gout_o", [4 * NCORE, H * T], BF16, kind="ExternalOutput")
    qsp = nc.dram_tensor("qsp", [2, H, T], BF16, kind=ikind)
    zsp = nc.dram_tensor("zsp", [2, H, T], BF16, kind=ikind)
    usp = nc.dram_tensor("usp", [2, I, T], BF16, kind=ikind)
    ctxsp = nc.dram_tensor("ctxsp", [2, H, T], BF16, kind=ikind)
    wdfm = nc.dram_tensor("wdfm", [HC, 2, IC, 128, 128], BF16)
    wofm = nc.dram_tensor("wofm", [HC, 2, HC, 128, 128], BF16)

    ident_c = nc.inline_tensor(np.eye(128, dtype=np.float32), name="ident_c").ap()
    identb_c = nc.inline_tensor(np.eye(128, dtype=ml_dtypes.bfloat16), name="identb_c").ap()
    h128_c = nc.inline_tensor(_hadamard(128).astype(ml_dtypes.bfloat16), name="h128_c").ap()

    def kfm(sec):
        return gin.ap()[sec].rearrange("(r c) -> r c", r=H)

    def g_k(rank, hl):
        return gout.ap()[4 * rank + hl].rearrange("(r c) -> r c", r=H)

    def g_v(rank, hl):
        return gout.ap()[4 * rank + 2 + hl].rearrange("(r c) -> r c", r=T)

    def fm3(src2d):
        return src2d.rearrange("(c p) t -> p c t", p=128)

    def d3(dst2d, t):
        return dst2d.rearrange("p (c t) -> p c t", t=t)

    ops = {k: [] for k in ("sync", "gpsimd", "vector", "scalar", "tensor")}

    def em(eng, fn):
        ops[eng].append(fn)

    semnames = ["tP", "vP", "sP", "cc", "d_x", "d_wa", "d_wb", "d_k", "d_v",
                "d_q", "d_z", "d_u", "d_ctx", "d_wd", "d_wo", "d_st", "d_out", "d_c"]
    cnt = {k: 0 for k in semnames}
    from contextlib import ExitStack
    stack = ExitStack()
    sems = {k: stack.enter_context(nc.semaphore(k)) for k in semnames}

    def w(eng, sem, val=None):
        v = cnt[sem] if val is None else val
        if v > 0:
            em(eng, lambda e, s=sems[sem], v=v: e.wait_ge(s, v))

    def dma(eng, out, in_, sem):
        cnt[sem] += 16
        em(eng, lambda e, o=out, i=in_, s=sems[sem]: e.dma_start(out=o, in_=i).then_inc(s, 16))

    def dma_nc(eng, out, in_, sem):
        cnt[sem] += 16
        em(eng, lambda e, o=out, i=in_, s=sems[sem]: e.dma_start(
            out=o, in_=i, allow_slow_non_contiguous=True).then_inc(s, 16))

    def step(eng, sem, fn):
        cnt[sem] += 1
        em(eng, lambda e, f=fn, s=sems[sem]: f(e).then_inc(s, 1))

    arena = nc.alloc_sbuf_tensor("arena", [128, 200 * KB], mybir.dt.uint8).ap()

    def bview(off, nbytes, dt):
        a = arena[:, off:off + nbytes]
        return a.bitcast(dt)

    po = [0]

    def perm(nbytes, dt, align=64):
        o = (po[0] + align - 1) // align * align
        po[0] = o + nbytes
        return bview(o, nbytes, dt)

    ident = perm(512, F32)
    identb = perm(256, BF16)
    h128 = perm(256, BF16)
    h128n = perm(256, BF16)
    hsu_p = perm(HC * 256, BF16)
    hsu_n = perm(HC * 256, BF16)
    bqkv = perm(4 * 64, F32)
    boutc = perm(4 * 64, F32)
    s_up = perm(4 * 64, F32)
    s_gt = perm(4 * 64, F32)
    s_dpc = perm(4 * 64, F32)
    sudp = perm(4 * 64, F32)
    stat = perm(4 * 256, F32)
    assert po[0] <= 12 * KB, po[0]

    A0, B0, C0 = 12 * KB, 76 * KB, 140 * KB

    def A(off, nbytes, dt):
        assert off + nbytes <= 64 * KB
        return bview(A0 + off, nbytes, dt)

    def B(off, nbytes, dt):
        assert off + nbytes <= 64 * KB
        return bview(B0 + off, nbytes, dt)

    def C(off, nbytes, dt):
        assert off + nbytes <= 60 * KB
        return bview(C0 + off, nbytes, dt)

    def st_col(i):
        return stat[:, i:i + 1]

    psum = [stack.enter_context(nc.psum_tensor(f"ps{i}", [128, 512], F32)) for i in range(8)]

    def ps(i):
        return psum[i].ap()

    def psb(i):
        return ps(i).bitcast(BF16)

    dma("gpsimd", ident[:, :], ident_c, "d_c")
    dma("gpsimd", identb[:, :], identb_c, "d_c")
    dma("gpsimd", h128[:, :], h128_c, "d_c")
    dma_nc("gpsimd", bqkv[:, 0:48], b_qkv.rearrange("(j p) -> p j", p=128), "d_c")
    dma_nc("gpsimd", boutc[:, 0:HC], b_out.rearrange("(j p) -> p j", p=128), "d_c")
    dma_nc("gpsimd", stat[:, 0:HC], su_ug.rearrange("(j p) -> p j", p=128), "d_c")
    dma_nc("gpsimd", sudp[:, 0:IC], su_dp.rearrange("(j p) -> p j", p=128), "d_c")
    dma_nc("gpsimd", s_up[:, 0:IC], wsc_ug[0:I].rearrange("(j p) -> p j", p=128), "d_c")
    dma_nc("gpsimd", s_gt[:, 0:IC], wsc_ug[I:2 * I].rearrange("(j p) -> p j", p=128), "d_c")
    dma_nc("gpsimd", s_dpc[:, 0:HC], wsc_dp.rearrange("(j p) -> p j", p=128), "d_c")
    w("vector", "d_c")
    step("vector", "vP", lambda e: e.tensor_scalar_mul(s_up[:, 0:IC], s_up[:, 0:IC], RS2))
    step("vector", "vP", lambda e: e.tensor_scalar_mul(s_gt[:, 0:IC], s_gt[:, 0:IC], RS2))
    step("vector", "vP", lambda e: e.tensor_scalar_mul(s_dpc[:, 0:HC], s_dpc[:, 0:HC], RS8))
    step("vector", "vP", lambda e: e.tensor_scalar_mul(h128n[:, :], h128[:, :], -1.0))
    for c in range(HC):
        step("vector", "vP", lambda e, c=c: e.tensor_scalar_mul(
            hsu_p[:, c * 128:(c + 1) * 128], h128[:, :], st_col(c)))
    w("vector", "vP")
    for c in range(HC):
        step("vector", "vP", lambda e, c=c: e.tensor_scalar_mul(
            hsu_n[:, c * 128:(c + 1) * 128], hsu_p[:, c * 128:(c + 1) * 128], -1.0))
    consts_v = cnt["vP"]

    xn_hi = A(0, 32 * KB, BF16)
    xn_lo = A(32 * KB, 32 * KB, BF16)
    x_t = C(0, 8 * KB, F32)
    sq_t = C(8 * KB, 8 * KB, F32)
    xnt = C(16 * KB, 8 * KB, F32)
    sM = lambda tt: st_col(32 + tt)
    sV = lambda tt: st_col(40 + tt)
    sR = lambda tt: st_col(48 + tt)
    sBi = lambda tt: st_col(56 + tt)
    t1 = st_col(64)

    for tt in range(NTT):
        if tt > 0:
            w("sync", "vP")
        dma("sync", x_t[:, :], x_in[tt * 128:(tt + 1) * 128, :], "d_x")
        w("vector", "d_x")
        if tt == 0:
            w("vector", "vP", consts_v)
        step("vector", "vP", lambda e: e.tensor_mul(sq_t[:, :], x_t[:, :], x_t[:, :]))
        step("vector", "vP", lambda e, tt=tt: e.reduce_sum(sM(tt), x_t[:, :], axis=AX.X))
        w("vector", "vP")
        step("vector", "vP", lambda e, tt=tt: e.reduce_sum(sV(tt), sq_t[:, :], axis=AX.X))
        w("vector", "vP")
        step("vector", "vP", lambda e, tt=tt: e.tensor_scalar_mul(sM(tt), sM(tt), 1.0 / H))
        step("vector", "vP", lambda e, tt=tt: e.tensor_scalar_mul(sV(tt), sV(tt), 1.0 / H))
        w("vector", "vP")
        step("vector", "vP", lambda e, tt=tt: e.tensor_mul(t1, sM(tt), sM(tt)))
        w("vector", "vP")
        step("vector", "vP", lambda e, tt=tt: e.tensor_sub(sV(tt), sV(tt), t1))
        w("vector", "vP")
        step("vector", "vP", lambda e, tt=tt: e.tensor_scalar_add(sV(tt), sV(tt), 1e-5))
        w("vector", "vP")
        step("vector", "vP", lambda e, tt=tt: e.reciprocal(t1, sV(tt)))
        w("scalar", "vP")
        step("scalar", "sP", lambda e, tt=tt: e.activation(sR(tt), t1, AF.Sqrt))
        w("vector", "sP")
        step("vector", "vP", lambda e, tt=tt: e.tensor_mul(t1, sR(tt), sR(tt)))
        w("vector", "vP")
        step("vector", "vP", lambda e, tt=tt: e.tensor_mul(t1, t1, sV(tt)))
        w("vector", "vP")
        step("vector", "vP", lambda e: e.tensor_scalar(t1, t1, -0.5, 1.5, op0=ALU.mult, op1=ALU.add))
        w("vector", "vP")
        step("vector", "vP", lambda e, tt=tt: e.tensor_mul(sR(tt), sR(tt), t1))
        w("vector", "vP")
        step("vector", "vP", lambda e, tt=tt: e.tensor_mul(sBi(tt), sM(tt), sR(tt)))
        w("vector", "vP")
        step("vector", "vP", lambda e, tt=tt: e.tensor_scalar_mul(sBi(tt), sBi(tt), -1.0))
        w("vector", "vP")
        step("vector", "vP", lambda e, tt=tt: e.tensor_scalar(
            xnt[:, :], x_t[:, :], sR(tt), sBi(tt), op0=ALU.mult, op1=ALU.add))
        w("tensor", "vP")
        w("tensor", "d_c", 16 * 3)
        for c in range(HC):
            bk = 6 + (c % 2)
            if tt > 0 or c >= 2:
                w("tensor", "sP")
                w("tensor", "vP")
            step("tensor", "tP", lambda e, c=c, b=bk: e.transpose(
                ps(b)[:, 0:128], xnt[:, c * 128:(c + 1) * 128], ident[:, :]))
            w("scalar", "tP")
            w("vector", "tP")
            col = c * T + tt * 128
            step("scalar", "sP", lambda e, b=bk, col=col: e.copy(
                xn_hi[:, col:col + 128], ps(b)[:, 0:128]))
            w("vector", "sP")
            step("vector", "vP", lambda e, b=bk, col=col: e.scalar_tensor_tensor(
                xn_lo[:, col:col + 128], ps(b)[:, 0:128], 1.0,
                xn_hi[:, col:col + 128], op0=ALU.mult, op1=ALU.subtract))
    xn_v, xn_s = cnt["vP"], cnt["sP"]

    if phases < 2:
        return _finish(nc, ops, stack, sems, cnt)

    slab_raw = [B(s * 24 * KB, 8 * KB, F32) for s in range(2)]
    slab_thi = [B(s * 24 * KB + 8 * KB, 4 * KB, BF16) for s in range(2)]
    slab_tlo = [B(s * 24 * KB + 12 * KB, 4 * KB, BF16) for s in range(2)]
    wt_hi = [B(s * 24 * KB + 16 * KB, 4 * KB, BF16) for s in range(2)]
    wt_lo = [B(s * 24 * KB + 20 * KB, 4 * KB, BF16) for s in range(2)]
    stg_hi = B(48 * KB, 2 * KB, BF16)
    stg_lo = B(50 * KB, 2 * KB, BF16)
    vfm = B(52 * KB, 4 * KB, F32)
    vst_hi = B(56 * KB, 2 * KB, BF16)
    vst_lo = B(58 * KB, 2 * KB, BF16)
    slab_free = [(0, 0, 0), (0, 0, 0)]

    def prep_slab(slot, src_ap, dsem, scale_col=None):
        raw, thi, tlo = slab_raw[slot], slab_thi[slot], slab_tlo[slot]
        whi, wlo = wt_hi[slot], wt_lo[slot]
        tm_, sm_, vm_ = slab_free[slot]
        w("sync", "tP", tm_)
        dma("sync", raw[:, :], src_ap, dsem)
        w("scalar", dsem)
        w("scalar", "sP", sm_)
        w("scalar", "vP", vm_)
        if scale_col is not None:
            step("scalar", "sP", lambda e, r=raw, sc=scale_col: e.mul(r[:, :], r[:, :], sc))
            w("scalar", "sP")
        step("scalar", "sP", lambda e, r=raw, o=thi: e.copy(o[:, :], r[:, :]))
        w("vector", "sP")
        w("vector", "vP", vm_)
        step("vector", "vP", lambda e, r=raw, h=thi, o=tlo: e.scalar_tensor_tensor(
            o[:, :], r[:, :], 1.0, h[:, :], op0=ALU.mult, op1=ALU.subtract))
        w("tensor", "sP")
        w("tensor", "vP")
        for g in range(4):
            bk = 6 + (g % 2)
            if g >= 2:
                w("tensor", "sP")
                w("tensor", "vP")
            for k in range(4):
                c = g * 4 + k
                step("tensor", "tP", lambda e, c=c, b=bk, k=k, s=thi: e.transpose(
                    psb(b)[:, k * 256:k * 256 + 128], s[:, c * 128:(c + 1) * 128], identb[:, :]))
                step("tensor", "tP", lambda e, c=c, b=bk, k=k, s=tlo: e.transpose(
                    psb(b)[:, k * 256 + 128:k * 256 + 256], s[:, c * 128:(c + 1) * 128], identb[:, :]))
            w("vector", "tP")
            step("vector", "vP", lambda e, b=bk, g=g, o=whi: e.tensor_copy(
                o.rearrange("p (c q) -> p c q", q=128)[:, g * 4:(g + 1) * 4, :],
                psb(b).rearrange("p (c q) -> p c q", q=256)[:, :, 0:128]))
            step("vector", "vP", lambda e, b=bk, g=g, o=wlo: e.tensor_copy(
                o.rearrange("p (c q) -> p c q", q=128)[:, g * 4:(g + 1) * 4, :],
                psb(b).rearrange("p (c q) -> p c q", q=256)[:, :, 128:256]))
        return cnt["sP"], cnt["vP"]

    def mm3(bank, a_h, a_l, b_h, b_l, first, last):
        em("tensor", lambda e, o=bank, a=a_h, b=b_h, st=first: e.matmul(
            ps(o)[:, 0:b.shape[1]], a, b, start=st, stop=False))
        em("tensor", lambda e, o=bank, a=a_h, b=b_l: e.matmul(
            ps(o)[:, 0:b.shape[1]], a, b, start=False, stop=False))
        if not last:
            em("tensor", lambda e, o=bank, a=a_l, b=b_h: e.matmul(
                ps(o)[:, 0:b.shape[1]], a, b, start=False, stop=False))
        else:
            step("tensor", "tP", lambda e, o=bank, a=a_l, b=b_h: e.matmul(
                ps(o)[:, 0:b.shape[1]], a, b, start=False, stop=True))

    order = [(j, "k") for j in range(HC, 2 * HC)] + \
            [(j, "v") for j in range(2 * HC, 3 * HC)] + \
            [(j, "q") for j in range(0, HC)]
    if qkv_tiles is not None:
        order = order[:qkv_tiles]
    qkv_banks = [(0, 1), (2, 3)]
    stg_free = (0, 0)
    for idx, (j, kind) in enumerate(order):
        slot = idx % 2
        tiles_s, tiles_v = prep_slab(slot, w_qkv[j * 128:(j + 1) * 128, :],
                                     "d_wa" if slot == 0 else "d_wb")
        bks = qkv_banks[idx % 2]
        if idx == 0:
            w("tensor", "vP", xn_v)
            w("tensor", "sP", xn_s)
        w("tensor", "sP", tiles_s)
        w("tensor", "vP", tiles_v)
        if idx >= 2:
            w("tensor", "sP")
            w("tensor", "vP")
        whi, wlo = wt_hi[slot], wt_lo[slot]
        for half in range(2):
            tb = half * 512
            for c in range(HC):
                mm3(bks[half],
                    whi[:, c * 128:(c + 1) * 128], wlo[:, c * 128:(c + 1) * 128],
                    xn_hi[:, c * T + tb:c * T + tb + 512], xn_lo[:, c * T + tb:c * T + tb + 512],
                    c == 0, c == HC - 1)
        slab_free[slot] = (cnt["tP"], cnt["sP"], cnt["vP"])
        bias = bqkv[:, j:j + 1]
        w("scalar", "tP")
        w("vector", "tP")
        w("scalar", "d_st")
        w("vector", "d_st")
        for half in range(2):
            tb = half * 512
            step("scalar", "sP", lambda e, b=bks[half], tb=tb, bias=bias: e.add(
                stg_hi[:, tb:tb + 512], ps(b), bias))
            w("vector", "sP")
            step("vector", "vP", lambda e, b=bks[half], tb=tb, bias=bias: e.scalar_tensor_tensor(
                stg_lo[:, tb:tb + 512], ps(b), bias, stg_hi[:, tb:tb + 512],
                op0=ALU.add, op1=ALU.subtract))
        w("gpsimd", "sP")
        w("gpsimd", "vP")
        if kind == "k":
            r = j - HC
            dma("gpsimd", kfm(0)[r * 128:(r + 1) * 128, :], stg_hi[:, :], "d_st")
            dma("gpsimd", kfm(1)[r * 128:(r + 1) * 128, :], stg_lo[:, :], "d_st")
        elif kind == "q":
            dma("gpsimd", qsp.ap()[0, j * 128:(j + 1) * 128, :], stg_hi[:, :], "d_st")
            dma("gpsimd", qsp.ap()[1, j * 128:(j + 1) * 128, :], stg_lo[:, :], "d_st")
        else:
            r = j - 2 * HC
            w("vector", "vP")
            step("vector", "vP", lambda e: e.tensor_add(vfm[:, :], stg_hi[:, :], stg_lo[:, :]))
            w("tensor", "vP")
            for tt in range(NTT):
                bk = 4 + (tt % 2)
                if tt >= 2:
                    w("tensor", "sP")
                    w("tensor", "vP")
                step("tensor", "tP", lambda e, tt=tt, b=bk: e.transpose(
                    ps(b)[:, 0:128], vfm[:, tt * 128:(tt + 1) * 128], ident[:, :]))
                w("scalar", "tP")
                w("vector", "tP")
                step("scalar", "sP", lambda e, b=bk, tt=tt: e.copy(
                    vst_hi[:, tt * 128:(tt + 1) * 128], ps(b)[:, 0:128]))
                w("vector", "sP")
                step("vector", "vP", lambda e, b=bk, tt=tt: e.scalar_tensor_tensor(
                    vst_lo[:, tt * 128:(tt + 1) * 128], ps(b)[:, 0:128], 1.0,
                    vst_hi[:, tt * 128:(tt + 1) * 128], op0=ALU.mult, op1=ALU.subtract))
            w("gpsimd", "sP")
            w("gpsimd", "vP")
            vh = gin.ap()[2].rearrange("(a p Hh) -> p a Hh", p=128, Hh=H)[:, :, r * 128:(r + 1) * 128]
            vl = gin.ap()[3].rearrange("(a p Hh) -> p a Hh", p=128, Hh=H)[:, :, r * 128:(r + 1) * 128]
            dma("gpsimd", vh, vst_hi.rearrange("p (a q) -> p a q", q=128), "d_st")
            dma("gpsimd", vl, vst_lo.rearrange("p (a q) -> p a q", q=128), "d_st")

    w("gpsimd", "d_st")
    if phases < 3:
        if dbg:
            dma("gpsimd", gin_o.ap(), gin.ap(), "d_st")
        return _finish(nc, ops, stack, sems, cnt)
    step("gpsimd", "cc", lambda e: e.collective_compute(
        "AllGather", ALU.bypass, replica_groups=[list(range(NCORE))],
        ins=[gin.ap()], outs=[gout.ap()]))
    if dbg:
        w("gpsimd", "cc")
        dma("gpsimd", gin_o.ap(), gin.ap(), "d_st")
        dma("gpsimd", gout_o.ap(), gout.ap(), "d_st")

    if phases < 3:
        return _finish(nc, ops, stack, sems, cnt)

    for cp in range(HC):
        bks = qkv_banks[cp % 2]
        if cp >= 2:
            w("tensor", "sP")
            w("tensor", "vP")
        for half in range(2):
            tb = half * 512
            for c in range(HC):
                hs = hsu_p if H16[c, cp] > 0 else hsu_n
                a = hs[:, c * 128:(c + 1) * 128]
                b_h = xn_hi[:, c * T + tb:c * T + tb + 512]
                b_l = xn_lo[:, c * T + tb:c * T + tb + 512]
                em("tensor", lambda e, o=bks[half], a=a, b=b_h, st=(c == 0): e.matmul(
                    ps(o), a, b, start=st, stop=False))
                if c < HC - 1:
                    em("tensor", lambda e, o=bks[half], a=a, b=b_l: e.matmul(
                        ps(o), a, b, start=False, stop=False))
                else:
                    step("tensor", "tP", lambda e, o=bks[half], a=a, b=b_l: e.matmul(
                        ps(o), a, b, start=False, stop=True))
        w("scalar", "tP")
        w("vector", "tP")
        w("scalar", "d_st")
        w("vector", "d_st")
        for half in range(2):
            tb = half * 512
            step("scalar", "sP", lambda e, b=bks[half], tb=tb: e.copy(stg_hi[:, tb:tb + 512], ps(b)))
            w("vector", "sP")
            step("vector", "vP", lambda e, b=bks[half], tb=tb: e.scalar_tensor_tensor(
                stg_lo[:, tb:tb + 512], ps(b), 1.0, stg_hi[:, tb:tb + 512],
                op0=ALU.mult, op1=ALU.subtract))
        w("gpsimd", "sP")
        w("gpsimd", "vP")
        dma("gpsimd", zsp.ap()[0, cp * 128:(cp + 1) * 128, :], stg_hi[:, :], "d_st")
        dma("gpsimd", zsp.ap()[1, cp * 128:(cp + 1) * 128, :], stg_lo[:, :], "d_st")
    fwht1_t = cnt["tP"]

    if phases < 4:
        return _finish(nc, ops, stack, sems, cnt)

    z_hi = A(0, 32 * KB, BF16)
    z_lo = A(32 * KB, 32 * KB, BF16)
    xup_f = C(24 * KB, 2 * KB, F32)
    xgt_f = C(26 * KB, 2 * KB, F32)
    sg_f = C(28 * KB, 2 * KB, F32)
    u_f = C(30 * KB, 2 * KB, F32)
    w("sync", "d_st")
    w("sync", "tP", fwht1_t)
    dma("sync", d3(z_hi, T), fm3(zsp.ap()[0]), "d_z")
    dma("sync", d3(z_lo, T), fm3(zsp.ap()[1]), "d_z")

    for j in range(IC):
        t0s, t0v = prep_slab(0, w_up[j * 128:(j + 1) * 128, :], "d_wa")
        t1s, t1v = prep_slab(1, w_gate[j * 128:(j + 1) * 128, :], "d_wb")
        if j == 0:
            w("tensor", "d_z")
        w("tensor", "sP", t1s)
        w("tensor", "vP", t1v)
        for half in range(2):
            tb = half * 512
            bu, bg = qkv_banks[half]
            if j > 0 or half > 0:
                w("tensor", "sP")
                w("tensor", "vP")
            for c in range(HC):
                mm3(bu, wt_hi[0][:, c * 128:(c + 1) * 128], wt_lo[0][:, c * 128:(c + 1) * 128],
                    z_hi[:, c * T + tb:c * T + tb + 512], z_lo[:, c * T + tb:c * T + tb + 512],
                    c == 0, c == HC - 1)
            for c in range(HC):
                mm3(bg, wt_hi[1][:, c * 128:(c + 1) * 128], wt_lo[1][:, c * 128:(c + 1) * 128],
                    z_hi[:, c * T + tb:c * T + tb + 512], z_lo[:, c * T + tb:c * T + tb + 512],
                    c == 0, c == HC - 1)
            w("vector", "tP")
            w("vector", "d_st")
            step("vector", "vP", lambda e, b=bu, j=j: e.tensor_scalar_mul(
                xup_f[:, :], ps(b), s_up[:, j:j + 1]))
            step("vector", "vP", lambda e, b=bg, j=j: e.tensor_scalar_mul(
                xgt_f[:, :], ps(b), s_gt[:, j:j + 1]))
            w("scalar", "vP")
            step("scalar", "sP", lambda e: e.activation(sg_f[:, :], xgt_f[:, :], AF.Silu))
            w("vector", "sP")
            step("vector", "vP", lambda e, j=j: e.scalar_tensor_tensor(
                u_f[:, :], sg_f[:, :], sudp[:, j:j + 1], xup_f[:, :], op0=ALU.mult, op1=ALU.mult))
            w("scalar", "vP")
            w("scalar", "d_st")
            step("scalar", "sP", lambda e, tb=tb: e.copy(stg_hi[:, tb:tb + 512], u_f[:, :]))
            w("vector", "sP")
            step("vector", "vP", lambda e, tb=tb: e.scalar_tensor_tensor(
                stg_lo[:, tb:tb + 512], u_f[:, :], 1.0, stg_hi[:, tb:tb + 512],
                op0=ALU.mult, op1=ALU.subtract))
        slab_free[0] = (cnt["tP"], cnt["sP"], cnt["vP"])
        slab_free[1] = (cnt["tP"], cnt["sP"], cnt["vP"])
        w("gpsimd", "sP")
        w("gpsimd", "vP")
        dma("gpsimd", usp.ap()[0, j * 128:(j + 1) * 128, :], stg_hi[:, :], "d_st")
        dma("gpsimd", usp.ap()[1, j * 128:(j + 1) * 128, :], stg_lo[:, :], "d_st")

    for j in range(HC):
        for s4 in range(4):
            slot = s4 % 2
            ts_, tv_ = prep_slab(slot, w_down[j * 128:(j + 1) * 128, s4 * H:(s4 + 1) * H],
                                 "d_wa" if slot == 0 else "d_wb",
                                 scale_col=s_dpc[:, j:j + 1])
            w("gpsimd", "sP", ts_)
            w("gpsimd", "vP", tv_)
            dma("gpsimd", wdfm.ap()[j, 0, s4 * HC:(s4 + 1) * HC].rearrange("c p q -> p c q"),
                d3(wt_hi[slot][:, :], 128), "d_wd")
            dma("gpsimd", wdfm.ap()[j, 1, s4 * HC:(s4 + 1) * HC].rearrange("c p q -> p c q"),
                d3(wt_lo[slot][:, :], 128), "d_wd")
            slab_free[slot] = (cnt["tP"], cnt["sP"], cnt["vP"])
            w("sync", "d_wd")
        ts_, tv_ = prep_slab(0, w_out[j * 128:(j + 1) * 128, :], "d_wa")
        w("gpsimd", "sP", ts_)
        w("gpsimd", "vP", tv_)
        dma("gpsimd", wofm.ap()[j, 0].rearrange("c p q -> p c q"), d3(wt_hi[0][:, :], 128), "d_wo")
        dma("gpsimd", wofm.ap()[j, 1].rearrange("c p q -> p c q"), d3(wt_lo[0][:, :], 128), "d_wo")
        slab_free[0] = (cnt["tP"], cnt["sP"], cnt["vP"])
        w("sync", "d_wo")

    if phases < 5:
        return _finish(nc, ops, stack, sems, cnt)

    ctx_sb = A(0, 64 * KB, F32)
    kvs_hi = [B(0, 16 * KB, BF16), B(32 * KB, 16 * KB, BF16)]
    kvs_lo = [B(16 * KB, 16 * KB, BF16), B(48 * KB, 16 * KB, BF16)]
    q_hi = C(0, 4 * KB, BF16)
    q_lo = C(4 * KB, 4 * KB, BF16)
    sc_sb = [C(8 * KB + qt * 2 * KB, 2 * KB, F32) for qt in range(NQ)]
    pt_hi = C(24 * KB, 8 * KB, BF16)
    pt_lo = C(32 * KB, 8 * KB, BF16)
    pstg_h = C(40 * KB, 1 * KB, BF16)
    pstg_l = C(41 * KB, 1 * KB, BF16)
    aM = lambda qt: st_col(72 + qt)
    aL = lambda qt: st_col(80 + qt)
    aMn = lambda qt: st_col(88 + qt)
    aRs = lambda qt: st_col(96 + qt)
    aBi = lambda qt: st_col(104 + qt)
    aT = st_col(112)

    w("vector", "d_st")
    step("vector", "vP", lambda e: e.memset(stat[:, 72:80], -1e30))
    step("vector", "vP", lambda e: e.memset(stat[:, 80:88], 0.0))
    attn_init_v = cnt["vP"]

    for s in range(NS):
        rank, off = s // 2, (s % 2) * 512
        if s == 0:
            w("sync", "cc", 1)
            w("sync", "vP", attn_init_v)
        else:
            w("sync", "tP")
        dma("sync", d3(kvs_hi[0][:, :], 512), fm3(g_k(rank, 0)[:, off:off + 512]), "d_k")
        dma("sync", d3(kvs_lo[0][:, :], 512), fm3(g_k(rank, 1)[:, off:off + 512]), "d_k")
        dma("sync", d3(kvs_hi[1][:, :], H), fm3(g_v(rank, 0)[off:off + 512, :]), "d_v")
        dma("sync", d3(kvs_lo[1][:, :], H), fm3(g_v(rank, 1)[off:off + 512, :]), "d_v")
        for qt in range(NQ):
            if s > 0 or qt > 0:
                w("sync", "tP")
            dma("sync", d3(q_hi[:, :], 128), fm3(qsp.ap()[0, :, qt * 128:(qt + 1) * 128]), "d_q")
            dma("sync", d3(q_lo[:, :], 128), fm3(qsp.ap()[1, :, qt * 128:(qt + 1) * 128]), "d_q")
            bk = 4 + (qt % 2)
            w("tensor", "d_q")
            if qt == 0:
                w("tensor", "d_k")
                if s == 0:
                    w("tensor", "vP", attn_init_v)
            if s > 0 or qt >= 2:
                w("tensor", "vP")
            for c in range(HC):
                mm3(bk, q_hi[:, c * 128:(c + 1) * 128], q_lo[:, c * 128:(c + 1) * 128],
                    kvs_hi[0][:, c * 512:(c + 1) * 512], kvs_lo[0][:, c * 512:(c + 1) * 512],
                    c == 0, c == HC - 1)
            w("vector", "tP")
            step("vector", "vP", lambda e, b=bk, qt=qt: e.tensor_copy(sc_sb[qt][:, :], ps(b)))
            w("vector", "vP")
            step("vector", "vP", lambda e, qt=qt: e.reduce_max(aMn(qt), sc_sb[qt][:, :], axis=AX.X))
            w("vector", "vP")
            step("vector", "vP", lambda e, qt=qt: e.tensor_max(aMn(qt), aMn(qt), aM(qt)))
            w("vector", "vP")
            step("vector", "vP", lambda e, qt=qt: e.tensor_sub(aRs(qt), aM(qt), aMn(qt)))
            step("vector", "vP", lambda e, qt=qt: e.tensor_scalar_mul(aBi(qt), aMn(qt), -SCL))
            step("vector", "vP", lambda e, qt=qt: e.tensor_copy(aM(qt), aMn(qt)))
            w("scalar", "vP")
            step("scalar", "sP", lambda e, qt=qt: e.activation(aRs(qt), aRs(qt), AF.Exp, scale=SCL))
            step("scalar", "sP", lambda e, qt=qt: e.activation(
                sc_sb[qt][:, :], sc_sb[qt][:, :], AF.Exp, bias=aBi(qt), scale=SCL))
            w("vector", "sP")
            step("vector", "vP", lambda e, qt=qt: e.reduce_sum(aT, sc_sb[qt][:, :], axis=AX.X))
            w("vector", "vP")
            step("vector", "vP", lambda e, qt=qt: e.scalar_tensor_tensor(
                aL(qt), aL(qt), aRs(qt), aT, op0=ALU.mult, op1=ALU.add))
            w("scalar", "tP")
            step("scalar", "sP", lambda e, qt=qt: e.copy(pstg_h[:, :], sc_sb[qt][:, :]))
            w("vector", "sP")
            step("vector", "vP", lambda e, qt=qt: e.scalar_tensor_tensor(
                pstg_l[:, :], sc_sb[qt][:, :], 1.0, pstg_h[:, :], op0=ALU.mult, op1=ALU.subtract))
            bk_t = 6 + (qt % 2)
            w("tensor", "sP")
            w("tensor", "vP")
            for c in range(4):
                step("tensor", "tP", lambda e, c=c, b=bk_t: e.transpose(
                    psb(b)[:, c * 256:c * 256 + 128], pstg_h[:, c * 128:(c + 1) * 128], identb[:, :]))
                step("tensor", "tP", lambda e, c=c, b=bk_t: e.transpose(
                    psb(b)[:, c * 256 + 128:c * 256 + 256], pstg_l[:, c * 128:(c + 1) * 128], identb[:, :]))
            w("vector", "tP")
            step("vector", "vP", lambda e, b=bk_t, qt=qt: e.tensor_copy(
                pt_hi.rearrange("p (c g q) -> p c g q", g=NQ, q=128)[:, :, qt, :],
                psb(b).rearrange("p (c q) -> p c q", q=256)[:, :, 0:128]))
            step("vector", "vP", lambda e, b=bk_t, qt=qt: e.tensor_copy(
                pt_lo.rearrange("p (c g q) -> p c g q", g=NQ, q=128)[:, :, qt, :],
                psb(b).rearrange("p (c q) -> p c q", q=256)[:, :, 128:256]))
        w("tensor", "d_v")
        w("tensor", "vP")
        for pair in range(NQ // 2):
            if pair > 0:
                w("tensor", "vP")
            for qi in range(2):
                qt = pair * 2 + qi
                gbase = qi * 4
                for c in range(4):
                    a_h = pt_hi.rearrange("p (c g q) -> p c g q", g=NQ, q=128)[:, c, qt, :]
                    a_l = pt_lo.rearrange("p (c g q) -> p c g q", g=NQ, q=128)[:, c, qt, :]
                    for hb in range(4):
                        b_h = kvs_hi[1][:, c * 2048 + hb * 512:c * 2048 + (hb + 1) * 512]
                        b_l = kvs_lo[1][:, c * 2048 + hb * 512:c * 2048 + (hb + 1) * 512]
                        mm3(gbase + hb, a_h, a_l, b_h, b_l, c == 0, c == 3)
            w("vector", "tP")
            for qi in range(2):
                qt = pair * 2 + qi
                for hb in range(4):
                    step("vector", "vP", lambda e, qt=qt, b=qi * 4 + hb, hb=hb: e.scalar_tensor_tensor(
                        ctx_sb[:, qt * H + hb * 512:qt * H + (hb + 1) * 512],
                        ctx_sb[:, qt * H + hb * 512:qt * H + (hb + 1) * 512],
                        aRs(qt), ps(b), op0=ALU.mult, op1=ALU.add))

    cth = B(0, 4 * KB, BF16)
    ctl = B(4 * KB, 4 * KB, BF16)
    cfs_h = B(8 * KB, 4 * KB, BF16)
    cfs_l = B(12 * KB, 4 * KB, BF16)
    for qt in range(NQ):
        w("vector", "tP")
        w("scalar", "vP")
        step("vector", "vP", lambda e, qt=qt: e.reciprocal(aT, aL(qt)))
        w("scalar", "vP")
        step("scalar", "sP", lambda e, qt=qt: e.mul(cth[:, :], ctx_sb[:, qt * H:(qt + 1) * H], aT))
        w("vector", "sP")
        step("vector", "vP", lambda e, qt=qt: e.scalar_tensor_tensor(
            ctl[:, :], ctx_sb[:, qt * H:(qt + 1) * H], aT, cth[:, :],
            op0=ALU.mult, op1=ALU.subtract))
        w("tensor", "sP")
        w("tensor", "vP")
        for g in range(4):
            bk = 6 + (g % 2)
            if g >= 2 or qt > 0:
                w("tensor", "vP")
            for k in range(4):
                c = g * 4 + k
                step("tensor", "tP", lambda e, c=c, b=bk, k=k: e.transpose(
                    psb(b)[:, k * 256:k * 256 + 128], cth[:, c * 128:(c + 1) * 128], identb[:, :]))
                step("tensor", "tP", lambda e, c=c, b=bk, k=k: e.transpose(
                    psb(b)[:, k * 256 + 128:k * 256 + 256], ctl[:, c * 128:(c + 1) * 128], identb[:, :]))
            w("vector", "tP")
            w("vector", "d_ctx")
            step("vector", "vP", lambda e, b=bk, g=g: e.tensor_copy(
                cfs_h.rearrange("p (c q) -> p c q", q=128)[:, g * 4:(g + 1) * 4, :],
                psb(b).rearrange("p (c q) -> p c q", q=256)[:, :, 0:128]))
            step("vector", "vP", lambda e, b=bk, g=g: e.tensor_copy(
                cfs_l.rearrange("p (c q) -> p c q", q=128)[:, g * 4:(g + 1) * 4, :],
                psb(b).rearrange("p (c q) -> p c q", q=256)[:, :, 128:256]))
        w("gpsimd", "vP")
        dma("gpsimd", fm3(ctxsp.ap()[0, :, qt * 128:(qt + 1) * 128]), d3(cfs_h[:, :], 128), "d_ctx")
        dma("gpsimd", fm3(ctxsp.ap()[1, :, qt * 128:(qt + 1) * 128]), d3(cfs_l[:, :], 128), "d_ctx")

    if phases < 6:
        return _finish(nc, ops, stack, sems, cnt)

    u_hi = A(0, 32 * KB, BF16)
    u_lo = A(32 * KB, 32 * KB, BF16)
    w1f = B(0, 64 * KB, F32)
    w1tmp = C(0, 32 * KB, F32)
    ctxq_h = C(0, 8 * KB, BF16)
    ctxq_l = C(8 * KB, 8 * KB, BF16)
    wo_h = [C(16 * KB + 8 * k * KB, 4 * KB, BF16) for k in range(2)]
    wo_l = [C(20 * KB + 8 * k * KB, 4 * KB, BF16) for k in range(2)]
    comb = C(32 * KB, 16 * KB, F32)
    resid = C(48 * KB, 8 * KB, F32)
    wd_h = [B(0, 16 * KB, BF16), B(32 * KB, 16 * KB, BF16)]
    wd_l = [B(16 * KB, 16 * KB, BF16), B(48 * KB, 16 * KB, BF16)]
    ostg = B(0, 8 * KB, F32)

    for q in range(4):
        cb = q * QT4
        if q == 0:
            w("sync", "d_ctx")
            w("sync", "tP")
            w("sync", "vP")
        else:
            w("sync", "tP")
        dma("sync", d3(u_hi[:, :], QT4), fm3(usp.ap()[0])[:, :, cb:cb + QT4], "d_u")
        dma("sync", d3(u_lo[:, :], QT4), fm3(usp.ap()[1])[:, :, cb:cb + QT4], "d_u")
        w("tensor", "d_u")
        for cp in range(IC):
            bk = cp % 2
            c0, c1 = (cp % 32), (cp % 32) + 32
            hsgn = h128 if cp < 32 else h128n
            if cp >= 2:
                w("tensor", "vP")
            em("tensor", lambda e, b=bk, a=h128, c=c0: e.matmul(
                ps(b)[:, 0:QT4], a[:, :], u_hi[:, c * QT4:(c + 1) * QT4], start=True, stop=False))
            em("tensor", lambda e, b=bk, a=h128, c=c0: e.matmul(
                ps(b)[:, 0:QT4], a[:, :], u_lo[:, c * QT4:(c + 1) * QT4], start=False, stop=False))
            em("tensor", lambda e, b=bk, a=hsgn, c=c1: e.matmul(
                ps(b)[:, 0:QT4], a[:, :], u_hi[:, c * QT4:(c + 1) * QT4], start=False, stop=False))
            step("tensor", "tP", lambda e, b=bk, a=hsgn, c=c1: e.matmul(
                ps(b)[:, 0:QT4], a[:, :], u_lo[:, c * QT4:(c + 1) * QT4], start=False, stop=True))
            w("vector", "tP")
            step("vector", "vP", lambda e, b=bk, cp=cp: e.tensor_copy(
                w1f[:, cp * QT4:(cp + 1) * QT4], ps(b)[:, 0:QT4]))
        for bit in (16, 8, 4, 2, 1):
            na, nb = IC // (2 * bit), bit
            va = w1f.rearrange("p (a d t) -> p a d t", a=na, d=2 * bit)[:, :, 0:bit, :]
            vb = w1f.rearrange("p (a d t) -> p a d t", a=na, d=2 * bit)[:, :, bit:2 * bit, :]
            vt = w1tmp.rearrange("p (a d t) -> p a d t", a=na, d=bit)
            w("vector", "vP")
            step("vector", "vP", lambda e, a=va, b=vb, t=vt: e.tensor_sub(t, a, b))
            w("vector", "vP")
            step("vector", "vP", lambda e, a=va, b=vb: e.tensor_add(a, a, b))
            w("vector", "vP")
            step("vector", "vP", lambda e, b=vb, t=vt: e.tensor_copy(b, t))
        w("vector", "vP")
        w("scalar", "vP")
        step("scalar", "sP", lambda e: e.copy(u_hi[:, :], w1f[:, :]))
        w("vector", "sP")
        step("vector", "vP", lambda e: e.scalar_tensor_tensor(
            u_lo[:, :], w1f[:, :], 1.0, u_hi[:, :], op0=ALU.mult, op1=ALU.subtract))
        w("sync", "vP")
        dma("sync", d3(ctxq_h[:, :], QT4), fm3(ctxsp.ap()[0])[:, :, cb:cb + QT4], "d_ctx")
        dma("sync", d3(ctxq_l[:, :], QT4), fm3(ctxsp.ap()[1])[:, :, cb:cb + QT4], "d_ctx")
        for j in range(HC):
            slot = j % 2
            if j >= 2:
                w("sync", "tP")
            elif q > 0:
                w("sync", "tP")
                w("sync", "d_out")
            dma("sync", d3(wd_h[slot][:, :], 128), wdfm.ap()[j, 0].rearrange("c p q -> p c q"), "d_wd")
            dma("sync", d3(wd_l[slot][:, :], 128), wdfm.ap()[j, 1].rearrange("c p q -> p c q"), "d_wd")
            dma("sync", d3(wo_h[slot][:, :], 128), wofm.ap()[j, 0].rearrange("c p q -> p c q"), "d_wo")
            dma("sync", d3(wo_l[slot][:, :], 128), wofm.ap()[j, 1].rearrange("c p q -> p c q"), "d_wo")
            bk = 4 + (j % 2)
            w("tensor", "d_wd")
            w("tensor", "d_wo")
            if j == 0:
                w("tensor", "sP")
                w("tensor", "vP")
                w("tensor", "d_ctx")
            if j >= 2:
                w("tensor", "sP")
            for c in range(IC):
                mm3(bk, wd_h[slot][:, c * 128:(c + 1) * 128], wd_l[slot][:, c * 128:(c + 1) * 128],
                    u_hi[:, c * QT4:(c + 1) * QT4], u_lo[:, c * QT4:(c + 1) * QT4],
                    c == 0, False)
            for c in range(HC):
                mm3(bk, wo_h[slot][:, c * 128:(c + 1) * 128], wo_l[slot][:, c * 128:(c + 1) * 128],
                    ctxq_h[:, c * QT4:(c + 1) * QT4], ctxq_l[:, c * QT4:(c + 1) * QT4],
                    False, c == HC - 1)
            w("scalar", "tP")
            step("scalar", "sP", lambda e, b=bk, j=j: e.add(
                comb[:, j * QT4:(j + 1) * QT4], ps(b)[:, 0:QT4], boutc[:, j:j + 1]))
        for tt in range(2):
            trow = cb + tt * 128
            w("sync", "d_out")
            dma("sync", resid[:, :], x_in[trow:trow + 128, :], "d_x")
            w("tensor", "sP")
            w("tensor", "d_out")
            for g in range(4):
                bk = 2 + (g % 2)
                if g >= 2 or tt > 0 or q > 0:
                    w("tensor", "vP")
                for k in range(4):
                    j = g * 4 + k
                    step("tensor", "tP", lambda e, j=j, b=bk, k=k, tt=tt: e.transpose(
                        ps(b)[:, k * 128:(k + 1) * 128],
                        comb[:, j * QT4 + tt * 128:j * QT4 + tt * 128 + 128], ident[:, :]))
                w("vector", "tP")
                w("vector", "d_x")
                step("vector", "vP", lambda e, b=bk, g=g: e.tensor_add(
                    ostg[:, g * 512:(g + 1) * 512], ps(b), resid[:, g * 512:(g + 1) * 512]))
            w("gpsimd", "vP")
            dma("gpsimd", out_d[trow:trow + 128, :], ostg[:, :], "d_out")

    return _finish(nc, ops, stack, sems, cnt)


def _finish(nc, ops, stack, sems, cnt):
    em = lambda eng, fn: ops[eng].append(fn)
    if cnt["d_out"]:
        em("gpsimd", lambda e: e.wait_ge(sems["d_out"], cnt["d_out"]))
    if cnt["d_st"]:
        em("gpsimd", lambda e: e.wait_ge(sems["d_st"], cnt["d_st"]))
    if cnt["d_ctx"]:
        em("gpsimd", lambda e: e.wait_ge(sems["d_ctx"], cnt["d_ctx"]))
    with nc.Block() as block:
        @block.sync
        def _(e):
            for f in ops["sync"]:
                f(e)

        @block.gpsimd
        def _(e):
            for f in ops["gpsimd"]:
                f(e)

        @block.vector
        def _(e):
            for f in ops["vector"]:
                f(e)

        @block.scalar
        def _(e):
            for f in ops["scalar"]:
                f(e)

        @block.tensor
        def _(e):
            for f in ops["tensor"]:
                f(e)
    stack.close()
    return nc


def kernel(L0_attn_hidden_states, L0_attn_w_qkv, L0_attn_b_qkv, L0_attn_w_out,
           L0_attn_b_out, L0_ffn_SU_ug, L0_ffn_SU_dp, L0_ffn_Wscale_ug,
           L0_ffn_Wscale_dp, L0_ffn_W_up, L0_ffn_W_gate, L0_ffn_W_down):
    x = np.ascontiguousarray(np.asarray(L0_attn_hidden_states, dtype=np.float32).reshape(NCORE * T, H))
    full = {
        "w_qkv": np.ascontiguousarray(np.asarray(L0_attn_w_qkv, dtype=np.float32)),
        "b_qkv": np.ascontiguousarray(np.asarray(L0_attn_b_qkv, dtype=np.float32)),
        "w_out": np.ascontiguousarray(np.asarray(L0_attn_w_out, dtype=np.float32)),
        "b_out": np.ascontiguousarray(np.asarray(L0_attn_b_out, dtype=np.float32)),
        "su_ug": np.ascontiguousarray(np.asarray(L0_ffn_SU_ug, dtype=np.float32)),
        "su_dp": np.ascontiguousarray(np.asarray(L0_ffn_SU_dp, dtype=np.float32)),
        "wsc_ug": np.ascontiguousarray(np.asarray(L0_ffn_Wscale_ug, dtype=np.float32)),
        "wsc_dp": np.ascontiguousarray(np.asarray(L0_ffn_Wscale_dp, dtype=np.float32)),
        "w_up": np.ascontiguousarray(np.asarray(L0_ffn_W_up, dtype=np.float32)),
        "w_gate": np.ascontiguousarray(np.asarray(L0_ffn_W_gate, dtype=np.float32)),
        "w_down": np.ascontiguousarray(np.asarray(L0_ffn_W_down, dtype=np.float32)),
    }
    nc = build_program()
    in_maps = [dict(full, x=x[c * T:(c + 1) * T]) for c in range(NCORE)]
    res = run_bass_kernel_spmd(nc, in_maps, core_ids=list(range(NCORE)))
    out = np.concatenate([res.results[c]["out"] for c in range(NCORE)], axis=0)
    return out.reshape(1, NCORE * T, H).astype(np.float32)


# version 05
# speedup vs baseline: 1.3710x; 1.3710x over previous
import numpy as np
import ml_dtypes
import concourse.bass as bass
import concourse.mybir as mybir
from concourse.bass_utils import run_bass_kernel_spmd

F32 = mybir.dt.float32
BF16 = mybir.dt.bfloat16
AF = mybir.ActivationFunctionType
ALU = mybir.AluOpType
AX = mybir.AxisListType

NCORE = 8
T = 1024
NTT = T // 128
H = 2048
HC = H // 128
I = 8192
IC = I // 128
SB = 512
NS = (NCORE * T) // SB
NQ = NTT
QT4 = T // 4
RS2 = float(1.0 / np.sqrt(2048.0))
RS8 = float(1.0 / np.sqrt(8192.0))
SCL = float(1.0 / np.sqrt(2048.0))
KB = 1024


def _hadamard(n):
    h = np.array([[1.0]], dtype=np.float64)
    while h.shape[0] < n:
        h = np.block([[h, h], [h, -h]])
    return h


H16 = _hadamard(16)


def build_program(dbg=False, phases=99):
    nc = bass.Bass("TRN2", target_bir_lowering=False, detect_race_conditions=False)

    x_in = nc.dram_tensor("x", [T, H], F32, kind="ExternalInput").ap()
    w_qkv = nc.dram_tensor("w_qkv", [3 * H, H], F32, kind="ExternalInput").ap()
    b_qkv = nc.dram_tensor("b_qkv", [3 * H], F32, kind="ExternalInput").ap()
    w_out = nc.dram_tensor("w_out", [H, H], F32, kind="ExternalInput").ap()
    b_out = nc.dram_tensor("b_out", [H], F32, kind="ExternalInput").ap()
    su_ug = nc.dram_tensor("su_ug", [H], F32, kind="ExternalInput").ap()
    su_dp = nc.dram_tensor("su_dp", [I], F32, kind="ExternalInput").ap()
    wsc_ug = nc.dram_tensor("wsc_ug", [2 * I], F32, kind="ExternalInput").ap()
    wsc_dp = nc.dram_tensor("wsc_dp", [H], F32, kind="ExternalInput").ap()
    w_up = nc.dram_tensor("w_up", [I, H], F32, kind="ExternalInput").ap()
    w_gate = nc.dram_tensor("w_gate", [I, H], F32, kind="ExternalInput").ap()
    w_down = nc.dram_tensor("w_down", [H, I], F32, kind="ExternalInput").ap()
    out_d = nc.dram_tensor("out", [T, H], F32, kind="ExternalOutput").ap()

    gin = nc.dram_tensor("gin", [4, H * T], BF16)
    gout = nc.dram_tensor("gout", [4 * NCORE, H * T], BF16)
    if dbg:
        gin_o = nc.dram_tensor("gin_o", [4, H * T], BF16, kind="ExternalOutput")
        gout_o = nc.dram_tensor("gout_o", [4 * NCORE, H * T], BF16, kind="ExternalOutput")
    ikind = "ExternalOutput" if dbg else "Internal"
    qsp = nc.dram_tensor("qsp", [2, H, T], BF16, kind=ikind)
    zsp = nc.dram_tensor("zsp", [2, H, T], BF16, kind=ikind)
    usp = nc.dram_tensor("usp", [2, I, T], BF16, kind=ikind)
    ctxsp = nc.dram_tensor("ctxsp", [2, H, T], BF16, kind=ikind)
    wdfm = nc.dram_tensor("wdfm", [HC, 2, IC, 128, 128], BF16)
    wofm = nc.dram_tensor("wofm", [HC, 2, HC, 128, 128], BF16)

    ident_c = nc.inline_tensor(np.eye(128, dtype=np.float32), name="ident_c").ap()
    identb_c = nc.inline_tensor(np.eye(128, dtype=ml_dtypes.bfloat16), name="identb_c").ap()
    h128_c = nc.inline_tensor(_hadamard(128).astype(ml_dtypes.bfloat16), name="h128_c").ap()

    def kfm(sec):
        return gin.ap()[sec].rearrange("(r c) -> r c", r=H)

    def g_k(rank, hl):
        return gout.ap()[4 * rank + hl].rearrange("(r c) -> r c", r=H)

    def g_v(rank, hl):
        return gout.ap()[4 * rank + 2 + hl].rearrange("(r c) -> r c", r=T)

    def fm3(src2d):
        return src2d.rearrange("(c p) t -> p c t", p=128)

    def d3(dst2d, t):
        return dst2d.rearrange("p (c t) -> p c t", t=t)

    ops = {k: [] for k in ("sync", "gpsimd", "vector", "scalar", "tensor")}

    def em(eng, fn):
        ops[eng].append(fn)

    semnames = ["tP", "vP", "sP", "cc", "d_x", "d_wa", "d_wb", "d_k", "d_v",
                "d_q", "d_z", "d_u", "d_ctx", "d_wd", "d_wo", "d_st", "d_out", "d_c"]
    cnt = {k: 0 for k in semnames}
    from contextlib import ExitStack
    stack = ExitStack()
    sems = {k: stack.enter_context(nc.semaphore(k)) for k in semnames}

    def w(eng, sem, val=None):
        v = cnt[sem] if val is None else val
        if v > 0:
            em(eng, lambda e, s=sems[sem], v=v: e.wait_ge(s, v))

    def dma(eng, out, in_, sem):
        cnt[sem] += 16
        em(eng, lambda e, o=out, i=in_, s=sems[sem]: e.dma_start(out=o, in_=i).then_inc(s, 16))

    def dma_nc(eng, out, in_, sem):
        cnt[sem] += 16
        em(eng, lambda e, o=out, i=in_, s=sems[sem]: e.dma_start(
            out=o, in_=i, allow_slow_non_contiguous=True).then_inc(s, 16))

    def step(eng, sem, fn):
        cnt[sem] += 1
        em(eng, lambda e, f=fn, s=sems[sem]: f(e).then_inc(s, 1))

    class Res:

        def __init__(self, n):
            self.m = [dict() for _ in range(n)]

        def wait(self, eng, slot):
            for sem, val in self.m[slot].items():
                w(eng, sem, val)

        def rel(self, slot, *semlist):
            self.m[slot] = {s: cnt[s] for s in semlist}

    arena = nc.alloc_sbuf_tensor("arena", [128, 200 * KB], mybir.dt.uint8).ap()

    def bview(off, nbytes, dt):
        return arena[:, off:off + nbytes].bitcast(dt)

    po = [0]

    def perm(nbytes, dt, align=64):
        o = (po[0] + align - 1) // align * align
        po[0] = o + nbytes
        return bview(o, nbytes, dt)

    ident = perm(512, F32)
    identb = perm(256, BF16)
    h128 = perm(256, BF16)
    h128n = perm(256, BF16)
    hsu_p = perm(HC * 256, BF16)
    hsu_n = perm(HC * 256, BF16)
    bqkv = perm(4 * 64, F32)
    boutc = perm(4 * 64, F32)
    s_up = perm(4 * 64, F32)
    s_gt = perm(4 * 64, F32)
    s_dpc = perm(4 * 64, F32)
    sudp = perm(4 * 64, F32)
    stat = perm(4 * 256, F32)
    assert po[0] <= 12 * KB, po[0]

    A0, B0, C0 = 12 * KB, 76 * KB, 140 * KB

    def A(off, nbytes, dt):
        assert off + nbytes <= 64 * KB
        return bview(A0 + off, nbytes, dt)

    def B(off, nbytes, dt):
        assert off + nbytes <= 64 * KB
        return bview(B0 + off, nbytes, dt)

    def C(off, nbytes, dt):
        assert off + nbytes <= 60 * KB
        return bview(C0 + off, nbytes, dt)

    def st_col(i):
        return stat[:, i:i + 1]

    psum = [stack.enter_context(nc.psum_tensor(f"ps{i}", [128, 512], F32)) for i in range(8)]

    def ps(i):
        return psum[i].ap()

    def psb(i):
        return ps(i).bitcast(BF16)

    banks = Res(8)

    def bwait(eng, *bks):
        for b in bks:
            banks.wait(eng, b)

    def brel(bks, *semlist):
        for b in bks:
            banks.rel(b, *semlist)

    dma("gpsimd", ident[:, :], ident_c, "d_c")
    dma("gpsimd", identb[:, :], identb_c, "d_c")
    dma("gpsimd", h128[:, :], h128_c, "d_c")
    dma_nc("gpsimd", bqkv[:, 0:48], b_qkv.rearrange("(j p) -> p j", p=128), "d_c")
    dma_nc("gpsimd", boutc[:, 0:HC], b_out.rearrange("(j p) -> p j", p=128), "d_c")
    dma_nc("gpsimd", stat[:, 0:HC], su_ug.rearrange("(j p) -> p j", p=128), "d_c")
    dma_nc("gpsimd", sudp[:, 0:IC], su_dp.rearrange("(j p) -> p j", p=128), "d_c")
    dma_nc("gpsimd", s_up[:, 0:IC], wsc_ug[0:I].rearrange("(j p) -> p j", p=128), "d_c")
    dma_nc("gpsimd", s_gt[:, 0:IC], wsc_ug[I:2 * I].rearrange("(j p) -> p j", p=128), "d_c")
    dma_nc("gpsimd", s_dpc[:, 0:HC], wsc_dp.rearrange("(j p) -> p j", p=128), "d_c")
    w("vector", "d_c")
    step("vector", "vP", lambda e: e.tensor_scalar_mul(s_up[:, 0:IC], s_up[:, 0:IC], RS2))
    step("vector", "vP", lambda e: e.tensor_scalar_mul(s_gt[:, 0:IC], s_gt[:, 0:IC], RS2))
    step("vector", "vP", lambda e: e.tensor_scalar_mul(s_dpc[:, 0:HC], s_dpc[:, 0:HC], RS8))
    step("vector", "vP", lambda e: e.tensor_scalar_mul(h128n[:, :], h128[:, :], -1.0))
    for c in range(HC):
        step("vector", "vP", lambda e, c=c: e.tensor_scalar_mul(
            hsu_p[:, c * 128:(c + 1) * 128], h128[:, :], st_col(c)))
    w("vector", "vP")
    for c in range(HC):
        step("vector", "vP", lambda e, c=c: e.tensor_scalar_mul(
            hsu_n[:, c * 128:(c + 1) * 128], hsu_p[:, c * 128:(c + 1) * 128], -1.0))
    consts_v = cnt["vP"]

    xn_hi = A(0, 32 * KB, BF16)
    xn_lo = A(32 * KB, 32 * KB, BF16)
    x_t = C(0, 8 * KB, F32)
    sq_t = C(8 * KB, 8 * KB, F32)
    xnt = [C(16 * KB, 8 * KB, F32), C(24 * KB, 8 * KB, F32)]
    sM = lambda tt: st_col(32 + tt)
    sV = lambda tt: st_col(40 + tt)
    sR = lambda tt: st_col(48 + tt)
    sBi = lambda tt: st_col(56 + tt)
    t1 = st_col(64)
    xnt_res = Res(2)

    for tt in range(NTT):
        if tt > 0:
            w("sync", "vP")
        dma("sync", x_t[:, :], x_in[tt * 128:(tt + 1) * 128, :], "d_x")
        w("vector", "d_x")
        if tt == 0:
            w("vector", "vP", consts_v)
        step("vector", "vP", lambda e: e.tensor_mul(sq_t[:, :], x_t[:, :], x_t[:, :]))
        step("vector", "vP", lambda e, tt=tt: e.reduce_sum(sM(tt), x_t[:, :], axis=AX.X))
        w("vector", "vP")
        step("vector", "vP", lambda e, tt=tt: e.reduce_sum(sV(tt), sq_t[:, :], axis=AX.X))
        w("vector", "vP")
        step("vector", "vP", lambda e, tt=tt: e.tensor_scalar_mul(sM(tt), sM(tt), 1.0 / H))
        step("vector", "vP", lambda e, tt=tt: e.tensor_scalar_mul(sV(tt), sV(tt), 1.0 / H))
        w("vector", "vP")
        step("vector", "vP", lambda e, tt=tt: e.tensor_mul(t1, sM(tt), sM(tt)))
        w("vector", "vP")
        step("vector", "vP", lambda e, tt=tt: e.tensor_sub(sV(tt), sV(tt), t1))
        w("vector", "vP")
        step("vector", "vP", lambda e, tt=tt: e.tensor_scalar_add(sV(tt), sV(tt), 1e-5))
        w("vector", "vP")
        step("vector", "vP", lambda e, tt=tt: e.reciprocal(t1, sV(tt)))
        w("scalar", "vP")
        step("scalar", "sP", lambda e, tt=tt: e.activation(sR(tt), t1, AF.Sqrt))
        w("vector", "sP")
        step("vector", "vP", lambda e, tt=tt: e.tensor_mul(t1, sR(tt), sR(tt)))
        w("vector", "vP")
        step("vector", "vP", lambda e, tt=tt: e.tensor_mul(t1, t1, sV(tt)))
        w("vector", "vP")
        step("vector", "vP", lambda e: e.tensor_scalar(t1, t1, -0.5, 1.5, op0=ALU.mult, op1=ALU.add))
        w("vector", "vP")
        step("vector", "vP", lambda e, tt=tt: e.tensor_mul(sR(tt), sR(tt), t1))
        w("vector", "vP")
        step("vector", "vP", lambda e, tt=tt: e.tensor_mul(sBi(tt), sM(tt), sR(tt)))
        w("vector", "vP")
        step("vector", "vP", lambda e, tt=tt: e.tensor_scalar_mul(sBi(tt), sBi(tt), -1.0))
        w("vector", "vP")
        xs = tt % 2
        xnt_res.wait("vector", xs)
        step("vector", "vP", lambda e, tt=tt, xs=xs: e.tensor_scalar(
            xnt[xs][:, :], x_t[:, :], sR(tt), sBi(tt), op0=ALU.mult, op1=ALU.add))
        w("tensor", "vP")
        if tt == 0:
            w("tensor", "d_c", 16 * 3)
        for c in range(HC):
            bk = 6 + (c % 2)
            bwait("tensor", bk)
            step("tensor", "tP", lambda e, c=c, b=bk, xs=xs: e.transpose(
                ps(b)[:, 0:128], xnt[xs][:, c * 128:(c + 1) * 128], ident[:, :]))
            w("scalar", "tP")
            w("vector", "tP")
            col = c * T + tt * 128
            step("scalar", "sP", lambda e, b=bk, col=col: e.copy(
                xn_hi[:, col:col + 128], ps(b)[:, 0:128]))
            w("vector", "sP")
            step("vector", "vP", lambda e, b=bk, col=col: e.scalar_tensor_tensor(
                xn_lo[:, col:col + 128], ps(b)[:, 0:128], 1.0,
                xn_hi[:, col:col + 128], op0=ALU.mult, op1=ALU.subtract))
            brel([bk], "sP", "vP")
        xnt_res.rel(xs, "tP")
    xn_v, xn_s = cnt["vP"], cnt["sP"]

    if phases < 2:
        return _finish(nc, ops, stack, sems, cnt)

    slab_raw = [B(s * 24 * KB, 8 * KB, F32) for s in range(2)]
    slab_thi = [B(s * 24 * KB + 8 * KB, 4 * KB, BF16) for s in range(2)]
    slab_tlo = [B(s * 24 * KB + 12 * KB, 4 * KB, BF16) for s in range(2)]
    wt_hi = [B(s * 24 * KB + 16 * KB, 4 * KB, BF16) for s in range(2)]
    wt_lo = [B(s * 24 * KB + 20 * KB, 4 * KB, BF16) for s in range(2)]
    stg_hi = [B(48 * KB + 4 * s * KB, 2 * KB, BF16) for s in range(2)]
    stg_lo = [B(50 * KB + 4 * s * KB, 2 * KB, BF16) for s in range(2)]
    vfm = B(56 * KB, 4 * KB, F32)
    vst_hi = B(60 * KB, 2 * KB, BF16)
    vst_lo = B(62 * KB, 2 * KB, BF16)
    slab_res = Res(2)
    stg_res = Res(2)

    def prep_load(slot, src_ap, dsem, scale_col=None):
        raw, thi, tlo = slab_raw[slot], slab_thi[slot], slab_tlo[slot]
        slab_res.wait("sync", slot)
        dma("sync", raw[:, :], src_ap, dsem)
        w("scalar", dsem)
        if scale_col is not None:
            step("scalar", "sP", lambda e, r=raw, sc=scale_col: e.mul(r[:, :], r[:, :], sc))
            w("scalar", "sP")
        step("scalar", "sP", lambda e, r=raw, o=thi: e.copy(o[:, :], r[:, :]))
        w("vector", "sP")
        step("vector", "vP", lambda e, r=raw, h=thi, o=tlo: e.scalar_tensor_tensor(
            o[:, :], r[:, :], 1.0, h[:, :], op0=ALU.mult, op1=ALU.subtract))
        return cnt["sP"], cnt["vP"]

    def prep_tp(slot, lm):
        thi, tlo = slab_thi[slot], slab_tlo[slot]
        whi, wlo = wt_hi[slot], wt_lo[slot]
        w("tensor", "sP", lm[0])
        w("tensor", "vP", lm[1])
        for g in range(4):
            bk = 6 + (g % 2)
            bwait("tensor", bk)
            for k in range(4):
                c = g * 4 + k
                step("tensor", "tP", lambda e, c=c, b=bk, k=k, s=thi: e.transpose(
                    psb(b)[:, k * 256:k * 256 + 128], s[:, c * 128:(c + 1) * 128], identb[:, :]))
                step("tensor", "tP", lambda e, c=c, b=bk, k=k, s=tlo: e.transpose(
                    psb(b)[:, k * 256 + 128:k * 256 + 256], s[:, c * 128:(c + 1) * 128], identb[:, :]))
            w("vector", "tP")
            step("vector", "vP", lambda e, b=bk, g=g, o=whi: e.tensor_copy(
                o.rearrange("p (c q) -> p c q", q=128)[:, g * 4:(g + 1) * 4, :],
                psb(b).rearrange("p (c q) -> p c q", q=256)[:, :, 0:128]))
            step("vector", "vP", lambda e, b=bk, g=g, o=wlo: e.tensor_copy(
                o.rearrange("p (c q) -> p c q", q=128)[:, g * 4:(g + 1) * 4, :],
                psb(b).rearrange("p (c q) -> p c q", q=256)[:, :, 128:256]))
            brel([bk], "vP")
        return cnt["sP"], cnt["vP"]

    def prep_slab(slot, src_ap, dsem, scale_col=None):
        lm = prep_load(slot, src_ap, dsem, scale_col)
        return prep_tp(slot, lm)

    def mm3(bank, a_h, a_l, b_h, b_l, first, last, n=512):
        em("tensor", lambda e, o=bank, a=a_h, b=b_h, st=first, n=n: e.matmul(
            ps(o)[:, 0:n], a, b, start=st, stop=False))
        em("tensor", lambda e, o=bank, a=a_h, b=b_l, n=n: e.matmul(
            ps(o)[:, 0:n], a, b, start=False, stop=False))
        if not last:
            em("tensor", lambda e, o=bank, a=a_l, b=b_h, n=n: e.matmul(
                ps(o)[:, 0:n], a, b, start=False, stop=False))
        else:
            step("tensor", "tP", lambda e, o=bank, a=a_l, b=b_h, n=n: e.matmul(
                ps(o)[:, 0:n], a, b, start=False, stop=True))

    order = [(j, "k") for j in range(HC, 2 * HC)] + \
            [(j, "v") for j in range(2 * HC, 3 * HC)] + \
            [(j, "q") for j in range(0, HC)]
    vst_res = Res(1)
    qkv_lm = {0: prep_load(0, w_qkv[HC * 128:(HC + 1) * 128, :], "d_wa")}
    qkv_pm = {0: prep_tp(0, qkv_lm[0])}
    slab_res.m[0] = {"tP": cnt["tP"]}
    for idx, (j, kind) in enumerate(order):
        slot = idx % 2
        if idx + 1 < len(order):
            j2 = order[idx + 1][0]
            qkv_lm[idx + 1] = prep_load((idx + 1) % 2, w_qkv[j2 * 128:(j2 + 1) * 128, :],
                                        "d_wa" if (idx + 1) % 2 == 0 else "d_wb")
        tiles_s, tiles_v = qkv_pm[idx]
        if idx == 0:
            w("tensor", "vP", xn_v)
            w("tensor", "sP", xn_s)
        w("tensor", "sP", tiles_s)
        w("tensor", "vP", tiles_v)
        bwait("tensor", slot * 2, slot * 2 + 1)
        whi, wlo = wt_hi[slot], wt_lo[slot]
        for c in range(HC):
            for half in range(2):
                tb = half * 512
                mm3(slot * 2 + half,
                    whi[:, c * 128:(c + 1) * 128], wlo[:, c * 128:(c + 1) * 128],
                    xn_hi[:, c * T + tb:c * T + tb + 512], xn_lo[:, c * T + tb:c * T + tb + 512],
                    c == 0, c == HC - 1)
        slab_res.m[slot] = {"tP": cnt["tP"]}
        bias = bqkv[:, j:j + 1]
        w("scalar", "tP")
        w("vector", "tP")
        sslot = idx % 2
        stg_res.wait("scalar", sslot)
        stg_res.wait("vector", sslot)
        for half in range(2):
            bk = slot * 2 + half
            tb = half * 512
            step("scalar", "sP", lambda e, b=bk, tb=tb, bias=bias, ss=sslot: e.add(
                stg_hi[ss][:, tb:tb + 512], ps(b), bias))
            w("vector", "sP")
            step("vector", "vP", lambda e, b=bk, tb=tb, bias=bias, ss=sslot: e.scalar_tensor_tensor(
                stg_lo[ss][:, tb:tb + 512], ps(b), bias, stg_hi[ss][:, tb:tb + 512],
                op0=ALU.add, op1=ALU.subtract))
        brel([slot * 2, slot * 2 + 1], "sP", "vP")
        w("gpsimd", "sP")
        w("gpsimd", "vP")
        if kind == "k":
            r = j - HC
            dma("gpsimd", kfm(0)[r * 128:(r + 1) * 128, :], stg_hi[sslot][:, :], "d_st")
            dma("gpsimd", kfm(1)[r * 128:(r + 1) * 128, :], stg_lo[sslot][:, :], "d_st")
            stg_res.rel(sslot, "d_st")
        elif kind == "q":
            dma("gpsimd", qsp.ap()[0, j * 128:(j + 1) * 128, :], stg_hi[sslot][:, :], "d_st")
            dma("gpsimd", qsp.ap()[1, j * 128:(j + 1) * 128, :], stg_lo[sslot][:, :], "d_st")
            stg_res.rel(sslot, "d_st")
        else:
            r = j - 2 * HC
            w("vector", "vP")
            step("vector", "vP", lambda e, ss=sslot: e.tensor_add(
                vfm[:, :], stg_hi[ss][:, :], stg_lo[ss][:, :]))
            stg_res.rel(sslot, "vP")
            w("tensor", "vP")
            vst_res.wait("scalar", 0)
            vst_res.wait("vector", 0)
            for tt in range(NTT):
                bk = 4 + (tt % 2)
                bwait("tensor", bk)
                step("tensor", "tP", lambda e, tt=tt, b=bk: e.transpose(
                    ps(b)[:, 0:128], vfm[:, tt * 128:(tt + 1) * 128], ident[:, :]))
                w("scalar", "tP")
                w("vector", "tP")
                step("scalar", "sP", lambda e, b=bk, tt=tt: e.copy(
                    vst_hi[:, tt * 128:(tt + 1) * 128], ps(b)[:, 0:128]))
                w("vector", "sP")
                step("vector", "vP", lambda e, b=bk, tt=tt: e.scalar_tensor_tensor(
                    vst_lo[:, tt * 128:(tt + 1) * 128], ps(b)[:, 0:128], 1.0,
                    vst_hi[:, tt * 128:(tt + 1) * 128], op0=ALU.mult, op1=ALU.subtract))
                brel([bk], "sP", "vP")
            w("gpsimd", "sP")
            w("gpsimd", "vP")
            vh = gin.ap()[2].rearrange("(a p Hh) -> p a Hh", p=128, Hh=H)[:, :, r * 128:(r + 1) * 128]
            vl = gin.ap()[3].rearrange("(a p Hh) -> p a Hh", p=128, Hh=H)[:, :, r * 128:(r + 1) * 128]
            dma("gpsimd", vh, vst_hi.rearrange("p (a q) -> p a q", q=128), "d_st")
            dma("gpsimd", vl, vst_lo.rearrange("p (a q) -> p a q", q=128), "d_st")
            vst_res.rel(0, "d_st")
        if idx + 1 < len(order):
            qkv_pm[idx + 1] = prep_tp((idx + 1) % 2, qkv_lm[idx + 1])

    w("gpsimd", "d_st")
    if phases < 3:
        if dbg:
            dma("gpsimd", gin_o.ap(), gin.ap(), "d_st")
        return _finish(nc, ops, stack, sems, cnt)
    step("gpsimd", "cc", lambda e: e.collective_compute(
        "AllGather", ALU.bypass, replica_groups=[list(range(NCORE))],
        ins=[gin.ap()], outs=[gout.ap()]))
    if dbg:
        w("gpsimd", "cc")
        dma("gpsimd", gin_o.ap(), gin.ap(), "d_st")
        dma("gpsimd", gout_o.ap(), gout.ap(), "d_st")

    for cp in range(HC):
        slot = cp % 2
        bwait("tensor", slot * 2, slot * 2 + 1)
        for c in range(HC):
            hs = hsu_p if H16[c, cp] > 0 else hsu_n
            a = hs[:, c * 128:(c + 1) * 128]
            for half in range(2):
                tb = half * 512
                bk = slot * 2 + half
                b_h = xn_hi[:, c * T + tb:c * T + tb + 512]
                b_l = xn_lo[:, c * T + tb:c * T + tb + 512]
                em("tensor", lambda e, o=bk, a=a, b=b_h, st=(c == 0): e.matmul(
                    ps(o), a, b, start=st, stop=False))
                if c < HC - 1:
                    em("tensor", lambda e, o=bk, a=a, b=b_l: e.matmul(
                        ps(o), a, b, start=False, stop=False))
                else:
                    step("tensor", "tP", lambda e, o=bk, a=a, b=b_l: e.matmul(
                        ps(o), a, b, start=False, stop=True))
        w("scalar", "tP")
        w("vector", "tP")
        sslot = cp % 2
        stg_res.wait("scalar", sslot)
        stg_res.wait("vector", sslot)
        for half in range(2):
            bk = slot * 2 + half
            tb = half * 512
            step("scalar", "sP", lambda e, b=bk, tb=tb, ss=sslot: e.copy(
                stg_hi[ss][:, tb:tb + 512], ps(b)))
            w("vector", "sP")
            step("vector", "vP", lambda e, b=bk, tb=tb, ss=sslot: e.scalar_tensor_tensor(
                stg_lo[ss][:, tb:tb + 512], ps(b), 1.0, stg_hi[ss][:, tb:tb + 512],
                op0=ALU.mult, op1=ALU.subtract))
        brel([slot * 2, slot * 2 + 1], "sP", "vP")
        w("gpsimd", "sP")
        w("gpsimd", "vP")
        dma("gpsimd", zsp.ap()[0, cp * 128:(cp + 1) * 128, :], stg_hi[sslot][:, :], "d_st")
        dma("gpsimd", zsp.ap()[1, cp * 128:(cp + 1) * 128, :], stg_lo[sslot][:, :], "d_st")
        stg_res.rel(sslot, "d_st")
    fwht1_t = cnt["tP"]

    if phases < 4:
        return _finish(nc, ops, stack, sems, cnt)

    z_hi = A(0, 32 * KB, BF16)
    z_lo = A(32 * KB, 32 * KB, BF16)
    xup_f = C(32 * KB, 2 * KB, F32)
    xgt_f = C(34 * KB, 2 * KB, F32)
    sg_f = C(36 * KB, 2 * KB, F32)
    u_f = C(38 * KB, 2 * KB, F32)
    w("sync", "d_st")
    w("sync", "tP", fwht1_t)
    dma("sync", d3(z_hi, T), fm3(zsp.ap()[0]), "d_z")
    dma("sync", d3(z_lo, T), fm3(zsp.ap()[1]), "d_z")

    ug_lm = {0: (prep_load(0, w_up[0:128, :], "d_wa"),
                 prep_load(1, w_gate[0:128, :], "d_wb"))}
    ug_pm = {0: (prep_tp(0, ug_lm[0][0]), prep_tp(1, ug_lm[0][1]))}
    slab_res.m[0] = {"tP": cnt["tP"]}
    slab_res.m[1] = {"tP": cnt["tP"]}
    for j in range(IC):
        if j + 1 < IC:
            ug_lm[j + 1] = (prep_load(0, w_up[(j + 1) * 128:(j + 2) * 128, :], "d_wa"),
                            prep_load(1, w_gate[(j + 1) * 128:(j + 2) * 128, :], "d_wb"))
        if j == 0:
            w("tensor", "d_z")
        (t0s, t0v), (t1s, t1v) = ug_pm[j]
        w("tensor", "sP", max(t0s, t1s))
        w("tensor", "vP", max(t0v, t1v))
        for half in range(2):
            tb = half * 512
            bwait("tensor", 0 + half, 2 + half)
            for c in range(HC):
                mm3(0 + half, wt_hi[0][:, c * 128:(c + 1) * 128], wt_lo[0][:, c * 128:(c + 1) * 128],
                    z_hi[:, c * T + tb:c * T + tb + 512], z_lo[:, c * T + tb:c * T + tb + 512],
                    c == 0, c == HC - 1)
            for c in range(HC):
                mm3(2 + half, wt_hi[1][:, c * 128:(c + 1) * 128], wt_lo[1][:, c * 128:(c + 1) * 128],
                    z_hi[:, c * T + tb:c * T + tb + 512], z_lo[:, c * T + tb:c * T + tb + 512],
                    c == 0, c == HC - 1)
            w("vector", "tP")
            step("vector", "vP", lambda e, b=0 + half, j=j: e.tensor_scalar_mul(
                xup_f[:, :], ps(b), s_up[:, j:j + 1]))
            step("vector", "vP", lambda e, b=2 + half, j=j: e.tensor_scalar_mul(
                xgt_f[:, :], ps(b), s_gt[:, j:j + 1]))
            brel([0 + half, 2 + half], "vP")
            w("scalar", "vP")
            step("scalar", "sP", lambda e: e.activation(sg_f[:, :], xgt_f[:, :], AF.Silu))
            w("vector", "sP")
            step("vector", "vP", lambda e, j=j: e.scalar_tensor_tensor(
                u_f[:, :], sg_f[:, :], sudp[:, j:j + 1], xup_f[:, :], op0=ALU.mult, op1=ALU.mult))
            sslot = (2 * j + half) % 2
            stg_res.wait("vector", sslot)
            stg_res.wait("scalar", sslot)
            w("scalar", "vP")
            step("scalar", "sP", lambda e, ss=sslot: e.copy(stg_hi[ss][:, 0:512], u_f[:, :]))
            w("vector", "sP")
            step("vector", "vP", lambda e, ss=sslot: e.scalar_tensor_tensor(
                stg_lo[ss][:, 0:512], u_f[:, :], 1.0, stg_hi[ss][:, 0:512],
                op0=ALU.mult, op1=ALU.subtract))
            w("gpsimd", "sP")
            w("gpsimd", "vP")
            dma("gpsimd", usp.ap()[0, j * 128:(j + 1) * 128, half * 512:(half + 1) * 512],
                stg_hi[sslot][:, 0:512], "d_st")
            dma("gpsimd", usp.ap()[1, j * 128:(j + 1) * 128, half * 512:(half + 1) * 512],
                stg_lo[sslot][:, 0:512], "d_st")
            stg_res.rel(sslot, "d_st")
        if j + 1 < IC:
            ug_pm[j + 1] = (prep_tp(0, ug_lm[j + 1][0]), prep_tp(1, ug_lm[j + 1][1]))
        slab_res.m[0] = {"tP": cnt["tP"]}
        slab_res.m[1] = {"tP": cnt["tP"]}

    wst_res = Res(2)
    units = []
    for j in range(HC):
        for s4 in range(4):
            units.append(("wd", j, s4))
        units.append(("wo", j, 0))

    def wprep_load(u, slot):
        kind, j, s4 = u
        wst_res.wait("sync", slot)
        if kind == "wd":
            return prep_load(slot, w_down[j * 128:(j + 1) * 128, s4 * H:(s4 + 1) * H],
                             "d_wa" if slot == 0 else "d_wb", scale_col=s_dpc[:, j:j + 1])
        return prep_load(slot, w_out[j * 128:(j + 1) * 128, :],
                         "d_wa" if slot == 0 else "d_wb")

    wp_lm = {0: wprep_load(units[0], 0)}
    for ui, u in enumerate(units):
        slot = ui % 2
        if ui + 1 < len(units):
            wp_lm[ui + 1] = wprep_load(units[ui + 1], (ui + 1) % 2)
        tv = prep_tp(slot, wp_lm[ui])
        kind, j, s4 = u
        w("gpsimd", "vP", tv[1])
        if kind == "wd":
            dma("gpsimd", wdfm.ap()[j, 0, s4 * HC:(s4 + 1) * HC].rearrange("c p q -> p c q"),
                d3(wt_hi[slot][:, :], 128), "d_wd")
            dma("gpsimd", wdfm.ap()[j, 1, s4 * HC:(s4 + 1) * HC].rearrange("c p q -> p c q"),
                d3(wt_lo[slot][:, :], 128), "d_wd")
            wst_res.rel(slot, "d_wd")
        else:
            dma("gpsimd", wofm.ap()[j, 0].rearrange("c p q -> p c q"), d3(wt_hi[slot][:, :], 128), "d_wo")
            dma("gpsimd", wofm.ap()[j, 1].rearrange("c p q -> p c q"), d3(wt_lo[slot][:, :], 128), "d_wo")
            wst_res.rel(slot, "d_wo")
        slab_res.m[slot] = {"tP": cnt["tP"], "vP": cnt["vP"]}

    if phases < 5:
        return _finish(nc, ops, stack, sems, cnt)

    ctx_sb = A(0, 64 * KB, F32)
    kvs_hi = [B(0, 16 * KB, BF16), B(32 * KB, 16 * KB, BF16)]
    kvs_lo = [B(16 * KB, 16 * KB, BF16), B(48 * KB, 16 * KB, BF16)]
    q_hi = [C(0, 4 * KB, BF16), C(42 * KB, 4 * KB, BF16)]
    q_lo = [C(4 * KB, 4 * KB, BF16), C(46 * KB, 4 * KB, BF16)]
    sc_sb = [C(8 * KB + qt * 2 * KB, 2 * KB, F32) for qt in range(NQ)]
    pt_hi = C(24 * KB, 8 * KB, BF16)
    pt_lo = C(32 * KB, 8 * KB, BF16)
    pstg_h = [C(40 * KB, 1 * KB, BF16), C(50 * KB, 1 * KB, BF16)]
    pstg_l = [C(41 * KB, 1 * KB, BF16), C(51 * KB, 1 * KB, BF16)]
    aM = lambda qt: st_col(72 + qt)
    aL = lambda qt: st_col(80 + qt)
    aMn = lambda qt: st_col(88 + qt)
    aRs = lambda qt: st_col(96 + qt)
    aBi = lambda qt: st_col(104 + qt)
    aT = st_col(112)
    k_res = Res(1)
    v_res = Res(1)
    q_res = Res(2)
    pstg_res = Res(2)
    pstg_ready = [(0, 0), (0, 0)]

    w("vector", "d_st")
    step("vector", "vP", lambda e: e.memset(stat[:, 72:80], -1e30))
    step("vector", "vP", lambda e: e.memset(stat[:, 80:88], 0.0))
    step("vector", "vP", lambda e: e.memset(ctx_sb[:, :], 0.0))
    attn_init_v = cnt["vP"]

    def pt_view(t, c, qt):
        return t.rearrange("p (c g q) -> p c g q", g=NQ, q=128)[:, c, qt, :]

    def emit_ptrans(qt):
        bk_t = 6 + (qt % 2)
        sl = qt % 2
        bwait("tensor", bk_t)
        w("tensor", "sP", pstg_ready[sl][0])
        w("tensor", "vP", pstg_ready[sl][1])
        for c in range(4):
            step("tensor", "tP", lambda e, c=c, b=bk_t, sl=sl: e.transpose(
                psb(b)[:, c * 256:c * 256 + 128], pstg_h[sl][:, c * 128:(c + 1) * 128], identb[:, :]))
            step("tensor", "tP", lambda e, c=c, b=bk_t, sl=sl: e.transpose(
                psb(b)[:, c * 256 + 128:c * 256 + 256], pstg_l[sl][:, c * 128:(c + 1) * 128], identb[:, :]))
        pstg_res.rel(sl, "tP")
        w("vector", "tP")
        step("vector", "vP", lambda e, b=bk_t, qt=qt: e.tensor_copy(
            pt_hi.rearrange("p (c g q) -> p c g q", g=NQ, q=128)[:, :, qt, :],
            psb(b).rearrange("p (c q) -> p c q", q=256)[:, :, 0:128]))
        step("vector", "vP", lambda e, b=bk_t, qt=qt: e.tensor_copy(
            pt_lo.rearrange("p (c g q) -> p c g q", g=NQ, q=128)[:, :, qt, :],
            psb(b).rearrange("p (c q) -> p c q", q=256)[:, :, 128:256]))
        brel([bk_t], "vP")

    for s in range(NS):
        rank, off = s // 2, (s % 2) * 512
        if s == 0:
            w("sync", "cc", 1)
            w("sync", "vP", attn_init_v)
        k_res.wait("sync", 0)
        dma("sync", d3(kvs_hi[0][:, :], 512), fm3(g_k(rank, 0)[:, off:off + 512]), "d_k")
        dma("sync", d3(kvs_lo[0][:, :], 512), fm3(g_k(rank, 1)[:, off:off + 512]), "d_k")
        v_res.wait("sync", 0)
        dma("sync", d3(kvs_hi[1][:, :], H), fm3(g_v(rank, 0)[off:off + 512, :]), "d_v")
        dma("sync", d3(kvs_lo[1][:, :], H), fm3(g_v(rank, 1)[off:off + 512, :]), "d_v")
        for qt in range(NQ):
            qs = qt % 2
            q_res.wait("sync", qs)
            dma("sync", d3(q_hi[qs][:, :], 128), fm3(qsp.ap()[0, :, qt * 128:(qt + 1) * 128]), "d_q")
            dma("sync", d3(q_lo[qs][:, :], 128), fm3(qsp.ap()[1, :, qt * 128:(qt + 1) * 128]), "d_q")
            bk = 4 + (qt % 2)
            w("tensor", "d_q")
            if qt == 0:
                w("tensor", "d_k")
                if s == 0:
                    w("tensor", "vP", attn_init_v)
            bwait("tensor", bk)
            for c in range(HC):
                mm3(bk, q_hi[qs][:, c * 128:(c + 1) * 128], q_lo[qs][:, c * 128:(c + 1) * 128],
                    kvs_hi[0][:, c * 512:(c + 1) * 512], kvs_lo[0][:, c * 512:(c + 1) * 512],
                    c == 0, c == HC - 1)
            q_res.rel(qs, "tP")
            if qt > 0:
                emit_ptrans(qt - 1)
            w("vector", "tP")
            step("vector", "vP", lambda e, b=bk, qt=qt: e.tensor_copy(sc_sb[qt][:, :], ps(b)))
            brel([bk], "vP")
            w("vector", "vP")
            step("vector", "vP", lambda e, qt=qt: e.reduce_max(aMn(qt), sc_sb[qt][:, :], axis=AX.X))
            w("vector", "vP")
            step("vector", "vP", lambda e, qt=qt: e.tensor_max(aMn(qt), aMn(qt), aM(qt)))
            w("vector", "vP")
            step("vector", "vP", lambda e, qt=qt: e.tensor_sub(aRs(qt), aM(qt), aMn(qt)))
            step("vector", "vP", lambda e, qt=qt: e.tensor_scalar_mul(aBi(qt), aMn(qt), -SCL))
            step("vector", "vP", lambda e, qt=qt: e.tensor_copy(aM(qt), aMn(qt)))
            w("scalar", "vP")
            step("scalar", "sP", lambda e, qt=qt: e.activation(aRs(qt), aRs(qt), AF.Exp, scale=SCL))
            step("scalar", "sP", lambda e, qt=qt: e.activation(
                sc_sb[qt][:, :], sc_sb[qt][:, :], AF.Exp, bias=aBi(qt), scale=SCL))
            w("vector", "sP")
            step("vector", "vP", lambda e, qt=qt: e.reduce_sum(aT, sc_sb[qt][:, :], axis=AX.X))
            w("vector", "vP")
            step("vector", "vP", lambda e, qt=qt: e.scalar_tensor_tensor(
                aL(qt), aL(qt), aRs(qt), aT, op0=ALU.mult, op1=ALU.add))
            pstg_res.wait("scalar", qs)
            step("scalar", "sP", lambda e, qt=qt, sl=qs: e.copy(pstg_h[sl][:, :], sc_sb[qt][:, :]))
            w("vector", "sP")
            step("vector", "vP", lambda e, qt=qt, sl=qs: e.scalar_tensor_tensor(
                pstg_l[sl][:, :], sc_sb[qt][:, :], 1.0, pstg_h[sl][:, :],
                op0=ALU.mult, op1=ALU.subtract))
            pstg_ready[qs] = (cnt["sP"], cnt["vP"])
        k_res.rel(0, "tP")
        emit_ptrans(NQ - 1)
        w("tensor", "d_v")
        for qt in range(NQ):
            grp = [0, 1, 2, 3] if qt % 2 == 0 else [4, 5, 6, 7]
            bwait("tensor", *grp)
            for c in range(4):
                a_h = pt_view(pt_hi, c, qt)
                a_l = pt_view(pt_lo, c, qt)
                for hb in range(4):
                    b_h = kvs_hi[1][:, c * 2048 + hb * 512:c * 2048 + (hb + 1) * 512]
                    b_l = kvs_lo[1][:, c * 2048 + hb * 512:c * 2048 + (hb + 1) * 512]
                    mm3(grp[hb], a_h, a_l, b_h, b_l, c == 0, c == 3)
            w("vector", "tP")
            for hb in range(4):
                step("vector", "vP", lambda e, qt=qt, b=grp[hb], hb=hb: e.scalar_tensor_tensor(
                    ctx_sb[:, qt * H + hb * 512:qt * H + (hb + 1) * 512],
                    ctx_sb[:, qt * H + hb * 512:qt * H + (hb + 1) * 512],
                    aRs(qt), ps(b), op0=ALU.mult, op1=ALU.add))
            brel(grp, "vP")
        v_res.rel(0, "tP")

    cth = B(0, 4 * KB, BF16)
    ctl = B(4 * KB, 4 * KB, BF16)
    cfs_h = [B(8 * KB, 4 * KB, BF16), B(16 * KB, 4 * KB, BF16)]
    cfs_l = [B(12 * KB, 4 * KB, BF16), B(20 * KB, 4 * KB, BF16)]
    cf_res = Res(2)
    for qt in range(NQ):
        w("vector", "tP")
        step("vector", "vP", lambda e, qt=qt: e.reciprocal(aT, aL(qt)))
        w("scalar", "vP")
        step("scalar", "sP", lambda e, qt=qt: e.mul(cth[:, :], ctx_sb[:, qt * H:(qt + 1) * H], aT))
        w("vector", "sP")
        step("vector", "vP", lambda e, qt=qt: e.scalar_tensor_tensor(
            ctl[:, :], ctx_sb[:, qt * H:(qt + 1) * H], aT, cth[:, :],
            op0=ALU.mult, op1=ALU.subtract))
        w("tensor", "sP")
        w("tensor", "vP")
        qs = qt % 2
        cf_res.wait("vector", qs)
        for g in range(4):
            bk = 6 + (g % 2)
            bwait("tensor", bk)
            for k in range(4):
                c = g * 4 + k
                step("tensor", "tP", lambda e, c=c, b=bk, k=k: e.transpose(
                    psb(b)[:, k * 256:k * 256 + 128], cth[:, c * 128:(c + 1) * 128], identb[:, :]))
                step("tensor", "tP", lambda e, c=c, b=bk, k=k: e.transpose(
                    psb(b)[:, k * 256 + 128:k * 256 + 256], ctl[:, c * 128:(c + 1) * 128], identb[:, :]))
            w("vector", "tP")
            step("vector", "vP", lambda e, b=bk, g=g, qs=qs: e.tensor_copy(
                cfs_h[qs].rearrange("p (c q) -> p c q", q=128)[:, g * 4:(g + 1) * 4, :],
                psb(b).rearrange("p (c q) -> p c q", q=256)[:, :, 0:128]))
            step("vector", "vP", lambda e, b=bk, g=g, qs=qs: e.tensor_copy(
                cfs_l[qs].rearrange("p (c q) -> p c q", q=128)[:, g * 4:(g + 1) * 4, :],
                psb(b).rearrange("p (c q) -> p c q", q=256)[:, :, 128:256]))
            brel([bk], "vP")
        w("gpsimd", "vP")
        dma("gpsimd", fm3(ctxsp.ap()[0, :, qt * 128:(qt + 1) * 128]), d3(cfs_h[qs][:, :], 128), "d_ctx")
        dma("gpsimd", fm3(ctxsp.ap()[1, :, qt * 128:(qt + 1) * 128]), d3(cfs_l[qs][:, :], 128), "d_ctx")
        cf_res.rel(qs, "d_ctx")

    if phases < 6:
        return _finish(nc, ops, stack, sems, cnt)

    u_hi = A(0, 32 * KB, BF16)
    u_lo = A(32 * KB, 32 * KB, BF16)
    w1f = B(0, 64 * KB, F32)
    w1tmp = C(0, 32 * KB, F32)
    ctxq_h = C(0, 8 * KB, BF16)
    ctxq_l = C(8 * KB, 8 * KB, BF16)
    wo_h = [C(16 * KB + 8 * k * KB, 4 * KB, BF16) for k in range(2)]
    wo_l = [C(20 * KB + 8 * k * KB, 4 * KB, BF16) for k in range(2)]
    comb = C(32 * KB, 16 * KB, F32)
    resid = C(48 * KB, 8 * KB, F32)
    wd_h = [B(0, 16 * KB, BF16), B(32 * KB, 16 * KB, BF16)]
    wd_l = [B(16 * KB, 16 * KB, BF16), B(48 * KB, 16 * KB, BF16)]
    ostg = B(0, 8 * KB, F32)
    wd_res = Res(2)
    ostg_res = Res(1)

    for q in range(4):
        cb = q * QT4
        if q == 0:
            w("sync", "d_ctx")
            w("sync", "tP")
            w("sync", "vP")
        else:
            w("sync", "tP")
            w("sync", "vP")
            w("sync", "d_out")
        dma("sync", d3(u_hi[:, :], QT4), fm3(usp.ap()[0])[:, :, cb:cb + QT4], "d_u")
        dma("sync", d3(u_lo[:, :], QT4), fm3(usp.ap()[1])[:, :, cb:cb + QT4], "d_u")
        w("tensor", "d_u")
        for cp in range(IC):
            bk = cp % 4
            c0, c1 = (cp % 32), (cp % 32) + 32
            hsgn = h128 if cp < 32 else h128n
            bwait("tensor", bk)
            em("tensor", lambda e, b=bk, c=c0: e.matmul(
                ps(b)[:, 0:QT4], h128[:, :], u_hi[:, c * QT4:(c + 1) * QT4], start=True, stop=False))
            em("tensor", lambda e, b=bk, c=c0: e.matmul(
                ps(b)[:, 0:QT4], h128[:, :], u_lo[:, c * QT4:(c + 1) * QT4], start=False, stop=False))
            em("tensor", lambda e, b=bk, a=hsgn, c=c1: e.matmul(
                ps(b)[:, 0:QT4], a[:, :], u_hi[:, c * QT4:(c + 1) * QT4], start=False, stop=False))
            step("tensor", "tP", lambda e, b=bk, a=hsgn, c=c1: e.matmul(
                ps(b)[:, 0:QT4], a[:, :], u_lo[:, c * QT4:(c + 1) * QT4], start=False, stop=True))
            w("vector", "tP")
            step("vector", "vP", lambda e, b=bk, cp=cp: e.tensor_copy(
                w1f[:, cp * QT4:(cp + 1) * QT4], ps(b)[:, 0:QT4]))
            brel([bk], "vP")
        for bit in (16, 8, 4, 2, 1):
            na = IC // (2 * bit)
            va = w1f.rearrange("p (a d t) -> p a d t", a=na, d=2 * bit)[:, :, 0:bit, :]
            vb = w1f.rearrange("p (a d t) -> p a d t", a=na, d=2 * bit)[:, :, bit:2 * bit, :]
            vt = w1tmp.rearrange("p (a d t) -> p a d t", a=na, d=bit)
            w("vector", "vP")
            step("vector", "vP", lambda e, a=va, b=vb, t=vt: e.tensor_sub(t, a, b))
            w("vector", "vP")
            step("vector", "vP", lambda e, a=va, b=vb: e.tensor_add(a, a, b))
            w("vector", "vP")
            step("vector", "vP", lambda e, b=vb, t=vt: e.tensor_copy(b, t))
        w("vector", "vP")
        w("scalar", "vP")
        step("scalar", "sP", lambda e: e.copy(u_hi[:, :], w1f[:, :]))
        w("vector", "sP")
        step("vector", "vP", lambda e: e.scalar_tensor_tensor(
            u_lo[:, :], w1f[:, :], 1.0, u_hi[:, :], op0=ALU.mult, op1=ALU.subtract))
        w("sync", "vP")
        dma("sync", d3(ctxq_h[:, :], QT4), fm3(ctxsp.ap()[0])[:, :, cb:cb + QT4], "d_ctx")
        dma("sync", d3(ctxq_l[:, :], QT4), fm3(ctxsp.ap()[1])[:, :, cb:cb + QT4], "d_ctx")
        for j in range(HC):
            slot = j % 2
            wd_res.wait("sync", slot)
            dma("sync", d3(wd_h[slot][:, :], 128), wdfm.ap()[j, 0].rearrange("c p q -> p c q"), "d_wd")
            dma("sync", d3(wd_l[slot][:, :], 128), wdfm.ap()[j, 1].rearrange("c p q -> p c q"), "d_wd")
            dma("sync", d3(wo_h[slot][:, :], 128), wofm.ap()[j, 0].rearrange("c p q -> p c q"), "d_wo")
            dma("sync", d3(wo_l[slot][:, :], 128), wofm.ap()[j, 1].rearrange("c p q -> p c q"), "d_wo")
            bk = 4 + (j % 2)
            w("tensor", "d_wd")
            w("tensor", "d_wo")
            if j == 0:
                w("tensor", "sP")
                w("tensor", "vP")
                w("tensor", "d_ctx")
            bwait("tensor", bk)
            for c in range(IC):
                mm3(bk, wd_h[slot][:, c * 128:(c + 1) * 128], wd_l[slot][:, c * 128:(c + 1) * 128],
                    u_hi[:, c * QT4:(c + 1) * QT4], u_lo[:, c * QT4:(c + 1) * QT4],
                    c == 0, False, n=QT4)
            for c in range(HC):
                mm3(bk, wo_h[slot][:, c * 128:(c + 1) * 128], wo_l[slot][:, c * 128:(c + 1) * 128],
                    ctxq_h[:, c * QT4:(c + 1) * QT4], ctxq_l[:, c * QT4:(c + 1) * QT4],
                    False, c == HC - 1, n=QT4)
            wd_res.rel(slot, "tP")
            w("scalar", "tP")
            step("scalar", "sP", lambda e, b=bk, j=j: e.add(
                comb[:, j * QT4:(j + 1) * QT4], ps(b)[:, 0:QT4], boutc[:, j:j + 1]))
            brel([bk], "sP")
        for tt in range(2):
            trow = cb + tt * 128
            w("sync", "d_out")
            dma("sync", resid[:, :], x_in[trow:trow + 128, :], "d_x")
            w("tensor", "sP")
            w("tensor", "d_out")
            for g in range(4):
                bk = 2 + (g % 2)
                bwait("tensor", bk)
                for k in range(4):
                    j = g * 4 + k
                    step("tensor", "tP", lambda e, j=j, b=bk, k=k, tt=tt: e.transpose(
                        ps(b)[:, k * 128:(k + 1) * 128],
                        comb[:, j * QT4 + tt * 128:j * QT4 + tt * 128 + 128], ident[:, :]))
                w("vector", "tP")
                w("vector", "d_x")
                if g == 0:
                    ostg_res.wait("vector", 0)
                step("vector", "vP", lambda e, b=bk, g=g: e.tensor_add(
                    ostg[:, g * 512:(g + 1) * 512], ps(b), resid[:, g * 512:(g + 1) * 512]))
                brel([bk], "vP")
            w("gpsimd", "vP")
            dma("gpsimd", out_d[trow:trow + 128, :], ostg[:, :], "d_out")
            ostg_res.rel(0, "d_out")

    return _finish(nc, ops, stack, sems, cnt)


def _finish(nc, ops, stack, sems, cnt):
    if cnt["d_out"]:
        ops["gpsimd"].append(lambda e: e.wait_ge(sems["d_out"], cnt["d_out"]))
    if cnt["d_st"]:
        ops["gpsimd"].append(lambda e: e.wait_ge(sems["d_st"], cnt["d_st"]))
    if cnt["d_ctx"]:
        ops["gpsimd"].append(lambda e: e.wait_ge(sems["d_ctx"], cnt["d_ctx"]))
    with nc.Block() as block:
        @block.sync
        def _(e):
            for f in ops["sync"]:
                f(e)

        @block.gpsimd
        def _(e):
            for f in ops["gpsimd"]:
                f(e)

        @block.vector
        def _(e):
            for f in ops["vector"]:
                f(e)

        @block.scalar
        def _(e):
            for f in ops["scalar"]:
                f(e)

        @block.tensor
        def _(e):
            for f in ops["tensor"]:
                f(e)
    stack.close()
    return nc


def kernel(L0_attn_hidden_states, L0_attn_w_qkv, L0_attn_b_qkv, L0_attn_w_out,
           L0_attn_b_out, L0_ffn_SU_ug, L0_ffn_SU_dp, L0_ffn_Wscale_ug,
           L0_ffn_Wscale_dp, L0_ffn_W_up, L0_ffn_W_gate, L0_ffn_W_down):
    x = np.ascontiguousarray(np.asarray(L0_attn_hidden_states, dtype=np.float32).reshape(NCORE * T, H))
    full = {
        "w_qkv": np.ascontiguousarray(np.asarray(L0_attn_w_qkv, dtype=np.float32)),
        "b_qkv": np.ascontiguousarray(np.asarray(L0_attn_b_qkv, dtype=np.float32)),
        "w_out": np.ascontiguousarray(np.asarray(L0_attn_w_out, dtype=np.float32)),
        "b_out": np.ascontiguousarray(np.asarray(L0_attn_b_out, dtype=np.float32)),
        "su_ug": np.ascontiguousarray(np.asarray(L0_ffn_SU_ug, dtype=np.float32)),
        "su_dp": np.ascontiguousarray(np.asarray(L0_ffn_SU_dp, dtype=np.float32)),
        "wsc_ug": np.ascontiguousarray(np.asarray(L0_ffn_Wscale_ug, dtype=np.float32)),
        "wsc_dp": np.ascontiguousarray(np.asarray(L0_ffn_Wscale_dp, dtype=np.float32)),
        "w_up": np.ascontiguousarray(np.asarray(L0_ffn_W_up, dtype=np.float32)),
        "w_gate": np.ascontiguousarray(np.asarray(L0_ffn_W_gate, dtype=np.float32)),
        "w_down": np.ascontiguousarray(np.asarray(L0_ffn_W_down, dtype=np.float32)),
    }
    nc = build_program()
    in_maps = [dict(full, x=x[c * T:(c + 1) * T]) for c in range(NCORE)]
    res = run_bass_kernel_spmd(nc, in_maps, core_ids=list(range(NCORE)))
    out = np.concatenate([res.results[c]["out"] for c in range(NCORE)], axis=0)
    return out.reshape(1, NCORE * T, H).astype(np.float32)


# version 08
# speedup vs baseline: 1.4474x; 1.0558x over previous
import numpy as np
import ml_dtypes
import concourse.bass as bass
import concourse.mybir as mybir
from concourse.bass_utils import run_bass_kernel_spmd

F32 = mybir.dt.float32
BF16 = mybir.dt.bfloat16
AF = mybir.ActivationFunctionType
ALU = mybir.AluOpType
AX = mybir.AxisListType

NCORE = 8
T = 1024
NTT = T // 128
H = 2048
HC = H // 128
I = 8192
IC = I // 128
SB = 512
NS = (NCORE * T) // SB
NQ = NTT
QT4 = T // 4
RS2 = float(1.0 / np.sqrt(2048.0))
RS8 = float(1.0 / np.sqrt(8192.0))
SCL = float(1.0 / np.sqrt(2048.0))
KB = 1024


def _hadamard(n):
    h = np.array([[1.0]], dtype=np.float64)
    while h.shape[0] < n:
        h = np.block([[h, h], [h, -h]])
    return h


H16 = _hadamard(16)


def build_program(dbg=False, phases=99):
    nc = bass.Bass("TRN2", target_bir_lowering=False, detect_race_conditions=False)

    x_in = nc.dram_tensor("x", [T, H], F32, kind="ExternalInput").ap()
    w_qkv = nc.dram_tensor("w_qkv", [3 * H, H], F32, kind="ExternalInput").ap()
    b_qkv = nc.dram_tensor("b_qkv", [3 * H], F32, kind="ExternalInput").ap()
    w_out = nc.dram_tensor("w_out", [H, H], F32, kind="ExternalInput").ap()
    b_out = nc.dram_tensor("b_out", [H], F32, kind="ExternalInput").ap()
    su_ug = nc.dram_tensor("su_ug", [H], F32, kind="ExternalInput").ap()
    su_dp = nc.dram_tensor("su_dp", [I], F32, kind="ExternalInput").ap()
    wsc_ug = nc.dram_tensor("wsc_ug", [2 * I], F32, kind="ExternalInput").ap()
    wsc_dp = nc.dram_tensor("wsc_dp", [H], F32, kind="ExternalInput").ap()
    w_up = nc.dram_tensor("w_up", [I, H], F32, kind="ExternalInput").ap()
    w_gate = nc.dram_tensor("w_gate", [I, H], F32, kind="ExternalInput").ap()
    w_down = nc.dram_tensor("w_down", [H, I], F32, kind="ExternalInput").ap()
    out_d = nc.dram_tensor("out", [T, H], F32, kind="ExternalOutput").ap()

    gin = nc.dram_tensor("gin", [4, H * T], BF16)
    gout = nc.dram_tensor("gout", [4 * NCORE, H * T], BF16)
    if dbg:
        gin_o = nc.dram_tensor("gin_o", [4, H * T], BF16, kind="ExternalOutput")
        gout_o = nc.dram_tensor("gout_o", [4 * NCORE, H * T], BF16, kind="ExternalOutput")
    ikind = "ExternalOutput" if dbg else "Internal"
    qsp = nc.dram_tensor("qsp", [2, H, T], BF16, kind=ikind)
    zsp = nc.dram_tensor("zsp", [2, H, T], BF16, kind=ikind)
    usp = nc.dram_tensor("usp", [2, I, T], BF16, kind=ikind)
    ctxsp = nc.dram_tensor("ctxsp", [2, H, T], BF16, kind=ikind)
    wdfm = nc.dram_tensor("wdfm", [HC, 2, IC, 128, 128], BF16)
    wofm = nc.dram_tensor("wofm", [HC, 2, HC, 128, 128], BF16)

    ident_c = nc.inline_tensor(np.eye(128, dtype=np.float32), name="ident_c").ap()
    identb_c = nc.inline_tensor(np.eye(128, dtype=ml_dtypes.bfloat16), name="identb_c").ap()
    h128_c = nc.inline_tensor(_hadamard(128).astype(ml_dtypes.bfloat16), name="h128_c").ap()

    def kfm(sec):
        return gin.ap()[sec].rearrange("(r c) -> r c", r=H)

    def g_k(rank, hl):
        return gout.ap()[4 * rank + hl].rearrange("(r c) -> r c", r=H)

    def g_v(rank, hl):
        return gout.ap()[4 * rank + 2 + hl].rearrange("(r c) -> r c", r=T)

    def fm3(src2d):
        return src2d.rearrange("(c p) t -> p c t", p=128)

    def d3(dst2d, t):
        return dst2d.rearrange("p (c t) -> p c t", t=t)

    ops = {k: [] for k in ("sync", "gpsimd", "vector", "scalar", "tensor")}

    def em(eng, fn):
        ops[eng].append(fn)

    semnames = ["tP", "vP", "sP", "cc", "d_x", "d_wa", "d_wb", "d_k", "d_v",
                "d_q", "d_z", "d_u", "d_ctx", "d_wd", "d_wo", "d_st", "d_out", "d_c"]
    cnt = {k: 0 for k in semnames}
    from contextlib import ExitStack
    stack = ExitStack()
    sems = {k: stack.enter_context(nc.semaphore(k)) for k in semnames}

    def w(eng, sem, val=None):
        v = cnt[sem] if val is None else val
        if v > 0:
            em(eng, lambda e, s=sems[sem], v=v: e.wait_ge(s, v))

    def dma(eng, out, in_, sem):
        cnt[sem] += 16
        em(eng, lambda e, o=out, i=in_, s=sems[sem]: e.dma_start(out=o, in_=i).then_inc(s, 16))

    def dma_nc(eng, out, in_, sem):
        cnt[sem] += 16
        em(eng, lambda e, o=out, i=in_, s=sems[sem]: e.dma_start(
            out=o, in_=i, allow_slow_non_contiguous=True).then_inc(s, 16))

    def step(eng, sem, fn):
        cnt[sem] += 1
        em(eng, lambda e, f=fn, s=sems[sem]: f(e).then_inc(s, 1))

    class Res:

        def __init__(self, n):
            self.m = [dict() for _ in range(n)]

        def wait(self, eng, slot):
            for sem, val in self.m[slot].items():
                w(eng, sem, val)

        def rel(self, slot, *semlist):
            self.m[slot] = {s: cnt[s] for s in semlist}

    arena = nc.alloc_sbuf_tensor("arena", [128, 200 * KB], mybir.dt.uint8).ap()

    def bview(off, nbytes, dt):
        return arena[:, off:off + nbytes].bitcast(dt)

    po = [0]

    def perm(nbytes, dt, align=64):
        o = (po[0] + align - 1) // align * align
        po[0] = o + nbytes
        return bview(o, nbytes, dt)

    ident = perm(512, F32)
    identb = perm(256, BF16)
    h128 = perm(256, BF16)
    h128n = perm(256, BF16)
    hsu_p = perm(HC * 256, BF16)
    hsu_n = perm(HC * 256, BF16)
    bqkv = perm(4 * 64, F32)
    boutc = perm(4 * 64, F32)
    s_up = perm(4 * 64, F32)
    s_gt = perm(4 * 64, F32)
    s_dpc = perm(4 * 64, F32)
    sudp = perm(4 * 64, F32)
    stat = perm(4 * 256, F32)
    assert po[0] <= 12 * KB, po[0]

    A0, B0, C0 = 12 * KB, 76 * KB, 140 * KB

    def A(off, nbytes, dt):
        assert off + nbytes <= 64 * KB
        return bview(A0 + off, nbytes, dt)

    def B(off, nbytes, dt):
        assert off + nbytes <= 64 * KB
        return bview(B0 + off, nbytes, dt)

    def C(off, nbytes, dt):
        assert off + nbytes <= 60 * KB
        return bview(C0 + off, nbytes, dt)

    def st_col(i):
        return stat[:, i:i + 1]

    psum = [stack.enter_context(nc.psum_tensor(f"ps{i}", [128, 512], F32)) for i in range(8)]

    def ps(i):
        return psum[i].ap()

    def psb(i):
        return ps(i).bitcast(BF16)

    banks = Res(8)

    def bwait(eng, *bks):
        for b in bks:
            banks.wait(eng, b)

    def brel(bks, *semlist):
        for b in bks:
            banks.rel(b, *semlist)

    dma("gpsimd", ident[:, :], ident_c, "d_c")
    dma("gpsimd", identb[:, :], identb_c, "d_c")
    dma("gpsimd", h128[:, :], h128_c, "d_c")
    dma_nc("gpsimd", bqkv[:, 0:48], b_qkv.rearrange("(j p) -> p j", p=128), "d_c")
    dma_nc("gpsimd", boutc[:, 0:HC], b_out.rearrange("(j p) -> p j", p=128), "d_c")
    dma_nc("gpsimd", stat[:, 0:HC], su_ug.rearrange("(j p) -> p j", p=128), "d_c")
    dma_nc("gpsimd", sudp[:, 0:IC], su_dp.rearrange("(j p) -> p j", p=128), "d_c")
    dma_nc("gpsimd", s_up[:, 0:IC], wsc_ug[0:I].rearrange("(j p) -> p j", p=128), "d_c")
    dma_nc("gpsimd", s_gt[:, 0:IC], wsc_ug[I:2 * I].rearrange("(j p) -> p j", p=128), "d_c")
    dma_nc("gpsimd", s_dpc[:, 0:HC], wsc_dp.rearrange("(j p) -> p j", p=128), "d_c")
    w("vector", "d_c")
    step("vector", "vP", lambda e: e.tensor_scalar_mul(s_up[:, 0:IC], s_up[:, 0:IC], RS2))
    step("vector", "vP", lambda e: e.tensor_scalar_mul(s_gt[:, 0:IC], s_gt[:, 0:IC], RS2))
    step("vector", "vP", lambda e: e.tensor_scalar_mul(s_dpc[:, 0:HC], s_dpc[:, 0:HC], RS8))
    step("vector", "vP", lambda e: e.tensor_scalar_mul(h128n[:, :], h128[:, :], -1.0))
    for c in range(HC):
        step("vector", "vP", lambda e, c=c: e.tensor_scalar_mul(
            hsu_p[:, c * 128:(c + 1) * 128], h128[:, :], st_col(c)))
    w("vector", "vP")
    for c in range(HC):
        step("vector", "vP", lambda e, c=c: e.tensor_scalar_mul(
            hsu_n[:, c * 128:(c + 1) * 128], hsu_p[:, c * 128:(c + 1) * 128], -1.0))
    consts_v = cnt["vP"]

    xn_hi = A(0, 32 * KB, BF16)
    xn_lo = A(32 * KB, 32 * KB, BF16)
    x_t = C(0, 8 * KB, F32)
    sq_t = C(8 * KB, 8 * KB, F32)
    xnt = [C(16 * KB, 8 * KB, F32), C(24 * KB, 8 * KB, F32)]
    sM = lambda tt: st_col(32 + tt)
    sV = lambda tt: st_col(40 + tt)
    sR = lambda tt: st_col(48 + tt)
    sBi = lambda tt: st_col(56 + tt)
    t1 = st_col(64)
    xnt_res = Res(2)

    for tt in range(NTT):
        if tt > 0:
            w("sync", "vP")
        dma("sync", x_t[:, :], x_in[tt * 128:(tt + 1) * 128, :], "d_x")
        w("vector", "d_x")
        if tt == 0:
            w("vector", "vP", consts_v)
        step("vector", "vP", lambda e: e.tensor_mul(sq_t[:, :], x_t[:, :], x_t[:, :]))
        step("vector", "vP", lambda e, tt=tt: e.reduce_sum(sM(tt), x_t[:, :], axis=AX.X))
        w("vector", "vP")
        step("vector", "vP", lambda e, tt=tt: e.reduce_sum(sV(tt), sq_t[:, :], axis=AX.X))
        w("vector", "vP")
        step("vector", "vP", lambda e, tt=tt: e.tensor_scalar_mul(sM(tt), sM(tt), 1.0 / H))
        step("vector", "vP", lambda e, tt=tt: e.tensor_scalar_mul(sV(tt), sV(tt), 1.0 / H))
        w("vector", "vP")
        step("vector", "vP", lambda e, tt=tt: e.tensor_mul(t1, sM(tt), sM(tt)))
        w("vector", "vP")
        step("vector", "vP", lambda e, tt=tt: e.tensor_sub(sV(tt), sV(tt), t1))
        w("vector", "vP")
        step("vector", "vP", lambda e, tt=tt: e.tensor_scalar_add(sV(tt), sV(tt), 1e-5))
        w("vector", "vP")
        step("vector", "vP", lambda e, tt=tt: e.reciprocal(t1, sV(tt)))
        w("scalar", "vP")
        step("scalar", "sP", lambda e, tt=tt: e.activation(sR(tt), t1, AF.Sqrt))
        w("vector", "sP")
        step("vector", "vP", lambda e, tt=tt: e.tensor_mul(t1, sR(tt), sR(tt)))
        w("vector", "vP")
        step("vector", "vP", lambda e, tt=tt: e.tensor_mul(t1, t1, sV(tt)))
        w("vector", "vP")
        step("vector", "vP", lambda e: e.tensor_scalar(t1, t1, -0.5, 1.5, op0=ALU.mult, op1=ALU.add))
        w("vector", "vP")
        step("vector", "vP", lambda e, tt=tt: e.tensor_mul(sR(tt), sR(tt), t1))
        w("vector", "vP")
        step("vector", "vP", lambda e, tt=tt: e.tensor_mul(sBi(tt), sM(tt), sR(tt)))
        w("vector", "vP")
        step("vector", "vP", lambda e, tt=tt: e.tensor_scalar_mul(sBi(tt), sBi(tt), -1.0))
        w("vector", "vP")
        xs = tt % 2
        xnt_res.wait("vector", xs)
        step("vector", "vP", lambda e, tt=tt, xs=xs: e.tensor_scalar(
            xnt[xs][:, :], x_t[:, :], sR(tt), sBi(tt), op0=ALU.mult, op1=ALU.add))
        w("tensor", "vP")
        if tt == 0:
            w("tensor", "d_c", 16 * 3)
        for c in range(HC):
            bk = 6 + (c % 2)
            bwait("tensor", bk)
            step("tensor", "tP", lambda e, c=c, b=bk, xs=xs: e.transpose(
                ps(b)[:, 0:128], xnt[xs][:, c * 128:(c + 1) * 128], ident[:, :]))
            w("scalar", "tP")
            w("vector", "tP")
            col = c * T + tt * 128
            step("scalar", "sP", lambda e, b=bk, col=col: e.copy(
                xn_hi[:, col:col + 128], ps(b)[:, 0:128]))
            w("vector", "sP")
            step("vector", "vP", lambda e, b=bk, col=col: e.scalar_tensor_tensor(
                xn_lo[:, col:col + 128], ps(b)[:, 0:128], 1.0,
                xn_hi[:, col:col + 128], op0=ALU.mult, op1=ALU.subtract))
            brel([bk], "sP", "vP")
        xnt_res.rel(xs, "tP")
    xn_v, xn_s = cnt["vP"], cnt["sP"]

    if phases < 2:
        return _finish(nc, ops, stack, sems, cnt)

    slab_raw = [B(s * 24 * KB, 8 * KB, F32) for s in range(2)]
    slab_thi = [B(s * 24 * KB + 8 * KB, 4 * KB, BF16) for s in range(2)]
    slab_tlo = [B(s * 24 * KB + 12 * KB, 4 * KB, BF16) for s in range(2)]
    wtil = [B(s * 24 * KB + 16 * KB, 8 * KB, BF16) for s in range(2)]

    def wt_hi_t(slot, c):
        return wtil[slot][:, c * 256:c * 256 + 128]

    def wt_lo_t(slot, c):
        return wtil[slot][:, c * 256 + 128:(c + 1) * 256]
    stg_hi = [B(48 * KB + 4 * s * KB, 2 * KB, BF16) for s in range(2)]
    stg_lo = [B(50 * KB + 4 * s * KB, 2 * KB, BF16) for s in range(2)]
    vfm = B(56 * KB, 4 * KB, F32)
    vst_hi = B(60 * KB, 2 * KB, BF16)
    vst_lo = B(62 * KB, 2 * KB, BF16)
    slab_res = Res(2)
    stg_res = Res(2)

    def prep_load(slot, src_ap, dsem, scale_col=None):
        raw, thi, tlo = slab_raw[slot], slab_thi[slot], slab_tlo[slot]
        slab_res.wait("sync", slot)
        dma("sync", raw[:, :], src_ap, dsem)
        w("scalar", dsem)
        if scale_col is not None:
            step("scalar", "sP", lambda e, r=raw, sc=scale_col: e.mul(r[:, :], r[:, :], sc))
            w("scalar", "sP")
        step("scalar", "sP", lambda e, r=raw, o=thi: e.copy(o[:, :], r[:, :]))
        w("vector", "sP")
        step("vector", "vP", lambda e, r=raw, h=thi, o=tlo: e.scalar_tensor_tensor(
            o[:, :], r[:, :], 1.0, h[:, :], op0=ALU.mult, op1=ALU.subtract))
        return cnt["sP"], cnt["vP"]

    def prep_tp(slot, lm, pbanks=(6, 7)):
        thi, tlo = slab_thi[slot], slab_tlo[slot]
        o = wtil[slot]
        w("tensor", "sP", lm[0])
        w("tensor", "vP", lm[1])
        for g in range(4):
            bk = pbanks[g % len(pbanks)]
            bwait("tensor", bk)
            for k in range(4):
                c = g * 4 + k
                step("tensor", "tP", lambda e, c=c, b=bk, k=k, s=thi: e.transpose(
                    psb(b)[:, k * 256:k * 256 + 128], s[:, c * 128:(c + 1) * 128], identb[:, :]))
                step("tensor", "tP", lambda e, c=c, b=bk, k=k, s=tlo: e.transpose(
                    psb(b)[:, k * 256 + 128:k * 256 + 256], s[:, c * 128:(c + 1) * 128], identb[:, :]))
            w("vector", "tP")
            step("vector", "vP", lambda e, b=bk, g=g, o=o: e.tensor_copy(
                o[:, g * 1024:(g + 1) * 1024], psb(b)[:, :]))
            brel([bk], "vP")
        return cnt["sP"], cnt["vP"]

    def prep_slab(slot, src_ap, dsem, scale_col=None):
        lm = prep_load(slot, src_ap, dsem, scale_col)
        return prep_tp(slot, lm)

    def mm3(bank, a_h, a_l, b_h, b_l, first, last, n=512):
        em("tensor", lambda e, o=bank, a=a_h, b=b_h, st=first, n=n: e.matmul(
            ps(o)[:, 0:n], a, b, start=st, stop=False))
        em("tensor", lambda e, o=bank, a=a_h, b=b_l, n=n: e.matmul(
            ps(o)[:, 0:n], a, b, start=False, stop=False))
        if not last:
            em("tensor", lambda e, o=bank, a=a_l, b=b_h, n=n: e.matmul(
                ps(o)[:, 0:n], a, b, start=False, stop=False))
        else:
            step("tensor", "tP", lambda e, o=bank, a=a_l, b=b_h, n=n: e.matmul(
                ps(o)[:, 0:n], a, b, start=False, stop=True))

    order = [(j, "k") for j in range(HC, 2 * HC)] + \
            [(j, "v") for j in range(2 * HC, 3 * HC)] + \
            [(j, "q") for j in range(0, HC)]
    vst_res = Res(1)
    qkv_lm = {0: prep_load(0, w_qkv[HC * 128:(HC + 1) * 128, :], "d_wa")}
    qkv_pm = {0: prep_tp(0, qkv_lm[0])}
    slab_res.m[0] = {"tP": cnt["tP"]}
    for idx, (j, kind) in enumerate(order):
        slot = idx % 2
        if idx + 1 < len(order):
            j2 = order[idx + 1][0]
            qkv_lm[idx + 1] = prep_load((idx + 1) % 2, w_qkv[j2 * 128:(j2 + 1) * 128, :],
                                        "d_wa" if (idx + 1) % 2 == 0 else "d_wb")
        tiles_s, tiles_v = qkv_pm[idx]
        if idx == 0:
            w("tensor", "vP", xn_v)
            w("tensor", "sP", xn_s)
        w("tensor", "sP", tiles_s)
        w("tensor", "vP", tiles_v)
        bwait("tensor", slot * 2, slot * 2 + 1)
        for c in range(HC):
            for half in range(2):
                tb = half * 512
                mm3(slot * 2 + half,
                    wt_hi_t(slot, c), wt_lo_t(slot, c),
                    xn_hi[:, c * T + tb:c * T + tb + 512], xn_lo[:, c * T + tb:c * T + tb + 512],
                    c == 0, c == HC - 1)
        slab_res.m[slot] = {"tP": cnt["tP"]}
        if idx + 1 < len(order):
            qkv_pm[idx + 1] = prep_tp((idx + 1) % 2, qkv_lm[idx + 1])
        bias = bqkv[:, j:j + 1]
        w("scalar", "tP")
        w("vector", "tP")
        sslot = idx % 2
        stg_res.wait("scalar", sslot)
        stg_res.wait("vector", sslot)
        for half in range(2):
            bk = slot * 2 + half
            tb = half * 512
            step("scalar", "sP", lambda e, b=bk, tb=tb, bias=bias, ss=sslot: e.add(
                stg_hi[ss][:, tb:tb + 512], ps(b), bias))
            w("vector", "sP")
            step("vector", "vP", lambda e, b=bk, tb=tb, bias=bias, ss=sslot: e.scalar_tensor_tensor(
                stg_lo[ss][:, tb:tb + 512], ps(b), bias, stg_hi[ss][:, tb:tb + 512],
                op0=ALU.add, op1=ALU.subtract))
        brel([slot * 2, slot * 2 + 1], "sP", "vP")
        w("gpsimd", "sP")
        w("gpsimd", "vP")
        if kind == "k":
            r = j - HC
            dma("gpsimd", kfm(0)[r * 128:(r + 1) * 128, :], stg_hi[sslot][:, :], "d_st")
            dma("gpsimd", kfm(1)[r * 128:(r + 1) * 128, :], stg_lo[sslot][:, :], "d_st")
            stg_res.rel(sslot, "d_st")
        elif kind == "q":
            dma("gpsimd", qsp.ap()[0, j * 128:(j + 1) * 128, :], stg_hi[sslot][:, :], "d_st")
            dma("gpsimd", qsp.ap()[1, j * 128:(j + 1) * 128, :], stg_lo[sslot][:, :], "d_st")
            stg_res.rel(sslot, "d_st")
        else:
            r = j - 2 * HC
            w("vector", "vP")
            step("vector", "vP", lambda e, ss=sslot: e.tensor_add(
                vfm[:, :], stg_hi[ss][:, :], stg_lo[ss][:, :]))
            stg_res.rel(sslot, "vP")
            w("tensor", "vP")
            vst_res.wait("scalar", 0)
            vst_res.wait("vector", 0)
            for tt in range(NTT):
                bk = 4 + (tt % 2)
                bwait("tensor", bk)
                step("tensor", "tP", lambda e, tt=tt, b=bk: e.transpose(
                    ps(b)[:, 0:128], vfm[:, tt * 128:(tt + 1) * 128], ident[:, :]))
                w("scalar", "tP")
                w("vector", "tP")
                step("scalar", "sP", lambda e, b=bk, tt=tt: e.copy(
                    vst_hi[:, tt * 128:(tt + 1) * 128], ps(b)[:, 0:128]))
                w("vector", "sP")
                step("vector", "vP", lambda e, b=bk, tt=tt: e.scalar_tensor_tensor(
                    vst_lo[:, tt * 128:(tt + 1) * 128], ps(b)[:, 0:128], 1.0,
                    vst_hi[:, tt * 128:(tt + 1) * 128], op0=ALU.mult, op1=ALU.subtract))
                brel([bk], "sP", "vP")
            w("gpsimd", "sP")
            w("gpsimd", "vP")
            vh = gin.ap()[2].rearrange("(a p Hh) -> p a Hh", p=128, Hh=H)[:, :, r * 128:(r + 1) * 128]
            vl = gin.ap()[3].rearrange("(a p Hh) -> p a Hh", p=128, Hh=H)[:, :, r * 128:(r + 1) * 128]
            dma("gpsimd", vh, vst_hi.rearrange("p (a q) -> p a q", q=128), "d_st")
            dma("gpsimd", vl, vst_lo.rearrange("p (a q) -> p a q", q=128), "d_st")
            vst_res.rel(0, "d_st")

    w("gpsimd", "d_st")
    if phases < 3:
        if dbg:
            dma("gpsimd", gin_o.ap(), gin.ap(), "d_st")
        return _finish(nc, ops, stack, sems, cnt)
    step("gpsimd", "cc", lambda e: e.collective_compute(
        "AllGather", ALU.bypass, replica_groups=[list(range(NCORE))],
        ins=[gin.ap()], outs=[gout.ap()]))
    if dbg:
        w("gpsimd", "cc")
        dma("gpsimd", gin_o.ap(), gin.ap(), "d_st")
        dma("gpsimd", gout_o.ap(), gout.ap(), "d_st")

    for cp in range(HC):
        slot = cp % 2
        bwait("tensor", slot * 2, slot * 2 + 1)
        for c in range(HC):
            hs = hsu_p if H16[c, cp] > 0 else hsu_n
            a = hs[:, c * 128:(c + 1) * 128]
            for half in range(2):
                tb = half * 512
                bk = slot * 2 + half
                b_h = xn_hi[:, c * T + tb:c * T + tb + 512]
                b_l = xn_lo[:, c * T + tb:c * T + tb + 512]
                em("tensor", lambda e, o=bk, a=a, b=b_h, st=(c == 0): e.matmul(
                    ps(o), a, b, start=st, stop=False))
                if c < HC - 1:
                    em("tensor", lambda e, o=bk, a=a, b=b_l: e.matmul(
                        ps(o), a, b, start=False, stop=False))
                else:
                    step("tensor", "tP", lambda e, o=bk, a=a, b=b_l: e.matmul(
                        ps(o), a, b, start=False, stop=True))
        w("scalar", "tP")
        w("vector", "tP")
        sslot = cp % 2
        stg_res.wait("scalar", sslot)
        stg_res.wait("vector", sslot)
        for half in range(2):
            bk = slot * 2 + half
            tb = half * 512
            step("scalar", "sP", lambda e, b=bk, tb=tb, ss=sslot: e.copy(
                stg_hi[ss][:, tb:tb + 512], ps(b)))
            w("vector", "sP")
            step("vector", "vP", lambda e, b=bk, tb=tb, ss=sslot: e.scalar_tensor_tensor(
                stg_lo[ss][:, tb:tb + 512], ps(b), 1.0, stg_hi[ss][:, tb:tb + 512],
                op0=ALU.mult, op1=ALU.subtract))
        brel([slot * 2, slot * 2 + 1], "sP", "vP")
        w("gpsimd", "sP")
        w("gpsimd", "vP")
        dma("gpsimd", zsp.ap()[0, cp * 128:(cp + 1) * 128, :], stg_hi[sslot][:, :], "d_st")
        dma("gpsimd", zsp.ap()[1, cp * 128:(cp + 1) * 128, :], stg_lo[sslot][:, :], "d_st")
        stg_res.rel(sslot, "d_st")
    fwht1_t = cnt["tP"]

    if phases < 4:
        return _finish(nc, ops, stack, sems, cnt)

    z_hi = A(0, 32 * KB, BF16)
    z_lo = A(32 * KB, 32 * KB, BF16)
    xup_f = C(32 * KB, 2 * KB, F32)
    xgt_f = C(34 * KB, 2 * KB, F32)
    sg_f = C(36 * KB, 2 * KB, F32)
    u_f = C(38 * KB, 2 * KB, F32)
    w("sync", "d_st")
    w("sync", "tP", fwht1_t)
    dma("sync", d3(z_hi, T), fm3(zsp.ap()[0]), "d_z")
    dma("sync", d3(z_lo, T), fm3(zsp.ap()[1]), "d_z")

    ug_lm = {0: (prep_load(0, w_up[0:128, :], "d_wa"),
                 prep_load(1, w_gate[0:128, :], "d_wb"))}
    ug_pm = {0: (prep_tp(0, ug_lm[0][0]), prep_tp(1, ug_lm[0][1]))}
    slab_res.m[0] = {"tP": cnt["tP"]}
    slab_res.m[1] = {"tP": cnt["tP"]}
    for j in range(IC):
        if j + 1 < IC:
            ug_lm[j + 1] = (prep_load(0, w_up[(j + 1) * 128:(j + 2) * 128, :], "d_wa"),
                            prep_load(1, w_gate[(j + 1) * 128:(j + 2) * 128, :], "d_wb"))
        if j == 0:
            w("tensor", "d_z")
        (t0s, t0v), (t1s, t1v) = ug_pm[j]
        w("tensor", "sP", max(t0s, t1s))
        w("tensor", "vP", max(t0v, t1v))
        pl = [(0, 1, 2, 3), (4, 5, 0, 1), (2, 3, 4, 5)][j % 3]
        ugmm = {}
        for half in range(2):
            tb = half * 512
            bwait("tensor", pl[half], pl[2 + half])
            for c in range(HC):
                mm3(pl[half], wt_hi_t(0, c), wt_lo_t(0, c),
                    z_hi[:, c * T + tb:c * T + tb + 512], z_lo[:, c * T + tb:c * T + tb + 512],
                    c == 0, c == HC - 1)
            for c in range(HC):
                mm3(pl[2 + half], wt_hi_t(1, c), wt_lo_t(1, c),
                    z_hi[:, c * T + tb:c * T + tb + 512], z_lo[:, c * T + tb:c * T + tb + 512],
                    c == 0, c == HC - 1)
            ugmm[half] = cnt["tP"]
        if j + 1 < IC:
            ug_pm[j + 1] = (prep_tp(0, ug_lm[j + 1][0]), prep_tp(1, ug_lm[j + 1][1]))
        slab_res.m[0] = {"tP": cnt["tP"]}
        slab_res.m[1] = {"tP": cnt["tP"]}
        for half in range(2):
            w("vector", "tP", ugmm[half])
            step("vector", "vP", lambda e, b=pl[half], j=j: e.tensor_scalar_mul(
                xup_f[:, :], ps(b), s_up[:, j:j + 1]))
            step("vector", "vP", lambda e, b=pl[2 + half], j=j: e.tensor_scalar_mul(
                xgt_f[:, :], ps(b), s_gt[:, j:j + 1]))
            brel([pl[half], pl[2 + half]], "vP")
            w("scalar", "vP")
            step("scalar", "sP", lambda e: e.activation(sg_f[:, :], xgt_f[:, :], AF.Silu))
            w("vector", "sP")
            step("vector", "vP", lambda e, j=j: e.scalar_tensor_tensor(
                u_f[:, :], sg_f[:, :], sudp[:, j:j + 1], xup_f[:, :], op0=ALU.mult, op1=ALU.mult))
            sslot = (2 * j + half) % 2
            stg_res.wait("vector", sslot)
            stg_res.wait("scalar", sslot)
            w("scalar", "vP")
            step("scalar", "sP", lambda e, ss=sslot: e.copy(stg_hi[ss][:, 0:512], u_f[:, :]))
            w("vector", "sP")
            step("vector", "vP", lambda e, ss=sslot: e.scalar_tensor_tensor(
                stg_lo[ss][:, 0:512], u_f[:, :], 1.0, stg_hi[ss][:, 0:512],
                op0=ALU.mult, op1=ALU.subtract))
            w("gpsimd", "sP")
            w("gpsimd", "vP")
            dma("gpsimd", usp.ap()[0, j * 128:(j + 1) * 128, half * 512:(half + 1) * 512],
                stg_hi[sslot][:, 0:512], "d_st")
            dma("gpsimd", usp.ap()[1, j * 128:(j + 1) * 128, half * 512:(half + 1) * 512],
                stg_lo[sslot][:, 0:512], "d_st")
            stg_res.rel(sslot, "d_st")

    wst_res = Res(2)
    units = []
    for j in range(HC):
        for s4 in range(4):
            units.append(("wd", j, s4))
        units.append(("wo", j, 0))

    def wprep_load(u, slot):
        kind, j, s4 = u
        wst_res.wait("sync", slot)
        if kind == "wd":
            return prep_load(slot, w_down[j * 128:(j + 1) * 128, s4 * H:(s4 + 1) * H],
                             "d_wa" if slot == 0 else "d_wb", scale_col=s_dpc[:, j:j + 1])
        return prep_load(slot, w_out[j * 128:(j + 1) * 128, :],
                         "d_wa" if slot == 0 else "d_wb")

    wp_lm = {0: wprep_load(units[0], 0)}
    for ui, u in enumerate(units):
        slot = ui % 2
        if ui + 1 < len(units):
            wp_lm[ui + 1] = wprep_load(units[ui + 1], (ui + 1) % 2)
        tv = prep_tp(slot, wp_lm[ui], pbanks=(0, 1, 2, 3, 4, 5))
        kind, j, s4 = u
        w("gpsimd", "vP", tv[1])
        whiv = d3(wtil[slot], 256).transpose([0, 1, 2])[:, :, 0:128]
        wlov = d3(wtil[slot], 256)[:, :, 128:256]
        if kind == "wd":
            dma("gpsimd", wdfm.ap()[j, 0, s4 * HC:(s4 + 1) * HC].rearrange("c p q -> p c q"),
                whiv, "d_wd")
            dma("gpsimd", wdfm.ap()[j, 1, s4 * HC:(s4 + 1) * HC].rearrange("c p q -> p c q"),
                wlov, "d_wd")
            wst_res.rel(slot, "d_wd")
        else:
            dma("gpsimd", wofm.ap()[j, 0].rearrange("c p q -> p c q"), whiv, "d_wo")
            dma("gpsimd", wofm.ap()[j, 1].rearrange("c p q -> p c q"), wlov, "d_wo")
            wst_res.rel(slot, "d_wo")
        slab_res.m[slot] = {"tP": cnt["tP"], "vP": cnt["vP"]}

    if phases < 5:
        return _finish(nc, ops, stack, sems, cnt)

    ctx_sb = A(0, 64 * KB, F32)
    kvs_hi = [B(0, 16 * KB, BF16), B(32 * KB, 16 * KB, BF16)]
    kvs_lo = [B(16 * KB, 16 * KB, BF16), B(48 * KB, 16 * KB, BF16)]
    q_hi = [C(0, 4 * KB, BF16), C(42 * KB, 4 * KB, BF16)]
    q_lo = [C(4 * KB, 4 * KB, BF16), C(46 * KB, 4 * KB, BF16)]
    sc_sb = [C(8 * KB + qt * 2 * KB, 2 * KB, F32) for qt in range(NQ)]
    pt_hi = C(24 * KB, 8 * KB, BF16)
    pt_lo = C(32 * KB, 8 * KB, BF16)
    pstg_h = [C(40 * KB, 1 * KB, BF16), C(50 * KB, 1 * KB, BF16)]
    pstg_l = [C(41 * KB, 1 * KB, BF16), C(51 * KB, 1 * KB, BF16)]
    aM = lambda qt: st_col(72 + qt)
    aL = lambda qt: st_col(80 + qt)
    aMn = lambda qt: st_col(88 + qt)
    aRs = lambda qt: st_col(96 + qt)
    aBi = lambda qt: st_col(104 + qt)
    aT = st_col(112)
    k_res = Res(1)
    v_res = Res(1)
    q_res = Res(2)
    pstg_res = Res(2)
    pstg_ready = [(0, 0), (0, 0)]

    w("vector", "d_st")
    step("vector", "vP", lambda e: e.memset(stat[:, 72:80], -1e30))
    step("vector", "vP", lambda e: e.memset(stat[:, 80:88], 0.0))
    step("vector", "vP", lambda e: e.memset(ctx_sb[:, :], 0.0))
    attn_init_v = cnt["vP"]

    def pt_view(t, c, qt):
        return t.rearrange("p (c g q) -> p c g q", g=NQ, q=128)[:, c, qt, :]

    def emit_ptrans(qt):
        bk_t = 6 + (qt % 2)
        sl = qt % 2
        bwait("tensor", bk_t)
        w("tensor", "sP", pstg_ready[sl][0])
        w("tensor", "vP", pstg_ready[sl][1])
        for c in range(4):
            step("tensor", "tP", lambda e, c=c, b=bk_t, sl=sl: e.transpose(
                psb(b)[:, c * 256:c * 256 + 128], pstg_h[sl][:, c * 128:(c + 1) * 128], identb[:, :]))
            step("tensor", "tP", lambda e, c=c, b=bk_t, sl=sl: e.transpose(
                psb(b)[:, c * 256 + 128:c * 256 + 256], pstg_l[sl][:, c * 128:(c + 1) * 128], identb[:, :]))
        pstg_res.rel(sl, "tP")
        w("vector", "tP")
        step("vector", "vP", lambda e, b=bk_t, qt=qt: e.tensor_copy(
            pt_hi.rearrange("p (c g q) -> p c g q", g=NQ, q=128)[:, :, qt, :],
            psb(b).rearrange("p (c q) -> p c q", q=256)[:, :, 0:128]))
        step("vector", "vP", lambda e, b=bk_t, qt=qt: e.tensor_copy(
            pt_lo.rearrange("p (c g q) -> p c g q", g=NQ, q=128)[:, :, qt, :],
            psb(b).rearrange("p (c q) -> p c q", q=256)[:, :, 128:256]))
        brel([bk_t], "vP")

    for s in range(NS):
        rank, off = s // 2, (s % 2) * 512
        if s == 0:
            w("sync", "cc", 1)
            w("sync", "vP", attn_init_v)
        k_res.wait("sync", 0)
        dma("sync", d3(kvs_hi[0][:, :], 512), fm3(g_k(rank, 0)[:, off:off + 512]), "d_k")
        dma("sync", d3(kvs_lo[0][:, :], 512), fm3(g_k(rank, 1)[:, off:off + 512]), "d_k")
        v_res.wait("sync", 0)
        dma("sync", d3(kvs_hi[1][:, :], H), fm3(g_v(rank, 0)[off:off + 512, :]), "d_v")
        dma("sync", d3(kvs_lo[1][:, :], H), fm3(g_v(rank, 1)[off:off + 512, :]), "d_v")
        for qt in range(NQ):
            qs = qt % 2
            q_res.wait("sync", qs)
            dma("sync", d3(q_hi[qs][:, :], 128), fm3(qsp.ap()[0, :, qt * 128:(qt + 1) * 128]), "d_q")
            dma("sync", d3(q_lo[qs][:, :], 128), fm3(qsp.ap()[1, :, qt * 128:(qt + 1) * 128]), "d_q")
            bk = 4 + (qt % 2)
            w("tensor", "d_q")
            if qt == 0:
                w("tensor", "d_k")
                if s == 0:
                    w("tensor", "vP", attn_init_v)
            bwait("tensor", bk)
            for c in range(HC):
                mm3(bk, q_hi[qs][:, c * 128:(c + 1) * 128], q_lo[qs][:, c * 128:(c + 1) * 128],
                    kvs_hi[0][:, c * 512:(c + 1) * 512], kvs_lo[0][:, c * 512:(c + 1) * 512],
                    c == 0, c == HC - 1)
            q_res.rel(qs, "tP")
            if qt > 0:
                emit_ptrans(qt - 1)
            w("vector", "tP")
            step("vector", "vP", lambda e, b=bk, qt=qt: e.tensor_copy(sc_sb[qt][:, :], ps(b)))
            brel([bk], "vP")
            w("vector", "vP")
            step("vector", "vP", lambda e, qt=qt: e.reduce_max(aMn(qt), sc_sb[qt][:, :], axis=AX.X))
            w("vector", "vP")
            step("vector", "vP", lambda e, qt=qt: e.tensor_max(aMn(qt), aMn(qt), aM(qt)))
            w("vector", "vP")
            step("vector", "vP", lambda e, qt=qt: e.tensor_sub(aRs(qt), aM(qt), aMn(qt)))
            step("vector", "vP", lambda e, qt=qt: e.tensor_scalar_mul(aBi(qt), aMn(qt), -SCL))
            step("vector", "vP", lambda e, qt=qt: e.tensor_copy(aM(qt), aMn(qt)))
            w("scalar", "vP")
            step("scalar", "sP", lambda e, qt=qt: e.activation(aRs(qt), aRs(qt), AF.Exp, scale=SCL))
            step("scalar", "sP", lambda e, qt=qt: e.activation(
                sc_sb[qt][:, :], sc_sb[qt][:, :], AF.Exp, bias=aBi(qt), scale=SCL))
            w("vector", "sP")
            step("vector", "vP", lambda e, qt=qt: e.reduce_sum(aT, sc_sb[qt][:, :], axis=AX.X))
            w("vector", "vP")
            step("vector", "vP", lambda e, qt=qt: e.scalar_tensor_tensor(
                aL(qt), aL(qt), aRs(qt), aT, op0=ALU.mult, op1=ALU.add))
            pstg_res.wait("scalar", qs)
            step("scalar", "sP", lambda e, qt=qt, sl=qs: e.copy(pstg_h[sl][:, :], sc_sb[qt][:, :]))
            w("vector", "sP")
            step("vector", "vP", lambda e, qt=qt, sl=qs: e.scalar_tensor_tensor(
                pstg_l[sl][:, :], sc_sb[qt][:, :], 1.0, pstg_h[sl][:, :],
                op0=ALU.mult, op1=ALU.subtract))
            pstg_ready[qs] = (cnt["sP"], cnt["vP"])
        k_res.rel(0, "tP")
        emit_ptrans(NQ - 1)
        w("tensor", "d_v")
        for qt in range(NQ):
            grp = [0, 1, 2, 3] if qt % 2 == 0 else [4, 5, 6, 7]
            bwait("tensor", *grp)
            for c in range(4):
                a_h = pt_view(pt_hi, c, qt)
                a_l = pt_view(pt_lo, c, qt)
                for hb in range(4):
                    b_h = kvs_hi[1][:, c * 2048 + hb * 512:c * 2048 + (hb + 1) * 512]
                    b_l = kvs_lo[1][:, c * 2048 + hb * 512:c * 2048 + (hb + 1) * 512]
                    mm3(grp[hb], a_h, a_l, b_h, b_l, c == 0, c == 3)
            w("vector", "tP")
            for hb in range(4):
                step("vector", "vP", lambda e, qt=qt, b=grp[hb], hb=hb: e.scalar_tensor_tensor(
                    ctx_sb[:, qt * H + hb * 512:qt * H + (hb + 1) * 512],
                    ctx_sb[:, qt * H + hb * 512:qt * H + (hb + 1) * 512],
                    aRs(qt), ps(b), op0=ALU.mult, op1=ALU.add))
            brel(grp, "vP")
        v_res.rel(0, "tP")

    cth = B(0, 4 * KB, BF16)
    ctl = B(4 * KB, 4 * KB, BF16)
    cfs_h = [B(8 * KB, 4 * KB, BF16), B(16 * KB, 4 * KB, BF16)]
    cfs_l = [B(12 * KB, 4 * KB, BF16), B(20 * KB, 4 * KB, BF16)]
    cf_res = Res(2)
    for qt in range(NQ):
        w("vector", "tP")
        step("vector", "vP", lambda e, qt=qt: e.reciprocal(aT, aL(qt)))
        w("scalar", "vP")
        step("scalar", "sP", lambda e, qt=qt: e.mul(cth[:, :], ctx_sb[:, qt * H:(qt + 1) * H], aT))
        w("vector", "sP")
        step("vector", "vP", lambda e, qt=qt: e.scalar_tensor_tensor(
            ctl[:, :], ctx_sb[:, qt * H:(qt + 1) * H], aT, cth[:, :],
            op0=ALU.mult, op1=ALU.subtract))
        w("tensor", "sP")
        w("tensor", "vP")
        qs = qt % 2
        cf_res.wait("vector", qs)
        for g in range(4):
            bk = 6 + (g % 2)
            bwait("tensor", bk)
            for k in range(4):
                c = g * 4 + k
                step("tensor", "tP", lambda e, c=c, b=bk, k=k: e.transpose(
                    psb(b)[:, k * 256:k * 256 + 128], cth[:, c * 128:(c + 1) * 128], identb[:, :]))
                step("tensor", "tP", lambda e, c=c, b=bk, k=k: e.transpose(
                    psb(b)[:, k * 256 + 128:k * 256 + 256], ctl[:, c * 128:(c + 1) * 128], identb[:, :]))
            w("vector", "tP")
            step("vector", "vP", lambda e, b=bk, g=g, qs=qs: e.tensor_copy(
                cfs_h[qs].rearrange("p (c q) -> p c q", q=128)[:, g * 4:(g + 1) * 4, :],
                psb(b).rearrange("p (c q) -> p c q", q=256)[:, :, 0:128]))
            step("vector", "vP", lambda e, b=bk, g=g, qs=qs: e.tensor_copy(
                cfs_l[qs].rearrange("p (c q) -> p c q", q=128)[:, g * 4:(g + 1) * 4, :],
                psb(b).rearrange("p (c q) -> p c q", q=256)[:, :, 128:256]))
            brel([bk], "vP")
        w("gpsimd", "vP")
        dma("gpsimd", fm3(ctxsp.ap()[0, :, qt * 128:(qt + 1) * 128]), d3(cfs_h[qs][:, :], 128), "d_ctx")
        dma("gpsimd", fm3(ctxsp.ap()[1, :, qt * 128:(qt + 1) * 128]), d3(cfs_l[qs][:, :], 128), "d_ctx")
        cf_res.rel(qs, "d_ctx")

    if phases < 6:
        return _finish(nc, ops, stack, sems, cnt)

    u_hi = A(0, 32 * KB, BF16)
    u_lo = A(32 * KB, 32 * KB, BF16)
    w1f = B(0, 64 * KB, F32)
    w1tmp = C(0, 32 * KB, F32)
    ctxq_h = C(0, 8 * KB, BF16)
    ctxq_l = C(8 * KB, 8 * KB, BF16)
    wo_h = [C(16 * KB + 8 * k * KB, 4 * KB, BF16) for k in range(2)]
    wo_l = [C(20 * KB + 8 * k * KB, 4 * KB, BF16) for k in range(2)]
    comb = C(32 * KB, 16 * KB, F32)
    resid = C(48 * KB, 8 * KB, F32)
    wd_h = [B(0, 16 * KB, BF16), B(32 * KB, 16 * KB, BF16)]
    wd_l = [B(16 * KB, 16 * KB, BF16), B(48 * KB, 16 * KB, BF16)]
    ostg = B(0, 8 * KB, F32)
    wd_res = Res(2)
    ostg_res = Res(1)

    for q in range(4):
        cb = q * QT4
        if q == 0:
            w("sync", "d_ctx")
            w("sync", "tP")
            w("sync", "vP")
        else:
            w("sync", "tP")
            w("sync", "vP")
            w("sync", "d_out")
        dma("sync", d3(u_hi[:, :], QT4), fm3(usp.ap()[0])[:, :, cb:cb + QT4], "d_u")
        dma("sync", d3(u_lo[:, :], QT4), fm3(usp.ap()[1])[:, :, cb:cb + QT4], "d_u")
        w("tensor", "d_u")
        for cp in range(IC):
            bk = cp % 4
            c0, c1 = (cp % 32), (cp % 32) + 32
            hsgn = h128 if cp < 32 else h128n
            bwait("tensor", bk)
            em("tensor", lambda e, b=bk, c=c0: e.matmul(
                ps(b)[:, 0:QT4], h128[:, :], u_hi[:, c * QT4:(c + 1) * QT4], start=True, stop=False))
            em("tensor", lambda e, b=bk, c=c0: e.matmul(
                ps(b)[:, 0:QT4], h128[:, :], u_lo[:, c * QT4:(c + 1) * QT4], start=False, stop=False))
            em("tensor", lambda e, b=bk, a=hsgn, c=c1: e.matmul(
                ps(b)[:, 0:QT4], a[:, :], u_hi[:, c * QT4:(c + 1) * QT4], start=False, stop=False))
            step("tensor", "tP", lambda e, b=bk, a=hsgn, c=c1: e.matmul(
                ps(b)[:, 0:QT4], a[:, :], u_lo[:, c * QT4:(c + 1) * QT4], start=False, stop=True))
            w("vector", "tP")
            step("vector", "vP", lambda e, b=bk, cp=cp: e.tensor_copy(
                w1f[:, cp * QT4:(cp + 1) * QT4], ps(b)[:, 0:QT4]))
            brel([bk], "vP")
        for bit in (16, 8, 4, 2, 1):
            na = IC // (2 * bit)
            va = w1f.rearrange("p (a d t) -> p a d t", a=na, d=2 * bit)[:, :, 0:bit, :]
            vb = w1f.rearrange("p (a d t) -> p a d t", a=na, d=2 * bit)[:, :, bit:2 * bit, :]
            vt = w1tmp.rearrange("p (a d t) -> p a d t", a=na, d=bit)
            w("vector", "vP")
            step("vector", "vP", lambda e, a=va, b=vb, t=vt: e.tensor_sub(t, a, b))
            w("vector", "vP")
            step("vector", "vP", lambda e, a=va, b=vb: e.tensor_add(a, a, b))
            w("vector", "vP")
            step("vector", "vP", lambda e, b=vb, t=vt: e.tensor_copy(b, t))
        w("vector", "vP")
        w("scalar", "vP")
        step("scalar", "sP", lambda e: e.copy(u_hi[:, :], w1f[:, :]))
        w("vector", "sP")
        step("vector", "vP", lambda e: e.scalar_tensor_tensor(
            u_lo[:, :], w1f[:, :], 1.0, u_hi[:, :], op0=ALU.mult, op1=ALU.subtract))
        w("sync", "vP")
        dma("sync", d3(ctxq_h[:, :], QT4), fm3(ctxsp.ap()[0])[:, :, cb:cb + QT4], "d_ctx")
        dma("sync", d3(ctxq_l[:, :], QT4), fm3(ctxsp.ap()[1])[:, :, cb:cb + QT4], "d_ctx")
        for j in range(HC):
            slot = j % 2
            wd_res.wait("sync", slot)
            dma("sync", d3(wd_h[slot][:, :], 128), wdfm.ap()[j, 0].rearrange("c p q -> p c q"), "d_wd")
            dma("sync", d3(wd_l[slot][:, :], 128), wdfm.ap()[j, 1].rearrange("c p q -> p c q"), "d_wd")
            dma("sync", d3(wo_h[slot][:, :], 128), wofm.ap()[j, 0].rearrange("c p q -> p c q"), "d_wo")
            dma("sync", d3(wo_l[slot][:, :], 128), wofm.ap()[j, 1].rearrange("c p q -> p c q"), "d_wo")
            bk = 4 + (j % 2)
            w("tensor", "d_wd")
            w("tensor", "d_wo")
            if j == 0:
                w("tensor", "sP")
                w("tensor", "vP")
                w("tensor", "d_ctx")
            bwait("tensor", bk)
            for c in range(IC):
                mm3(bk, wd_h[slot][:, c * 128:(c + 1) * 128], wd_l[slot][:, c * 128:(c + 1) * 128],
                    u_hi[:, c * QT4:(c + 1) * QT4], u_lo[:, c * QT4:(c + 1) * QT4],
                    c == 0, False, n=QT4)
            for c in range(HC):
                mm3(bk, wo_h[slot][:, c * 128:(c + 1) * 128], wo_l[slot][:, c * 128:(c + 1) * 128],
                    ctxq_h[:, c * QT4:(c + 1) * QT4], ctxq_l[:, c * QT4:(c + 1) * QT4],
                    False, c == HC - 1, n=QT4)
            wd_res.rel(slot, "tP")
            w("scalar", "tP")
            step("scalar", "sP", lambda e, b=bk, j=j: e.add(
                comb[:, j * QT4:(j + 1) * QT4], ps(b)[:, 0:QT4], boutc[:, j:j + 1]))
            brel([bk], "sP")
        for tt in range(2):
            trow = cb + tt * 128
            w("sync", "d_out")
            dma("sync", resid[:, :], x_in[trow:trow + 128, :], "d_x")
            w("tensor", "sP")
            w("tensor", "d_out")
            for g in range(4):
                bk = 2 + (g % 2)
                bwait("tensor", bk)
                for k in range(4):
                    j = g * 4 + k
                    step("tensor", "tP", lambda e, j=j, b=bk, k=k, tt=tt: e.transpose(
                        ps(b)[:, k * 128:(k + 1) * 128],
                        comb[:, j * QT4 + tt * 128:j * QT4 + tt * 128 + 128], ident[:, :]))
                w("vector", "tP")
                w("vector", "d_x")
                if g == 0:
                    ostg_res.wait("vector", 0)
                step("vector", "vP", lambda e, b=bk, g=g: e.tensor_add(
                    ostg[:, g * 512:(g + 1) * 512], ps(b), resid[:, g * 512:(g + 1) * 512]))
                brel([bk], "vP")
            w("gpsimd", "vP")
            dma("gpsimd", out_d[trow:trow + 128, :], ostg[:, :], "d_out")
            ostg_res.rel(0, "d_out")

    return _finish(nc, ops, stack, sems, cnt)


def _finish(nc, ops, stack, sems, cnt):
    if cnt["d_out"]:
        ops["gpsimd"].append(lambda e: e.wait_ge(sems["d_out"], cnt["d_out"]))
    if cnt["d_st"]:
        ops["gpsimd"].append(lambda e: e.wait_ge(sems["d_st"], cnt["d_st"]))
    if cnt["d_ctx"]:
        ops["gpsimd"].append(lambda e: e.wait_ge(sems["d_ctx"], cnt["d_ctx"]))
    with nc.Block() as block:
        @block.sync
        def _(e):
            for f in ops["sync"]:
                f(e)

        @block.gpsimd
        def _(e):
            for f in ops["gpsimd"]:
                f(e)

        @block.vector
        def _(e):
            for f in ops["vector"]:
                f(e)

        @block.scalar
        def _(e):
            for f in ops["scalar"]:
                f(e)

        @block.tensor
        def _(e):
            for f in ops["tensor"]:
                f(e)
    stack.close()
    return nc


def kernel(L0_attn_hidden_states, L0_attn_w_qkv, L0_attn_b_qkv, L0_attn_w_out,
           L0_attn_b_out, L0_ffn_SU_ug, L0_ffn_SU_dp, L0_ffn_Wscale_ug,
           L0_ffn_Wscale_dp, L0_ffn_W_up, L0_ffn_W_gate, L0_ffn_W_down):
    x = np.ascontiguousarray(np.asarray(L0_attn_hidden_states, dtype=np.float32).reshape(NCORE * T, H))
    full = {
        "w_qkv": np.ascontiguousarray(np.asarray(L0_attn_w_qkv, dtype=np.float32)),
        "b_qkv": np.ascontiguousarray(np.asarray(L0_attn_b_qkv, dtype=np.float32)),
        "w_out": np.ascontiguousarray(np.asarray(L0_attn_w_out, dtype=np.float32)),
        "b_out": np.ascontiguousarray(np.asarray(L0_attn_b_out, dtype=np.float32)),
        "su_ug": np.ascontiguousarray(np.asarray(L0_ffn_SU_ug, dtype=np.float32)),
        "su_dp": np.ascontiguousarray(np.asarray(L0_ffn_SU_dp, dtype=np.float32)),
        "wsc_ug": np.ascontiguousarray(np.asarray(L0_ffn_Wscale_ug, dtype=np.float32)),
        "wsc_dp": np.ascontiguousarray(np.asarray(L0_ffn_Wscale_dp, dtype=np.float32)),
        "w_up": np.ascontiguousarray(np.asarray(L0_ffn_W_up, dtype=np.float32)),
        "w_gate": np.ascontiguousarray(np.asarray(L0_ffn_W_gate, dtype=np.float32)),
        "w_down": np.ascontiguousarray(np.asarray(L0_ffn_W_down, dtype=np.float32)),
    }
    nc = build_program()
    in_maps = [dict(full, x=x[c * T:(c + 1) * T]) for c in range(NCORE)]
    res = run_bass_kernel_spmd(nc, in_maps, core_ids=list(range(NCORE)))
    out = np.concatenate([res.results[c]["out"] for c in range(NCORE)], axis=0)
    return out.reshape(1, NCORE * T, H).astype(np.float32)


# version 09
# speedup vs baseline: 1.4910x; 1.0301x over previous
import numpy as np
import ml_dtypes
import concourse.bass as bass
import concourse.mybir as mybir
from concourse.bass_utils import run_bass_kernel_spmd

F32 = mybir.dt.float32
BF16 = mybir.dt.bfloat16
AF = mybir.ActivationFunctionType
ALU = mybir.AluOpType
AX = mybir.AxisListType

NCORE = 8
T = 1024
NTT = T // 128
H = 2048
HC = H // 128
I = 8192
IC = I // 128
SB = 512
NS = (NCORE * T) // SB
NQ = NTT
QT4 = T // 4
RS2 = float(1.0 / np.sqrt(2048.0))
RS8 = float(1.0 / np.sqrt(8192.0))
SCL = float(1.0 / np.sqrt(2048.0))
KB = 1024


def _hadamard(n):
    h = np.array([[1.0]], dtype=np.float64)
    while h.shape[0] < n:
        h = np.block([[h, h], [h, -h]])
    return h


H16 = _hadamard(16)


def build_program(dbg=False, phases=99):
    nc = bass.Bass("TRN2", target_bir_lowering=False, detect_race_conditions=False)

    x_in = nc.dram_tensor("x", [T, H], F32, kind="ExternalInput").ap()
    w_qkv = nc.dram_tensor("w_qkv", [3 * H, H], F32, kind="ExternalInput").ap()
    b_qkv = nc.dram_tensor("b_qkv", [3 * H], F32, kind="ExternalInput").ap()
    w_out = nc.dram_tensor("w_out", [H, H], F32, kind="ExternalInput").ap()
    b_out = nc.dram_tensor("b_out", [H], F32, kind="ExternalInput").ap()
    su_ug = nc.dram_tensor("su_ug", [H], F32, kind="ExternalInput").ap()
    su_dp = nc.dram_tensor("su_dp", [I], F32, kind="ExternalInput").ap()
    wsc_ug = nc.dram_tensor("wsc_ug", [2 * I], F32, kind="ExternalInput").ap()
    wsc_dp = nc.dram_tensor("wsc_dp", [H], F32, kind="ExternalInput").ap()
    w_up = nc.dram_tensor("w_up", [I, H], F32, kind="ExternalInput").ap()
    w_gate = nc.dram_tensor("w_gate", [I, H], F32, kind="ExternalInput").ap()
    w_down = nc.dram_tensor("w_down", [H, I], F32, kind="ExternalInput").ap()
    out_d = nc.dram_tensor("out", [T, H], F32, kind="ExternalOutput").ap()

    gin = nc.dram_tensor("gin", [4, H * T], BF16)
    gout = nc.dram_tensor("gout", [4 * NCORE, H * T], BF16)
    if dbg:
        gin_o = nc.dram_tensor("gin_o", [4, H * T], BF16, kind="ExternalOutput")
        gout_o = nc.dram_tensor("gout_o", [4 * NCORE, H * T], BF16, kind="ExternalOutput")
    ikind = "ExternalOutput" if dbg else "Internal"
    qsp = nc.dram_tensor("qsp", [2, H, T], BF16, kind=ikind)
    zsp = nc.dram_tensor("zsp", [2, H, T], BF16, kind=ikind)
    usp = nc.dram_tensor("usp", [2, I, T], BF16, kind=ikind)
    ctxsp = nc.dram_tensor("ctxsp", [2, H, T], BF16, kind=ikind)
    wdfm = nc.dram_tensor("wdfm", [HC, 2, IC, 128, 128], BF16)
    wofm = nc.dram_tensor("wofm", [HC, 2, HC, 128, 128], BF16)

    ident_c = nc.inline_tensor(np.eye(128, dtype=np.float32), name="ident_c").ap()
    identb_c = nc.inline_tensor(np.eye(128, dtype=ml_dtypes.bfloat16), name="identb_c").ap()
    h128_c = nc.inline_tensor(_hadamard(128).astype(ml_dtypes.bfloat16), name="h128_c").ap()

    def kfm(sec):
        return gin.ap()[sec].rearrange("(r c) -> r c", r=H)

    def g_k(rank, hl):
        return gout.ap()[4 * rank + hl].rearrange("(r c) -> r c", r=H)

    def g_v(rank, hl):
        return gout.ap()[4 * rank + 2 + hl].rearrange("(r c) -> r c", r=T)

    def fm3(src2d):
        return src2d.rearrange("(c p) t -> p c t", p=128)

    def d3(dst2d, t):
        return dst2d.rearrange("p (c t) -> p c t", t=t)

    ops = {k: [] for k in ("sync", "gpsimd", "vector", "scalar", "tensor")}

    def em(eng, fn):
        ops[eng].append(fn)

    semnames = ["tP", "vP", "sP", "cc", "d_x", "d_wa", "d_wb", "d_k", "d_v",
                "d_q", "d_z", "d_u", "d_ctx", "d_wd", "d_wo", "d_st", "d_out", "d_c"]
    cnt = {k: 0 for k in semnames}
    from contextlib import ExitStack
    stack = ExitStack()
    sems = {k: stack.enter_context(nc.semaphore(k)) for k in semnames}

    def w(eng, sem, val=None):
        v = cnt[sem] if val is None else val
        if v > 0:
            em(eng, lambda e, s=sems[sem], v=v: e.wait_ge(s, v))

    def dma(eng, out, in_, sem):
        cnt[sem] += 16
        em(eng, lambda e, o=out, i=in_, s=sems[sem]: e.dma_start(out=o, in_=i).then_inc(s, 16))

    def dma_nc(eng, out, in_, sem):
        cnt[sem] += 16
        em(eng, lambda e, o=out, i=in_, s=sems[sem]: e.dma_start(
            out=o, in_=i, allow_slow_non_contiguous=True).then_inc(s, 16))

    def step(eng, sem, fn):
        cnt[sem] += 1
        em(eng, lambda e, f=fn, s=sems[sem]: f(e).then_inc(s, 1))

    class Res:

        def __init__(self, n):
            self.m = [dict() for _ in range(n)]

        def wait(self, eng, slot):
            for sem, val in self.m[slot].items():
                w(eng, sem, val)

        def rel(self, slot, *semlist):
            self.m[slot] = {s: cnt[s] for s in semlist}

    arena = nc.alloc_sbuf_tensor("arena", [128, 200 * KB], mybir.dt.uint8).ap()

    def bview(off, nbytes, dt):
        return arena[:, off:off + nbytes].bitcast(dt)

    po = [0]

    def perm(nbytes, dt, align=64):
        o = (po[0] + align - 1) // align * align
        po[0] = o + nbytes
        return bview(o, nbytes, dt)

    ident = perm(512, F32)
    identb = perm(256, BF16)
    h128 = perm(256, BF16)
    h128n = perm(256, BF16)
    hsu_p = perm(HC * 256, BF16)
    hsu_n = perm(HC * 256, BF16)
    bqkv = perm(4 * 64, F32)
    boutc = perm(4 * 64, F32)
    s_up = perm(4 * 64, F32)
    s_gt = perm(4 * 64, F32)
    s_dpc = perm(4 * 64, F32)
    sudp = perm(4 * 64, F32)
    stat = perm(4 * 256, F32)
    assert po[0] <= 12 * KB, po[0]

    A0, B0, C0 = 12 * KB, 76 * KB, 140 * KB

    def A(off, nbytes, dt):
        assert off + nbytes <= 64 * KB
        return bview(A0 + off, nbytes, dt)

    def B(off, nbytes, dt):
        assert off + nbytes <= 64 * KB
        return bview(B0 + off, nbytes, dt)

    def C(off, nbytes, dt):
        assert off + nbytes <= 60 * KB
        return bview(C0 + off, nbytes, dt)

    def st_col(i):
        return stat[:, i:i + 1]

    psum = [stack.enter_context(nc.psum_tensor(f"ps{i}", [128, 512], F32)) for i in range(8)]

    def ps(i):
        return psum[i].ap()

    def psb(i):
        return ps(i).bitcast(BF16)

    banks = Res(8)

    def bwait(eng, *bks):
        for b in bks:
            banks.wait(eng, b)

    def brel(bks, *semlist):
        for b in bks:
            banks.rel(b, *semlist)

    dma("gpsimd", ident[:, :], ident_c, "d_c")
    dma("gpsimd", identb[:, :], identb_c, "d_c")
    dma("gpsimd", h128[:, :], h128_c, "d_c")
    dma_nc("gpsimd", bqkv[:, 0:48], b_qkv.rearrange("(j p) -> p j", p=128), "d_c")
    dma_nc("gpsimd", boutc[:, 0:HC], b_out.rearrange("(j p) -> p j", p=128), "d_c")
    dma_nc("gpsimd", stat[:, 0:HC], su_ug.rearrange("(j p) -> p j", p=128), "d_c")
    dma_nc("gpsimd", sudp[:, 0:IC], su_dp.rearrange("(j p) -> p j", p=128), "d_c")
    dma_nc("gpsimd", s_up[:, 0:IC], wsc_ug[0:I].rearrange("(j p) -> p j", p=128), "d_c")
    dma_nc("gpsimd", s_gt[:, 0:IC], wsc_ug[I:2 * I].rearrange("(j p) -> p j", p=128), "d_c")
    dma_nc("gpsimd", s_dpc[:, 0:HC], wsc_dp.rearrange("(j p) -> p j", p=128), "d_c")
    w("vector", "d_c")
    step("vector", "vP", lambda e: e.tensor_scalar_mul(s_up[:, 0:IC], s_up[:, 0:IC], RS2))
    step("vector", "vP", lambda e: e.tensor_scalar_mul(s_gt[:, 0:IC], s_gt[:, 0:IC], RS2))
    step("vector", "vP", lambda e: e.tensor_scalar_mul(s_dpc[:, 0:HC], s_dpc[:, 0:HC], RS8))
    step("vector", "vP", lambda e: e.tensor_scalar_mul(h128n[:, :], h128[:, :], -1.0))
    for c in range(HC):
        step("vector", "vP", lambda e, c=c: e.tensor_scalar_mul(
            hsu_p[:, c * 128:(c + 1) * 128], h128[:, :], st_col(c)))
    w("vector", "vP")
    for c in range(HC):
        step("vector", "vP", lambda e, c=c: e.tensor_scalar_mul(
            hsu_n[:, c * 128:(c + 1) * 128], hsu_p[:, c * 128:(c + 1) * 128], -1.0))
    consts_v = cnt["vP"]

    xn_hi = A(0, 32 * KB, BF16)
    xn_lo = A(32 * KB, 32 * KB, BF16)
    x_t = C(0, 8 * KB, F32)
    sq_t = C(8 * KB, 8 * KB, F32)
    xnt = [C(16 * KB, 8 * KB, F32), C(24 * KB, 8 * KB, F32)]
    sM = lambda tt: st_col(32 + tt)
    sV = lambda tt: st_col(40 + tt)
    sR = lambda tt: st_col(48 + tt)
    sBi = lambda tt: st_col(56 + tt)
    t1 = st_col(64)
    xnt_res = Res(2)

    for tt in range(NTT):
        if tt > 0:
            w("sync", "vP")
        dma("sync", x_t[:, :], x_in[tt * 128:(tt + 1) * 128, :], "d_x")
        w("vector", "d_x")
        if tt == 0:
            w("vector", "vP", consts_v)
        step("vector", "vP", lambda e: e.tensor_mul(sq_t[:, :], x_t[:, :], x_t[:, :]))
        step("vector", "vP", lambda e, tt=tt: e.reduce_sum(sM(tt), x_t[:, :], axis=AX.X))
        w("vector", "vP")
        step("vector", "vP", lambda e, tt=tt: e.reduce_sum(sV(tt), sq_t[:, :], axis=AX.X))
        w("vector", "vP")
        step("vector", "vP", lambda e, tt=tt: e.tensor_scalar_mul(sM(tt), sM(tt), 1.0 / H))
        step("vector", "vP", lambda e, tt=tt: e.tensor_scalar_mul(sV(tt), sV(tt), 1.0 / H))
        w("vector", "vP")
        step("vector", "vP", lambda e, tt=tt: e.tensor_mul(t1, sM(tt), sM(tt)))
        w("vector", "vP")
        step("vector", "vP", lambda e, tt=tt: e.tensor_sub(sV(tt), sV(tt), t1))
        w("vector", "vP")
        step("vector", "vP", lambda e, tt=tt: e.tensor_scalar_add(sV(tt), sV(tt), 1e-5))
        w("vector", "vP")
        step("vector", "vP", lambda e, tt=tt: e.reciprocal(t1, sV(tt)))
        w("scalar", "vP")
        step("scalar", "sP", lambda e, tt=tt: e.activation(sR(tt), t1, AF.Sqrt))
        w("vector", "sP")
        step("vector", "vP", lambda e, tt=tt: e.tensor_mul(t1, sR(tt), sR(tt)))
        w("vector", "vP")
        step("vector", "vP", lambda e, tt=tt: e.tensor_mul(t1, t1, sV(tt)))
        w("vector", "vP")
        step("vector", "vP", lambda e: e.tensor_scalar(t1, t1, -0.5, 1.5, op0=ALU.mult, op1=ALU.add))
        w("vector", "vP")
        step("vector", "vP", lambda e, tt=tt: e.tensor_mul(sR(tt), sR(tt), t1))
        w("vector", "vP")
        step("vector", "vP", lambda e, tt=tt: e.tensor_mul(sBi(tt), sM(tt), sR(tt)))
        w("vector", "vP")
        step("vector", "vP", lambda e, tt=tt: e.tensor_scalar_mul(sBi(tt), sBi(tt), -1.0))
        w("vector", "vP")
        xs = tt % 2
        xnt_res.wait("vector", xs)
        step("vector", "vP", lambda e, tt=tt, xs=xs: e.tensor_scalar(
            xnt[xs][:, :], x_t[:, :], sR(tt), sBi(tt), op0=ALU.mult, op1=ALU.add))
        w("tensor", "vP")
        if tt == 0:
            w("tensor", "d_c", 16 * 3)
        for c in range(HC):
            bk = 6 + (c % 2)
            bwait("tensor", bk)
            step("tensor", "tP", lambda e, c=c, b=bk, xs=xs: e.transpose(
                ps(b)[:, 0:128], xnt[xs][:, c * 128:(c + 1) * 128], ident[:, :]))
            w("scalar", "tP")
            w("vector", "tP")
            col = c * T + tt * 128
            step("scalar", "sP", lambda e, b=bk, col=col: e.copy(
                xn_hi[:, col:col + 128], ps(b)[:, 0:128]))
            w("vector", "sP")
            step("vector", "vP", lambda e, b=bk, col=col: e.scalar_tensor_tensor(
                xn_lo[:, col:col + 128], ps(b)[:, 0:128], 1.0,
                xn_hi[:, col:col + 128], op0=ALU.mult, op1=ALU.subtract))
            brel([bk], "sP", "vP")
        xnt_res.rel(xs, "tP")
    xn_v, xn_s = cnt["vP"], cnt["sP"]

    if phases < 2:
        return _finish(nc, ops, stack, sems, cnt)

    slab_raw = [B(s * 24 * KB, 8 * KB, F32) for s in range(2)]
    slab_thi = [B(8 * KB, 4 * KB, BF16), B(24 * KB + 8 * KB, 4 * KB, BF16),
                C(0, 4 * KB, BF16), C(16 * KB, 4 * KB, BF16)]
    slab_tlo = [B(12 * KB, 4 * KB, BF16), B(24 * KB + 12 * KB, 4 * KB, BF16),
                C(4 * KB, 4 * KB, BF16), C(20 * KB, 4 * KB, BF16)]
    wtil = [B(16 * KB, 8 * KB, BF16), B(24 * KB + 16 * KB, 8 * KB, BF16),
            C(8 * KB, 8 * KB, BF16), C(24 * KB, 8 * KB, BF16)]
    raw_res = Res(2)
    thi_res = Res(4)
    wt_res = Res(4)

    def wt_hi_t(slot, c):
        return wtil[slot][:, c * 256:c * 256 + 128]

    def wt_lo_t(slot, c):
        return wtil[slot][:, c * 256 + 128:(c + 1) * 256]
    stg_hi = [B(48 * KB + 4 * s * KB, 2 * KB, BF16) for s in range(2)]
    stg_lo = [B(50 * KB + 4 * s * KB, 2 * KB, BF16) for s in range(2)]
    vfm = B(56 * KB, 4 * KB, F32)
    vst_hi = B(60 * KB, 2 * KB, BF16)
    vst_lo = B(62 * KB, 2 * KB, BF16)
    slab_res = Res(2)
    stg_res = Res(2)

    def prep_load(slot, src_ap, dsem, scale_col=None):
        raw, thi, tlo = slab_raw[slot % 2], slab_thi[slot], slab_tlo[slot]
        raw_res.wait("sync", slot % 2)
        dma("sync", raw[:, :], src_ap, dsem)
        w("scalar", dsem)
        thi_res.wait("scalar", slot)
        if scale_col is not None:
            step("scalar", "sP", lambda e, r=raw, sc=scale_col: e.mul(r[:, :], r[:, :], sc))
            w("scalar", "sP")
        step("scalar", "sP", lambda e, r=raw, o=thi: e.copy(o[:, :], r[:, :]))
        w("vector", "sP")
        thi_res.wait("vector", slot)
        step("vector", "vP", lambda e, r=raw, h=thi, o=tlo: e.scalar_tensor_tensor(
            o[:, :], r[:, :], 1.0, h[:, :], op0=ALU.mult, op1=ALU.subtract))
        raw_res.rel(slot % 2, "vP")
        return cnt["sP"], cnt["vP"]

    def prep_tp(slot, lm, pbanks=(6, 7)):
        thi, tlo = slab_thi[slot], slab_tlo[slot]
        o = wtil[slot]
        w("tensor", "sP", lm[0])
        w("tensor", "vP", lm[1])
        wt_res.wait("vector", slot)
        for g in range(4):
            bk = pbanks[g % len(pbanks)]
            bwait("tensor", bk)
            for k in range(4):
                c = g * 4 + k
                em("tensor", lambda e, c=c, b=bk, k=k, s=thi: e.transpose(
                    psb(b)[:, k * 256:k * 256 + 128], s[:, c * 128:(c + 1) * 128], identb[:, :]))
                if k < 3:
                    em("tensor", lambda e, c=c, b=bk, k=k, s=tlo: e.transpose(
                        psb(b)[:, k * 256 + 128:k * 256 + 256], s[:, c * 128:(c + 1) * 128], identb[:, :]))
                else:
                    step("tensor", "tP", lambda e, c=c, b=bk, k=k, s=tlo: e.transpose(
                        psb(b)[:, k * 256 + 128:k * 256 + 256], s[:, c * 128:(c + 1) * 128], identb[:, :]))
            w("vector", "tP")
            step("vector", "vP", lambda e, b=bk, g=g, o=o: e.tensor_copy(
                o[:, g * 1024:(g + 1) * 1024], psb(b)[:, :]))
            brel([bk], "vP")
        thi_res.rel(slot, "tP")
        return cnt["sP"], cnt["vP"]

    def prep_slab(slot, src_ap, dsem, scale_col=None):
        lm = prep_load(slot, src_ap, dsem, scale_col)
        return prep_tp(slot, lm)

    def mm3(bank, a_h, a_l, b_h, b_l, first, last, n=512):
        em("tensor", lambda e, o=bank, a=a_h, b=b_h, st=first, n=n: e.matmul(
            ps(o)[:, 0:n], a, b, start=st, stop=False))
        em("tensor", lambda e, o=bank, a=a_h, b=b_l, n=n: e.matmul(
            ps(o)[:, 0:n], a, b, start=False, stop=False))
        if not last:
            em("tensor", lambda e, o=bank, a=a_l, b=b_h, n=n: e.matmul(
                ps(o)[:, 0:n], a, b, start=False, stop=False))
        else:
            step("tensor", "tP", lambda e, o=bank, a=a_l, b=b_h, n=n: e.matmul(
                ps(o)[:, 0:n], a, b, start=False, stop=True))

    order = [(j, "k") for j in range(HC, 2 * HC)] + \
            [(j, "v") for j in range(2 * HC, 3 * HC)] + \
            [(j, "q") for j in range(0, HC)]
    vst_res = Res(1)
    qkv_lm = {0: prep_load(0, w_qkv[HC * 128:(HC + 1) * 128, :], "d_wa")}
    qkv_pm = {0: prep_tp(0, qkv_lm[0])}
    for idx, (j, kind) in enumerate(order):
        slot = idx % 2
        if idx + 1 < len(order):
            j2 = order[idx + 1][0]
            qkv_lm[idx + 1] = prep_load((idx + 1) % 2, w_qkv[j2 * 128:(j2 + 1) * 128, :],
                                        "d_wa" if (idx + 1) % 2 == 0 else "d_wb")
        tiles_s, tiles_v = qkv_pm[idx]
        if idx == 0:
            w("tensor", "vP", xn_v)
            w("tensor", "sP", xn_s)
        w("tensor", "sP", tiles_s)
        w("tensor", "vP", tiles_v)
        bwait("tensor", slot * 2, slot * 2 + 1)
        for c in range(HC):
            for half in range(2):
                tb = half * 512
                mm3(slot * 2 + half,
                    wt_hi_t(slot, c), wt_lo_t(slot, c),
                    xn_hi[:, c * T + tb:c * T + tb + 512], xn_lo[:, c * T + tb:c * T + tb + 512],
                    c == 0, c == HC - 1)
        wt_res.m[slot] = {"tP": cnt["tP"]}
        if idx + 1 < len(order):
            qkv_pm[idx + 1] = prep_tp((idx + 1) % 2, qkv_lm[idx + 1])
        bias = bqkv[:, j:j + 1]
        w("scalar", "tP")
        w("vector", "tP")
        sslot = idx % 2
        stg_res.wait("scalar", sslot)
        stg_res.wait("vector", sslot)
        for half in range(2):
            bk = slot * 2 + half
            tb = half * 512
            step("scalar", "sP", lambda e, b=bk, tb=tb, bias=bias, ss=sslot: e.add(
                stg_hi[ss][:, tb:tb + 512], ps(b), bias))
            w("vector", "sP")
            step("vector", "vP", lambda e, b=bk, tb=tb, bias=bias, ss=sslot: e.scalar_tensor_tensor(
                stg_lo[ss][:, tb:tb + 512], ps(b), bias, stg_hi[ss][:, tb:tb + 512],
                op0=ALU.add, op1=ALU.subtract))
        brel([slot * 2, slot * 2 + 1], "sP", "vP")
        w("gpsimd", "sP")
        w("gpsimd", "vP")
        if kind == "k":
            r = j - HC
            dma("gpsimd", kfm(0)[r * 128:(r + 1) * 128, :], stg_hi[sslot][:, :], "d_st")
            dma("gpsimd", kfm(1)[r * 128:(r + 1) * 128, :], stg_lo[sslot][:, :], "d_st")
            stg_res.rel(sslot, "d_st")
        elif kind == "q":
            dma("gpsimd", qsp.ap()[0, j * 128:(j + 1) * 128, :], stg_hi[sslot][:, :], "d_st")
            dma("gpsimd", qsp.ap()[1, j * 128:(j + 1) * 128, :], stg_lo[sslot][:, :], "d_st")
            stg_res.rel(sslot, "d_st")
        else:
            r = j - 2 * HC
            w("vector", "vP")
            step("vector", "vP", lambda e, ss=sslot: e.tensor_add(
                vfm[:, :], stg_hi[ss][:, :], stg_lo[ss][:, :]))
            stg_res.rel(sslot, "vP")
            w("tensor", "vP")
            vst_res.wait("scalar", 0)
            vst_res.wait("vector", 0)
            for tt in range(NTT):
                bk = 4 + (tt % 2)
                bwait("tensor", bk)
                step("tensor", "tP", lambda e, tt=tt, b=bk: e.transpose(
                    ps(b)[:, 0:128], vfm[:, tt * 128:(tt + 1) * 128], ident[:, :]))
                w("scalar", "tP")
                w("vector", "tP")
                step("scalar", "sP", lambda e, b=bk, tt=tt: e.copy(
                    vst_hi[:, tt * 128:(tt + 1) * 128], ps(b)[:, 0:128]))
                w("vector", "sP")
                step("vector", "vP", lambda e, b=bk, tt=tt: e.scalar_tensor_tensor(
                    vst_lo[:, tt * 128:(tt + 1) * 128], ps(b)[:, 0:128], 1.0,
                    vst_hi[:, tt * 128:(tt + 1) * 128], op0=ALU.mult, op1=ALU.subtract))
                brel([bk], "sP", "vP")
            w("gpsimd", "sP")
            w("gpsimd", "vP")
            vh = gin.ap()[2].rearrange("(a p Hh) -> p a Hh", p=128, Hh=H)[:, :, r * 128:(r + 1) * 128]
            vl = gin.ap()[3].rearrange("(a p Hh) -> p a Hh", p=128, Hh=H)[:, :, r * 128:(r + 1) * 128]
            dma("gpsimd", vh, vst_hi.rearrange("p (a q) -> p a q", q=128), "d_st")
            dma("gpsimd", vl, vst_lo.rearrange("p (a q) -> p a q", q=128), "d_st")
            vst_res.rel(0, "d_st")

    w("gpsimd", "d_st")
    if phases < 3:
        if dbg:
            dma("gpsimd", gin_o.ap(), gin.ap(), "d_st")
        return _finish(nc, ops, stack, sems, cnt)
    step("gpsimd", "cc", lambda e: e.collective_compute(
        "AllGather", ALU.bypass, replica_groups=[list(range(NCORE))],
        ins=[gin.ap()], outs=[gout.ap()]))
    if dbg:
        w("gpsimd", "cc")
        dma("gpsimd", gin_o.ap(), gin.ap(), "d_st")
        dma("gpsimd", gout_o.ap(), gout.ap(), "d_st")

    for cp in range(HC):
        slot = cp % 2
        bwait("tensor", slot * 2, slot * 2 + 1)
        for c in range(HC):
            hs = hsu_p if H16[c, cp] > 0 else hsu_n
            a = hs[:, c * 128:(c + 1) * 128]
            for half in range(2):
                tb = half * 512
                bk = slot * 2 + half
                b_h = xn_hi[:, c * T + tb:c * T + tb + 512]
                b_l = xn_lo[:, c * T + tb:c * T + tb + 512]
                em("tensor", lambda e, o=bk, a=a, b=b_h, st=(c == 0): e.matmul(
                    ps(o), a, b, start=st, stop=False))
                if c < HC - 1:
                    em("tensor", lambda e, o=bk, a=a, b=b_l: e.matmul(
                        ps(o), a, b, start=False, stop=False))
                else:
                    step("tensor", "tP", lambda e, o=bk, a=a, b=b_l: e.matmul(
                        ps(o), a, b, start=False, stop=True))
        w("scalar", "tP")
        w("vector", "tP")
        sslot = cp % 2
        stg_res.wait("scalar", sslot)
        stg_res.wait("vector", sslot)
        for half in range(2):
            bk = slot * 2 + half
            tb = half * 512
            step("scalar", "sP", lambda e, b=bk, tb=tb, ss=sslot: e.copy(
                stg_hi[ss][:, tb:tb + 512], ps(b)))
            w("vector", "sP")
            step("vector", "vP", lambda e, b=bk, tb=tb, ss=sslot: e.scalar_tensor_tensor(
                stg_lo[ss][:, tb:tb + 512], ps(b), 1.0, stg_hi[ss][:, tb:tb + 512],
                op0=ALU.mult, op1=ALU.subtract))
        brel([slot * 2, slot * 2 + 1], "sP", "vP")
        w("gpsimd", "sP")
        w("gpsimd", "vP")
        dma("gpsimd", zsp.ap()[0, cp * 128:(cp + 1) * 128, :], stg_hi[sslot][:, :], "d_st")
        dma("gpsimd", zsp.ap()[1, cp * 128:(cp + 1) * 128, :], stg_lo[sslot][:, :], "d_st")
        stg_res.rel(sslot, "d_st")
    fwht1_t = cnt["tP"]

    if phases < 4:
        return _finish(nc, ops, stack, sems, cnt)

    z_hi = A(0, 32 * KB, BF16)
    z_lo = A(32 * KB, 32 * KB, BF16)
    xup_f = C(32 * KB, 2 * KB, F32)
    xgt_f = C(34 * KB, 2 * KB, F32)
    sg_f = C(36 * KB, 2 * KB, F32)
    u_f = C(38 * KB, 2 * KB, F32)
    w("sync", "d_st")
    w("sync", "tP", fwht1_t)
    dma("sync", d3(z_hi, T), fm3(zsp.ap()[0]), "d_z")
    dma("sync", d3(z_lo, T), fm3(zsp.ap()[1]), "d_z")

    def ug_slots(j):
        return (2 * j) % 4, (2 * j + 1) % 4

    def ug_load(j):
        s0, s1 = ug_slots(j)
        return (prep_load(s0, w_up[j * 128:(j + 1) * 128, :], "d_wa"),
                prep_load(s1, w_gate[j * 128:(j + 1) * 128, :], "d_wb"))

    ug_lm = {0: ug_load(0), 1: ug_load(1)}
    ug_pm = {0: (prep_tp(ug_slots(0)[0], ug_lm[0][0]), prep_tp(ug_slots(0)[1], ug_lm[0][1]))}
    for j in range(IC):
        if j + 2 < IC:
            ug_lm[j + 2] = ug_load(j + 2)
        if j + 1 < IC:
            ug_pm[j + 1] = (prep_tp(ug_slots(j + 1)[0], ug_lm[j + 1][0]),
                            prep_tp(ug_slots(j + 1)[1], ug_lm[j + 1][1]))
        if j == 0:
            w("tensor", "d_z")
        su0, su1 = ug_slots(j)
        (t0s, t0v), (t1s, t1v) = ug_pm[j]
        w("tensor", "sP", max(t0s, t1s))
        w("tensor", "vP", max(t0v, t1v))
        pl = [(0, 1, 2, 3), (4, 5, 0, 1), (2, 3, 4, 5)][j % 3]
        ugmm = {}
        for half in range(2):
            tb = half * 512
            bwait("tensor", pl[half], pl[2 + half])
            for c in range(HC):
                mm3(pl[half], wt_hi_t(su0, c), wt_lo_t(su0, c),
                    z_hi[:, c * T + tb:c * T + tb + 512], z_lo[:, c * T + tb:c * T + tb + 512],
                    c == 0, c == HC - 1)
            for c in range(HC):
                mm3(pl[2 + half], wt_hi_t(su1, c), wt_lo_t(su1, c),
                    z_hi[:, c * T + tb:c * T + tb + 512], z_lo[:, c * T + tb:c * T + tb + 512],
                    c == 0, c == HC - 1)
            ugmm[half] = cnt["tP"]
        wt_res.m[su0] = {"tP": cnt["tP"]}
        wt_res.m[su1] = {"tP": cnt["tP"]}
        for half in range(2):
            w("vector", "tP", ugmm[half])
            step("vector", "vP", lambda e, b=pl[half], j=j: e.tensor_scalar_mul(
                xup_f[:, :], ps(b), s_up[:, j:j + 1]))
            step("vector", "vP", lambda e, b=pl[2 + half], j=j: e.tensor_scalar_mul(
                xgt_f[:, :], ps(b), s_gt[:, j:j + 1]))
            brel([pl[half], pl[2 + half]], "vP")
            w("scalar", "vP")
            step("scalar", "sP", lambda e: e.activation(sg_f[:, :], xgt_f[:, :], AF.Silu))
            w("vector", "sP")
            step("vector", "vP", lambda e, j=j: e.scalar_tensor_tensor(
                u_f[:, :], sg_f[:, :], sudp[:, j:j + 1], xup_f[:, :], op0=ALU.mult, op1=ALU.mult))
            sslot = (2 * j + half) % 2
            stg_res.wait("vector", sslot)
            stg_res.wait("scalar", sslot)
            w("scalar", "vP")
            step("scalar", "sP", lambda e, ss=sslot: e.copy(stg_hi[ss][:, 0:512], u_f[:, :]))
            w("vector", "sP")
            step("vector", "vP", lambda e, ss=sslot: e.scalar_tensor_tensor(
                stg_lo[ss][:, 0:512], u_f[:, :], 1.0, stg_hi[ss][:, 0:512],
                op0=ALU.mult, op1=ALU.subtract))
            w("gpsimd", "sP")
            w("gpsimd", "vP")
            dma("gpsimd", usp.ap()[0, j * 128:(j + 1) * 128, half * 512:(half + 1) * 512],
                stg_hi[sslot][:, 0:512], "d_st")
            dma("gpsimd", usp.ap()[1, j * 128:(j + 1) * 128, half * 512:(half + 1) * 512],
                stg_lo[sslot][:, 0:512], "d_st")
            stg_res.rel(sslot, "d_st")

    units = []
    for j in range(HC):
        for s4 in range(4):
            units.append(("wd", j, s4))
        units.append(("wo", j, 0))

    def wprep_load(u, slot):
        kind, j, s4 = u
        if kind == "wd":
            return prep_load(slot, w_down[j * 128:(j + 1) * 128, s4 * H:(s4 + 1) * H],
                             "d_wa" if slot == 0 else "d_wb", scale_col=s_dpc[:, j:j + 1])
        return prep_load(slot, w_out[j * 128:(j + 1) * 128, :],
                         "d_wa" if slot == 0 else "d_wb")

    wp_lm = {0: wprep_load(units[0], 0), 1: wprep_load(units[1], 1)}
    wp_tv = {0: prep_tp(0, wp_lm[0], pbanks=(2, 3, 4, 5, 6, 7))}
    for ui, u in enumerate(units):
        slot = ui % 4
        if ui + 2 < len(units):
            wp_lm[ui + 2] = wprep_load(units[ui + 2], (ui + 2) % 4)
        if ui + 1 < len(units):
            wp_tv[ui + 1] = prep_tp((ui + 1) % 4, wp_lm[ui + 1], pbanks=(2, 3, 4, 5, 6, 7))
        tv = wp_tv[ui]
        kind, j, s4 = u
        w("gpsimd", "vP", tv[1])
        whiv = d3(wtil[slot], 256).transpose([0, 1, 2])[:, :, 0:128]
        wlov = d3(wtil[slot], 256)[:, :, 128:256]
        if kind == "wd":
            dma("gpsimd", wdfm.ap()[j, 0, s4 * HC:(s4 + 1) * HC].rearrange("c p q -> p c q"),
                whiv, "d_wd")
            dma("gpsimd", wdfm.ap()[j, 1, s4 * HC:(s4 + 1) * HC].rearrange("c p q -> p c q"),
                wlov, "d_wd")

        else:
            dma("gpsimd", wofm.ap()[j, 0].rearrange("c p q -> p c q"), whiv, "d_wo")
            dma("gpsimd", wofm.ap()[j, 1].rearrange("c p q -> p c q"), wlov, "d_wo")

        wt_res.m[slot] = {"d_wd": cnt["d_wd"], "d_wo": cnt["d_wo"]}

    if phases < 5:
        return _finish(nc, ops, stack, sems, cnt)

    ctx_sb = A(0, 64 * KB, F32)
    kvs_hi = [B(0, 16 * KB, BF16), B(32 * KB, 16 * KB, BF16)]
    kvs_lo = [B(16 * KB, 16 * KB, BF16), B(48 * KB, 16 * KB, BF16)]
    q_hi = [C(0, 4 * KB, BF16), C(42 * KB, 4 * KB, BF16)]
    q_lo = [C(4 * KB, 4 * KB, BF16), C(46 * KB, 4 * KB, BF16)]
    sc_sb = [C(8 * KB + qt * 2 * KB, 2 * KB, F32) for qt in range(NQ)]
    pt_hi = C(24 * KB, 8 * KB, BF16)
    pt_lo = C(32 * KB, 8 * KB, BF16)
    pstg_h = [C(40 * KB, 1 * KB, BF16), C(50 * KB, 1 * KB, BF16)]
    pstg_l = [C(41 * KB, 1 * KB, BF16), C(51 * KB, 1 * KB, BF16)]
    aM = lambda qt: st_col(72 + qt)
    aL = lambda qt: st_col(80 + qt)
    aMn = lambda qt: st_col(88 + qt)
    aRs = lambda qt: st_col(96 + qt)
    aBi = lambda qt: st_col(104 + qt)
    aT = st_col(112)
    k_res = Res(1)
    v_res = Res(1)
    q_res = Res(2)
    pstg_res = Res(2)
    pstg_ready = [(0, 0), (0, 0)]

    w("vector", "d_st")
    step("vector", "vP", lambda e: e.memset(stat[:, 72:80], -1e30))
    step("vector", "vP", lambda e: e.memset(stat[:, 80:88], 0.0))
    step("vector", "vP", lambda e: e.memset(ctx_sb[:, :], 0.0))
    attn_init_v = cnt["vP"]

    def pt_view(t, c, qt):
        return t.rearrange("p (c g q) -> p c g q", g=NQ, q=128)[:, c, qt, :]

    def emit_ptrans(qt):
        bk_t = 6 + (qt % 2)
        sl = qt % 2
        bwait("tensor", bk_t)
        w("tensor", "sP", pstg_ready[sl][0])
        w("tensor", "vP", pstg_ready[sl][1])
        for c in range(4):
            step("tensor", "tP", lambda e, c=c, b=bk_t, sl=sl: e.transpose(
                psb(b)[:, c * 256:c * 256 + 128], pstg_h[sl][:, c * 128:(c + 1) * 128], identb[:, :]))
            step("tensor", "tP", lambda e, c=c, b=bk_t, sl=sl: e.transpose(
                psb(b)[:, c * 256 + 128:c * 256 + 256], pstg_l[sl][:, c * 128:(c + 1) * 128], identb[:, :]))
        pstg_res.rel(sl, "tP")
        w("vector", "tP")
        step("vector", "vP", lambda e, b=bk_t, qt=qt: e.tensor_copy(
            pt_hi.rearrange("p (c g q) -> p c g q", g=NQ, q=128)[:, :, qt, :],
            psb(b).rearrange("p (c q) -> p c q", q=256)[:, :, 0:128]))
        step("vector", "vP", lambda e, b=bk_t, qt=qt: e.tensor_copy(
            pt_lo.rearrange("p (c g q) -> p c g q", g=NQ, q=128)[:, :, qt, :],
            psb(b).rearrange("p (c q) -> p c q", q=256)[:, :, 128:256]))
        brel([bk_t], "vP")

    for s in range(NS):
        rank, off = s // 2, (s % 2) * 512
        if s == 0:
            w("sync", "cc", 1)
            w("sync", "vP", attn_init_v)
            w("sync", "tP")
        k_res.wait("sync", 0)
        dma("sync", d3(kvs_hi[0][:, :], 512), fm3(g_k(rank, 0)[:, off:off + 512]), "d_k")
        dma("sync", d3(kvs_lo[0][:, :], 512), fm3(g_k(rank, 1)[:, off:off + 512]), "d_k")
        v_res.wait("sync", 0)
        dma("sync", d3(kvs_hi[1][:, :], H), fm3(g_v(rank, 0)[off:off + 512, :]), "d_v")
        dma("sync", d3(kvs_lo[1][:, :], H), fm3(g_v(rank, 1)[off:off + 512, :]), "d_v")
        for qt in range(NQ):
            qs = qt % 2
            q_res.wait("sync", qs)
            dma("sync", d3(q_hi[qs][:, :], 128), fm3(qsp.ap()[0, :, qt * 128:(qt + 1) * 128]), "d_q")
            dma("sync", d3(q_lo[qs][:, :], 128), fm3(qsp.ap()[1, :, qt * 128:(qt + 1) * 128]), "d_q")
            bk = 4 + (qt % 2)
            w("tensor", "d_q")
            if qt == 0:
                w("tensor", "d_k")
                if s == 0:
                    w("tensor", "vP", attn_init_v)
            bwait("tensor", bk)
            for c in range(HC):
                mm3(bk, q_hi[qs][:, c * 128:(c + 1) * 128], q_lo[qs][:, c * 128:(c + 1) * 128],
                    kvs_hi[0][:, c * 512:(c + 1) * 512], kvs_lo[0][:, c * 512:(c + 1) * 512],
                    c == 0, c == HC - 1)
            q_res.rel(qs, "tP")
            if qt > 0:
                emit_ptrans(qt - 1)
            w("vector", "tP")
            step("vector", "vP", lambda e, b=bk, qt=qt: e.tensor_copy(sc_sb[qt][:, :], ps(b)))
            brel([bk], "vP")
            w("vector", "vP")
            step("vector", "vP", lambda e, qt=qt: e.reduce_max(aMn(qt), sc_sb[qt][:, :], axis=AX.X))
            w("vector", "vP")
            step("vector", "vP", lambda e, qt=qt: e.tensor_max(aMn(qt), aMn(qt), aM(qt)))
            w("vector", "vP")
            step("vector", "vP", lambda e, qt=qt: e.tensor_sub(aRs(qt), aM(qt), aMn(qt)))
            step("vector", "vP", lambda e, qt=qt: e.tensor_scalar_mul(aBi(qt), aMn(qt), -SCL))
            step("vector", "vP", lambda e, qt=qt: e.tensor_copy(aM(qt), aMn(qt)))
            w("scalar", "vP")
            step("scalar", "sP", lambda e, qt=qt: e.activation(aRs(qt), aRs(qt), AF.Exp, scale=SCL))
            step("scalar", "sP", lambda e, qt=qt: e.activation(
                sc_sb[qt][:, :], sc_sb[qt][:, :], AF.Exp, bias=aBi(qt), scale=SCL))
            w("vector", "sP")
            step("vector", "vP", lambda e, qt=qt: e.reduce_sum(aT, sc_sb[qt][:, :], axis=AX.X))
            w("vector", "vP")
            step("vector", "vP", lambda e, qt=qt: e.scalar_tensor_tensor(
                aL(qt), aL(qt), aRs(qt), aT, op0=ALU.mult, op1=ALU.add))
            pstg_res.wait("scalar", qs)
            step("scalar", "sP", lambda e, qt=qt, sl=qs: e.copy(pstg_h[sl][:, :], sc_sb[qt][:, :]))
            w("vector", "sP")
            step("vector", "vP", lambda e, qt=qt, sl=qs: e.scalar_tensor_tensor(
                pstg_l[sl][:, :], sc_sb[qt][:, :], 1.0, pstg_h[sl][:, :],
                op0=ALU.mult, op1=ALU.subtract))
            pstg_ready[qs] = (cnt["sP"], cnt["vP"])
        k_res.rel(0, "tP")
        emit_ptrans(NQ - 1)
        w("tensor", "d_v")
        for qt in range(NQ):
            grp = [0, 1, 2, 3] if qt % 2 == 0 else [4, 5, 6, 7]
            bwait("tensor", *grp)
            for c in range(4):
                a_h = pt_view(pt_hi, c, qt)
                a_l = pt_view(pt_lo, c, qt)
                for hb in range(4):
                    b_h = kvs_hi[1][:, c * 2048 + hb * 512:c * 2048 + (hb + 1) * 512]
                    b_l = kvs_lo[1][:, c * 2048 + hb * 512:c * 2048 + (hb + 1) * 512]
                    mm3(grp[hb], a_h, a_l, b_h, b_l, c == 0, c == 3)
            w("vector", "tP")
            for hb in range(4):
                step("vector", "vP", lambda e, qt=qt, b=grp[hb], hb=hb: e.scalar_tensor_tensor(
                    ctx_sb[:, qt * H + hb * 512:qt * H + (hb + 1) * 512],
                    ctx_sb[:, qt * H + hb * 512:qt * H + (hb + 1) * 512],
                    aRs(qt), ps(b), op0=ALU.mult, op1=ALU.add))
            brel(grp, "vP")
        v_res.rel(0, "tP")

    cth = B(0, 4 * KB, BF16)
    ctl = B(4 * KB, 4 * KB, BF16)
    cfs_h = [B(8 * KB, 4 * KB, BF16), B(16 * KB, 4 * KB, BF16)]
    cfs_l = [B(12 * KB, 4 * KB, BF16), B(20 * KB, 4 * KB, BF16)]
    cf_res = Res(2)
    for qt in range(NQ):
        w("vector", "tP")
        step("vector", "vP", lambda e, qt=qt: e.reciprocal(aT, aL(qt)))
        w("scalar", "vP")
        step("scalar", "sP", lambda e, qt=qt: e.mul(cth[:, :], ctx_sb[:, qt * H:(qt + 1) * H], aT))
        w("vector", "sP")
        step("vector", "vP", lambda e, qt=qt: e.scalar_tensor_tensor(
            ctl[:, :], ctx_sb[:, qt * H:(qt + 1) * H], aT, cth[:, :],
            op0=ALU.mult, op1=ALU.subtract))
        w("tensor", "sP")
        w("tensor", "vP")
        qs = qt % 2
        cf_res.wait("vector", qs)
        for g in range(4):
            bk = 6 + (g % 2)
            bwait("tensor", bk)
            for k in range(4):
                c = g * 4 + k
                step("tensor", "tP", lambda e, c=c, b=bk, k=k: e.transpose(
                    psb(b)[:, k * 256:k * 256 + 128], cth[:, c * 128:(c + 1) * 128], identb[:, :]))
                step("tensor", "tP", lambda e, c=c, b=bk, k=k: e.transpose(
                    psb(b)[:, k * 256 + 128:k * 256 + 256], ctl[:, c * 128:(c + 1) * 128], identb[:, :]))
            w("vector", "tP")
            step("vector", "vP", lambda e, b=bk, g=g, qs=qs: e.tensor_copy(
                cfs_h[qs].rearrange("p (c q) -> p c q", q=128)[:, g * 4:(g + 1) * 4, :],
                psb(b).rearrange("p (c q) -> p c q", q=256)[:, :, 0:128]))
            step("vector", "vP", lambda e, b=bk, g=g, qs=qs: e.tensor_copy(
                cfs_l[qs].rearrange("p (c q) -> p c q", q=128)[:, g * 4:(g + 1) * 4, :],
                psb(b).rearrange("p (c q) -> p c q", q=256)[:, :, 128:256]))
            brel([bk], "vP")
        w("gpsimd", "vP")
        dma("gpsimd", fm3(ctxsp.ap()[0, :, qt * 128:(qt + 1) * 128]), d3(cfs_h[qs][:, :], 128), "d_ctx")
        dma("gpsimd", fm3(ctxsp.ap()[1, :, qt * 128:(qt + 1) * 128]), d3(cfs_l[qs][:, :], 128), "d_ctx")
        cf_res.rel(qs, "d_ctx")

    if phases < 6:
        return _finish(nc, ops, stack, sems, cnt)

    u_hi = A(0, 32 * KB, BF16)
    u_lo = A(32 * KB, 32 * KB, BF16)
    w1f = B(0, 64 * KB, F32)
    w1tmp = C(0, 32 * KB, F32)
    ctxq_h = C(0, 8 * KB, BF16)
    ctxq_l = C(8 * KB, 8 * KB, BF16)
    wo_h = [C(16 * KB + 8 * k * KB, 4 * KB, BF16) for k in range(2)]
    wo_l = [C(20 * KB + 8 * k * KB, 4 * KB, BF16) for k in range(2)]
    comb = C(32 * KB, 16 * KB, F32)
    resid = C(48 * KB, 8 * KB, F32)
    wd_h = [B(0, 16 * KB, BF16), B(32 * KB, 16 * KB, BF16)]
    wd_l = [B(16 * KB, 16 * KB, BF16), B(48 * KB, 16 * KB, BF16)]
    ostg = B(0, 8 * KB, F32)
    wd_res = Res(2)
    ostg_res = Res(1)

    for q in range(4):
        cb = q * QT4
        if q == 0:
            w("sync", "d_ctx")
            w("sync", "tP")
            w("sync", "vP")
        else:
            w("sync", "tP")
            w("sync", "vP")
            w("sync", "d_out")
        dma("sync", d3(u_hi[:, :], QT4), fm3(usp.ap()[0])[:, :, cb:cb + QT4], "d_u")
        dma("sync", d3(u_lo[:, :], QT4), fm3(usp.ap()[1])[:, :, cb:cb + QT4], "d_u")
        w("tensor", "d_u")
        for cp in range(IC):
            bk = cp % 4
            c0, c1 = (cp % 32), (cp % 32) + 32
            hsgn = h128 if cp < 32 else h128n
            bwait("tensor", bk)
            em("tensor", lambda e, b=bk, c=c0: e.matmul(
                ps(b)[:, 0:QT4], h128[:, :], u_hi[:, c * QT4:(c + 1) * QT4], start=True, stop=False))
            em("tensor", lambda e, b=bk, c=c0: e.matmul(
                ps(b)[:, 0:QT4], h128[:, :], u_lo[:, c * QT4:(c + 1) * QT4], start=False, stop=False))
            em("tensor", lambda e, b=bk, a=hsgn, c=c1: e.matmul(
                ps(b)[:, 0:QT4], a[:, :], u_hi[:, c * QT4:(c + 1) * QT4], start=False, stop=False))
            step("tensor", "tP", lambda e, b=bk, a=hsgn, c=c1: e.matmul(
                ps(b)[:, 0:QT4], a[:, :], u_lo[:, c * QT4:(c + 1) * QT4], start=False, stop=True))
            w("vector", "tP")
            step("vector", "vP", lambda e, b=bk, cp=cp: e.tensor_copy(
                w1f[:, cp * QT4:(cp + 1) * QT4], ps(b)[:, 0:QT4]))
            brel([bk], "vP")
        for bit in (16, 8, 4, 2, 1):
            na = IC // (2 * bit)
            va = w1f.rearrange("p (a d t) -> p a d t", a=na, d=2 * bit)[:, :, 0:bit, :]
            vb = w1f.rearrange("p (a d t) -> p a d t", a=na, d=2 * bit)[:, :, bit:2 * bit, :]
            vt = w1tmp.rearrange("p (a d t) -> p a d t", a=na, d=bit)
            w("vector", "vP")
            step("vector", "vP", lambda e, a=va, b=vb, t=vt: e.tensor_sub(t, a, b))
            w("vector", "vP")
            step("vector", "vP", lambda e, a=va, b=vb: e.tensor_add(a, a, b))
            w("vector", "vP")
            step("vector", "vP", lambda e, b=vb, t=vt: e.tensor_copy(b, t))
        w("vector", "vP")
        w("scalar", "vP")
        step("scalar", "sP", lambda e: e.copy(u_hi[:, :], w1f[:, :]))
        w("vector", "sP")
        step("vector", "vP", lambda e: e.scalar_tensor_tensor(
            u_lo[:, :], w1f[:, :], 1.0, u_hi[:, :], op0=ALU.mult, op1=ALU.subtract))
        w("sync", "vP")
        dma("sync", d3(ctxq_h[:, :], QT4), fm3(ctxsp.ap()[0])[:, :, cb:cb + QT4], "d_ctx")
        dma("sync", d3(ctxq_l[:, :], QT4), fm3(ctxsp.ap()[1])[:, :, cb:cb + QT4], "d_ctx")
        for j in range(HC):
            slot = j % 2
            wd_res.wait("sync", slot)
            dma("sync", d3(wd_h[slot][:, :], 128), wdfm.ap()[j, 0].rearrange("c p q -> p c q"), "d_wd")
            dma("sync", d3(wd_l[slot][:, :], 128), wdfm.ap()[j, 1].rearrange("c p q -> p c q"), "d_wd")
            dma("sync", d3(wo_h[slot][:, :], 128), wofm.ap()[j, 0].rearrange("c p q -> p c q"), "d_wo")
            dma("sync", d3(wo_l[slot][:, :], 128), wofm.ap()[j, 1].rearrange("c p q -> p c q"), "d_wo")
            bk = 4 + (j % 2)
            w("tensor", "d_wd")
            w("tensor", "d_wo")
            if j == 0:
                w("tensor", "sP")
                w("tensor", "vP")
                w("tensor", "d_ctx")
            bwait("tensor", bk)
            for c in range(IC):
                mm3(bk, wd_h[slot][:, c * 128:(c + 1) * 128], wd_l[slot][:, c * 128:(c + 1) * 128],
                    u_hi[:, c * QT4:(c + 1) * QT4], u_lo[:, c * QT4:(c + 1) * QT4],
                    c == 0, False, n=QT4)
            for c in range(HC):
                mm3(bk, wo_h[slot][:, c * 128:(c + 1) * 128], wo_l[slot][:, c * 128:(c + 1) * 128],
                    ctxq_h[:, c * QT4:(c + 1) * QT4], ctxq_l[:, c * QT4:(c + 1) * QT4],
                    False, c == HC - 1, n=QT4)
            wd_res.rel(slot, "tP")
            w("scalar", "tP")
            step("scalar", "sP", lambda e, b=bk, j=j: e.add(
                comb[:, j * QT4:(j + 1) * QT4], ps(b)[:, 0:QT4], boutc[:, j:j + 1]))
            brel([bk], "sP")
        for tt in range(2):
            trow = cb + tt * 128
            w("sync", "d_out")
            dma("sync", resid[:, :], x_in[trow:trow + 128, :], "d_x")
            w("tensor", "sP")
            w("tensor", "d_out")
            for g in range(4):
                bk = 2 + (g % 2)
                bwait("tensor", bk)
                for k in range(4):
                    j = g * 4 + k
                    step("tensor", "tP", lambda e, j=j, b=bk, k=k, tt=tt: e.transpose(
                        ps(b)[:, k * 128:(k + 1) * 128],
                        comb[:, j * QT4 + tt * 128:j * QT4 + tt * 128 + 128], ident[:, :]))
                w("vector", "tP")
                w("vector", "d_x")
                if g == 0:
                    ostg_res.wait("vector", 0)
                step("vector", "vP", lambda e, b=bk, g=g: e.tensor_add(
                    ostg[:, g * 512:(g + 1) * 512], ps(b), resid[:, g * 512:(g + 1) * 512]))
                brel([bk], "vP")
            w("gpsimd", "vP")
            dma("gpsimd", out_d[trow:trow + 128, :], ostg[:, :], "d_out")
            ostg_res.rel(0, "d_out")

    return _finish(nc, ops, stack, sems, cnt)


def _finish(nc, ops, stack, sems, cnt):
    if cnt["d_out"]:
        ops["gpsimd"].append(lambda e: e.wait_ge(sems["d_out"], cnt["d_out"]))
    if cnt["d_st"]:
        ops["gpsimd"].append(lambda e: e.wait_ge(sems["d_st"], cnt["d_st"]))
    if cnt["d_ctx"]:
        ops["gpsimd"].append(lambda e: e.wait_ge(sems["d_ctx"], cnt["d_ctx"]))
    with nc.Block() as block:
        @block.sync
        def _(e):
            for f in ops["sync"]:
                f(e)

        @block.gpsimd
        def _(e):
            for f in ops["gpsimd"]:
                f(e)

        @block.vector
        def _(e):
            for f in ops["vector"]:
                f(e)

        @block.scalar
        def _(e):
            for f in ops["scalar"]:
                f(e)

        @block.tensor
        def _(e):
            for f in ops["tensor"]:
                f(e)
    stack.close()
    return nc


def kernel(L0_attn_hidden_states, L0_attn_w_qkv, L0_attn_b_qkv, L0_attn_w_out,
           L0_attn_b_out, L0_ffn_SU_ug, L0_ffn_SU_dp, L0_ffn_Wscale_ug,
           L0_ffn_Wscale_dp, L0_ffn_W_up, L0_ffn_W_gate, L0_ffn_W_down):
    x = np.ascontiguousarray(np.asarray(L0_attn_hidden_states, dtype=np.float32).reshape(NCORE * T, H))
    full = {
        "w_qkv": np.ascontiguousarray(np.asarray(L0_attn_w_qkv, dtype=np.float32)),
        "b_qkv": np.ascontiguousarray(np.asarray(L0_attn_b_qkv, dtype=np.float32)),
        "w_out": np.ascontiguousarray(np.asarray(L0_attn_w_out, dtype=np.float32)),
        "b_out": np.ascontiguousarray(np.asarray(L0_attn_b_out, dtype=np.float32)),
        "su_ug": np.ascontiguousarray(np.asarray(L0_ffn_SU_ug, dtype=np.float32)),
        "su_dp": np.ascontiguousarray(np.asarray(L0_ffn_SU_dp, dtype=np.float32)),
        "wsc_ug": np.ascontiguousarray(np.asarray(L0_ffn_Wscale_ug, dtype=np.float32)),
        "wsc_dp": np.ascontiguousarray(np.asarray(L0_ffn_Wscale_dp, dtype=np.float32)),
        "w_up": np.ascontiguousarray(np.asarray(L0_ffn_W_up, dtype=np.float32)),
        "w_gate": np.ascontiguousarray(np.asarray(L0_ffn_W_gate, dtype=np.float32)),
        "w_down": np.ascontiguousarray(np.asarray(L0_ffn_W_down, dtype=np.float32)),
    }
    nc = build_program()
    in_maps = [dict(full, x=x[c * T:(c + 1) * T]) for c in range(NCORE)]
    res = run_bass_kernel_spmd(nc, in_maps, core_ids=list(range(NCORE)))
    out = np.concatenate([res.results[c]["out"] for c in range(NCORE)], axis=0)
    return out.reshape(1, NCORE * T, H).astype(np.float32)
